# version 15
# baseline (speedup 1.0000x reference)
import contextlib
import numpy as np
import ml_dtypes
_bf16 = ml_dtypes.bfloat16
import concourse.bass as bass
import concourse.mybir as mybir
from concourse.bass_utils import run_bass_kernel_spmd

F32 = mybir.dt.float32
BF16 = mybir.dt.bfloat16
AF = mybir.ActivationFunctionType
OP = mybir.AluOpType
AX = mybir.AxisListType.X

B, T, C, H, D, FF = 4, 4096, 1024, 16, 64, 4096
NG = 8
NB = 32
NCH = 8
SCALE = float(D) ** -0.5

DEBUG = False


def build_program(debug_phase=0):
    nc = bass.Bass("TRN2", target_bir_lowering=False)
    x_in = nc.dram_tensor("x_in", [T, C], F32, kind="ExternalInput").ap()
    x_own = nc.dram_tensor("x_own", [2048, C], F32, kind="ExternalInput").ap()
    wqkv = nc.dram_tensor("wqkv", [C, 1536], F32, kind="ExternalInput").ap()
    wo_in = nc.dram_tensor("wo_in", [C, C], F32, kind="ExternalInput").ap()
    w1_in = nc.dram_tensor("w1_in", [C, FF], F32, kind="ExternalInput").ap()
    w2_in = nc.dram_tensor("w2_in", [FF, C], F32, kind="ExternalInput").ap()
    rw_in = nc.dram_tensor("rw_in", [C, 8], F32, kind="ExternalInput").ap()
    b1r_in = nc.dram_tensor("b1r_in", [128, 32], F32, kind="ExternalInput").ap()
    bo_in = nc.dram_tensor("bo_in", [1, C], F32, kind="ExternalInput").ap()
    b2_in = nc.dram_tensor("b2_in", [1, C], F32, kind="ExternalInput").ap()
    cos_in = nc.dram_tensor("cos_in", [128, T], F32, kind="ExternalInput").ap()
    sin_in = nc.dram_tensor("sin_in", [128, T], F32, kind="ExternalInput").ap()
    ident_in = nc.dram_tensor("ident_in", [128, 128], F32, kind="ExternalInput").ap()
    permT_in = nc.dram_tensor("permT_in", [128, 128], F32, kind="ExternalInput").ap()
    masks_in = nc.dram_tensor("masks_in", [128, 4 * 1024], BF16, kind="ExternalInput").ap()
    y_out = nc.dram_tensor("y_out", [2048, C], F32, kind="ExternalOutput").ap()

    xn_spill = nc.dram_tensor("xn_spill", [1024, 4096], BF16, kind="Internal").ap()
    cc_in = nc.dram_tensor("cc_in", [512, 4096], BF16, kind="Internal")
    cc_out = nc.dram_tensor("cc_out", [1024, 4096], BF16, kind="Internal")
    cc_in_ap = cc_in.ap()
    cc_out_ap = cc_out.ap()

    dbg = {}
    if debug_phase >= 1:
        dbg["k"] = nc.dram_tensor("dbg_k", [128, 4 * T], BF16, kind="ExternalOutput").ap()
        dbg["q"] = nc.dram_tensor("dbg_q", [128, 4 * T], BF16, kind="ExternalOutput").ap()
        dbg["v"] = nc.dram_tensor("dbg_v", [128, 32 * 512], BF16, kind="ExternalOutput").ap()
    if debug_phase >= 2:
        dbg["att"] = nc.dram_tensor("dbg_att", [128, 4 * T], BF16, kind="ExternalOutput").ap()
        dbg["ccout"] = nc.dram_tensor("dbg_ccout", [1024, 4096], BF16, kind="ExternalOutput").ap()

    est = contextlib.ExitStack()
    with est:
        ident = est.enter_context(nc.sbuf_tensor("ident", [128, 128], BF16))
        permT = est.enter_context(nc.sbuf_tensor("permT", [128, 128], BF16))
        ones128 = est.enter_context(nc.sbuf_tensor("ones128", [128, 1], BF16))
        b1r = est.enter_context(nc.sbuf_tensor("b1r", [128, 32], F32))
        rw_fm = est.enter_context(nc.sbuf_tensor("rw_fm", [128, 8 * 8], F32))
        bb = est.enter_context(nc.sbuf_tensor("bb", [128, C], F32))
        g_s = est.enter_context(nc.sbuf_tensor("g_s", [128, 16], F32))
        eps_s = est.enter_context(nc.sbuf_tensor("eps_s", [128, 1], F32))
        rw_bf = est.enter_context(nc.sbuf_tensor("rw_bf", [128, 8 * 8], BF16))

        sems = {}
        def sem(name):
            sems[name] = est.enter_context(nc.semaphore(name))
            return sems[name]
        class DmaRing:
            def __init__(self, name, per=16, ways=2):
                self.ways = ways
                self.s = [est.enter_context(nc.semaphore(name + str(k)))
                          for k in range(ways)]
                self.per = per
            def tag(self, i):
                return self.s[i % self.ways], self.per
            def tag6(self, i):
                return self.s[i % self.ways], self.per
            def wait(self, eng, i):
                eng.wait_ge(self.s[i % self.ways], self.per * (i // self.ways + 1))

        class Chain:
            def __init__(self, name):
                self.h = est.enter_context(nc.semaphore(name))
                self.n = 0
                self.last = None
            def inc(self, instr):
                instr.then_inc(self.h, 1)
                self.n += 1
                self.last = (self.h, self.n)
                return instr
            def note(self, semh, val):
                self.last = (semh, val)
            def wait(self, eng):
                if self.last is not None:
                    eng.wait_ge(self.last[0], self.last[1])

        vch = Chain("vch")
        sch = Chain("sch")
        tch = Chain("tch")
        gch = Chain("gch")
        qx_r = DmaRing("qx", 16, ways=4)
        qat_r = DmaRing("qat_r", 16)
        qwo_r = DmaRing("qwo_r", 16)
        qwf1_r = DmaRing("qwf1_r", 16, ways=4)
        qwf2_r = DmaRing("qwf2_r", 16)
        qxo_r = DmaRing("qxo_r", 16)
        qst_r = DmaRing("qst_r", 16)
        qdn_r = DmaRing("qdn_r", 16)
        qbc_r = DmaRing("qbc_r", 32)
        qmk_r = DmaRing("qmk_r", 16)
        qxs_r = DmaRing("qxs_r", 128)
        qw_r = DmaRing("qw", 16)
        qtab_r = DmaRing("qtab", 32)
        qpc1_r = DmaRing("qpc1", 16)
        qpc2_r = DmaRing("qpc2", 16)
        for n in ["qc",
                  "eS_sq", "eV_cf1", "eS_rstd", "eV_cf2", "eS_ln",
                  "eT_tp", "eV_xg", "eT_mm", "eV_raw", "eT_rot", "eV_e1", "eV_fin",
                  "eT_v", "eV_vv", "eV_wc", "eV_pc", "bar1",
                  "eT_sc", "eS_exp", "eV_msk", "eT_pv", "eV_av", "eV_dn", "eV_rc", "eV_nrm",
                  "qcc", "ecc", "bar2aT", "qxn", "qbb", "eV_bb", "eV_mk",
                  "eS_sq2", "eV_cf12", "eS_rstd2", "eV_cf22", "eS_ln2",
                  "eT_tp2", "eV_xg2", "eV_woc", "eT_y", "eV_ye",
                  "eT_r", "eS_re", "eV_rg",
                  "eT_h", "eS_ge", "eT_f", "eV_w1c", "eV_w2c",
                  "eV_fe", "eT_ft", "eV_ft", "eV_out", "qst"]:
            sem(n)

        pav = contextlib.ExitStack()
        k_fm = pav.enter_context(nc.sbuf_tensor("k_fm", [128, 4 * T], BF16))
        q_fm = pav.enter_context(nc.sbuf_tensor("q_fm", [128, 4 * T], BF16))
        v_tm = pav.enter_context(nc.sbuf_tensor("v_tm", [128, 32 * 512], BF16))
        p1s = contextlib.ExitStack()
        p_tp = p1s.enter_context(nc.psum_tensor("p_tp", [128, 1024], BF16))
        p_tp2 = p1s.enter_context(nc.psum_tensor("p_tp2", [128, 1024], BF16))
        p_mm = p1s.enter_context(nc.psum_tensor("p_mm", [128, 2 * 512], F32))
        p_rot = p1s.enter_context(nc.psum_tensor("p_rot", [128, 2 * 512], F32))
        p_v = p1s.enter_context(nc.psum_tensor("p_v", [128, 2 * 512], F32))
        x_blk = p1s.enter_context(nc.sbuf_tensor("x_blk", [128, 4 * C], F32))
        xn_tm = p1s.enter_context(nc.sbuf_tensor("xn_tm", [128, 4 * C], BF16))
        xn_grp = p1s.enter_context(nc.sbuf_tensor("xn_grp", [128, 2 * 8 * 512], BF16))
        wkv_bf = p1s.enter_context(nc.sbuf_tensor("wkv_bf", [128, 8 * 1536], BF16))
        tab_c = p1s.enter_context(nc.sbuf_tensor("tab_c", [128, 2 * 512], F32))
        tab_s = p1s.enter_context(nc.sbuf_tensor("tab_s", [128, 2 * 512], F32))
        kraw = p1s.enter_context(nc.sbuf_tensor("kraw", [128, 2 * 512], BF16))
        tmp1 = p1s.enter_context(nc.sbuf_tensor("tmp1", [128, 2 * 512], BF16))
        tmp2 = p1s.enter_context(nc.sbuf_tensor("tmp2", [128, 2 * 512], BF16))
        sum_s = p1s.enter_context(nc.sbuf_tensor("sum_s", [128, 4], F32))
        sq_s = p1s.enter_context(nc.sbuf_tensor("sq_s", [128, 4], F32))
        m_s = p1s.enter_context(nc.sbuf_tensor("m_s", [128, 4], F32))
        var_s = p1s.enter_context(nc.sbuf_tensor("var_s", [128, 4], F32))
        rstd_s = p1s.enter_context(nc.sbuf_tensor("rstd_s", [128, 4], F32))
        bneg_s = p1s.enter_context(nc.sbuf_tensor("bneg_s", [128, 4], F32))
        scr = p1s.enter_context(nc.sbuf_tensor("scr", [128, C], BF16))

        W1U = 16
        W2U = 32


        def p1_sync(eng):
            for Bi in range(NB):
                if Bi >= 4:
                    eng.wait_ge(sems["eS_ln"], Bi - 3)
                s_, p_ = qx_r.tag(Bi)
                eng.dma_start(out=x_blk[:, (Bi % 4) * C:(Bi % 4 + 1) * C],
                              in_=x_in[Bi * 128:(Bi + 1) * 128, :]).then_inc(s_, p_)

        def p1_gpsimd(eng):
            eng.dma_start(out=ident[:], in_=ident_in).then_inc(sems["qc"], 16)
            eng.dma_start(out=permT[:], in_=permT_in).then_inc(sems["qc"], 16)
            eng.dma_start(out=b1r[:], in_=b1r_in).then_inc(sems["qc"], 16)
            for cc in range(NCH):
                eng.dma_start(out=rw_fm[:, cc * 8:(cc + 1) * 8],
                              in_=rw_in[cc * 128:(cc + 1) * 128, :]).then_inc(sems["qc"], 16)
            for cc in range(NCH):
                s_, p_ = qw_r.tag(cc)
                eng.dma_start(out=wkv_bf[:, cc * 1536:(cc + 1) * 1536],
                              in_=wqkv[cc * 128:(cc + 1) * 128, :]).then_inc(s_, p_)
            for g in range(NG):
                if g >= 2:
                    eng.wait_ge(sems["eV_fin"], 8 * (g - 1))
                s_, _ = qtab_r.tag(g)
                eng.dma_start(out=tab_c[:, (g % 2) * 512:(g % 2 + 1) * 512],
                              in_=cos_in[:, g * 512:(g + 1) * 512]).then_inc(s_, 16)
                eng.dma_start(out=tab_s[:, (g % 2) * 512:(g % 2 + 1) * 512],
                              in_=sin_in[:, g * 512:(g + 1) * 512]).then_inc(s_, 16)
                if g >= 1:
                    eng.wait_ge(sems["eV_xg"], 4 * g)
                    sx_, px_ = qxs_r.tag(g - 1)
                    gb = ((g - 1) % 2) * 4096
                    for cc in range(NCH):
                        eng.dma_start(out=xn_spill[cc * 128:(cc + 1) * 128,
                                                   (g - 1) * 512: g * 512],
                                      in_=xn_grp[:, gb + cc * 512: gb + (cc + 1) * 512]).then_inc(sx_, 16)
            eng.wait_ge(sems["eV_xg"], 4 * NG)
            sx_, px_ = qxs_r.tag(NG - 1)
            gb = ((NG - 1) % 2) * 4096
            for cc in range(NCH):
                eng.dma_start(out=xn_spill[cc * 128:(cc + 1) * 128,
                                           (NG - 1) * 512: NG * 512],
                              in_=xn_grp[:, gb + cc * 512: gb + (cc + 1) * 512]).then_inc(sx_, 16)
            qxs_r.wait(eng, NG - 1)
            qxs_r.wait(eng, NG - 2)
        def p1_scalar(eng):
            eng.wait_ge(sems["eV_cf2"], 1)
            for Bi in range(NB):
                pb = Bi % 4
                qx_r.wait(eng, Bi)
                if Bi >= 4:
                    eng.wait_ge(sems["eT_tp"], Bi - 3)
                eng.activation(scr[:], x_blk[:, pb * C:(pb + 1) * C], AF.Square,
                               accum_out=sq_s[:, pb:pb + 1])
                sch.inc(eng.activation(xn_tm[:, pb * C:(pb + 1) * C],
                                       x_blk[:, pb * C:(pb + 1) * C], AF.Copy,
                                       accum_out=sum_s[:, pb:pb + 1]))
                sch.wait(eng)
                sch.inc(eng.mul(m_s[:, pb:pb + 1], sum_s[:, pb:pb + 1], 1.0 / C))
                sch.wait(eng)
                sch.inc(eng.activation(var_s[:, pb:pb + 1], m_s[:, pb:pb + 1], AF.Square))
                sch.wait(eng)
                sch.inc(eng.mul(var_s[:, pb:pb + 1], var_s[:, pb:pb + 1], -1.0))
                sch.wait(eng)
                sch.inc(eng.activation(var_s[:, pb:pb + 1], sq_s[:, pb:pb + 1], AF.Identity,
                                       scale=1.0 / C, bias=var_s[:, pb:pb + 1]))
                sch.wait(eng)
                sch.inc(eng.activation(var_s[:, pb:pb + 1], var_s[:, pb:pb + 1], AF.Ln,
                                       bias=eps_s[:, 0:1]))
                sch.wait(eng)
                sch.inc(eng.activation(rstd_s[:, pb:pb + 1], var_s[:, pb:pb + 1], AF.Exp,
                                       scale=-0.5))
                sch.wait(eng)
                sch.inc(eng.activation(bneg_s[:, pb:pb + 1], m_s[:, pb:pb + 1], AF.Identity,
                                       scale=rstd_s[:, pb:pb + 1]))
                sch.wait(eng)
                sch.inc(eng.mul(bneg_s[:, pb:pb + 1], bneg_s[:, pb:pb + 1], -1.0))
                sch.wait(eng)
                eng.activation(xn_tm[:, pb * C:(pb + 1) * C], x_blk[:, pb * C:(pb + 1) * C],
                               AF.Identity, bias=bneg_s[:, pb:pb + 1],
                               scale=rstd_s[:, pb:pb + 1]).then_inc(sems["eS_ln"], 1)

        def p1_vector(eng):
            eng.memset(eps_s[:], 1e-5)
            eng.memset(ones128[:], 1.0).then_inc(sems["eV_cf2"], 1)
            eng.wait_ge(sems["qc"], 176)
            eng.tensor_copy(rw_bf[:], rw_fm[:])

            def _evacs(g):
                for b in range(4):
                    Bi = 4 * g + b
                    eng.wait_ge(sems["eT_tp"], Bi + 1)
                    if b == 0 and g >= 2:
                        qxs_r.wait(eng, g - 2)
                    ptp = p_tp if Bi % 2 == 0 else p_tp2
                    srcv = ptp[:].rearrange("p (cc t) -> p cc t", cc=8)
                    dstv = bass.AP(xn_grp[:].tensor, (g % 2) * 4096 + b * 128,
                                   [[8192, 128], [512, 8], [1, 128]])
                    eng.tensor_copy(dstv, srcv).then_inc(sems["eV_xg"], 1)

            def _rope(g):
                for r in range(8):
                    rg = 8 * g + r
                    pr = rg % 2
                    is_k = r < 4
                    jc = r % 4
                    eng.wait_ge(sems["eT_mm"], rg + 1)
                    eng.tensor_copy(kraw[:, pr * 512:(pr + 1) * 512],
                                    p_mm[:, pr * 512:(pr + 1) * 512]).then_inc(sems["eV_raw"], 1)
                    qtab_r.wait(eng, g)
                    eng.tensor_tensor(tmp1[:, pr * 512:(pr + 1) * 512],
                                      p_mm[:, pr * 512:(pr + 1) * 512],
                                      tab_c[:, (g % 2) * 512:(g % 2 + 1) * 512],
                                      OP.mult).then_inc(sems["eV_e1"], 1)
                    vch.note(sems["eV_e1"], rg + 1)
                    eng.wait_ge(sems["eT_rot"], rg + 1)
                    vch.inc(eng.tensor_tensor(tmp2[:, pr * 512:(pr + 1) * 512],
                                              p_rot[:, pr * 512:(pr + 1) * 512],
                                              tab_s[:, (g % 2) * 512:(g % 2 + 1) * 512], OP.mult))
                    vch.wait(eng)
                    dst = (k_fm if is_k else q_fm)[:, jc * T + g * 512: jc * T + (g + 1) * 512]
                    eng.tensor_tensor(dst, tmp1[:, pr * 512:(pr + 1) * 512],
                                      tmp2[:, pr * 512:(pr + 1) * 512], OP.add
                                      ).then_inc(sems["eV_fin"], 1)
                for tb in range(4):
                    vb = 4 * g + tb
                    eng.wait_ge(sems["eT_v"], vb + 1)
                    eng.tensor_copy(v_tm[:, vb * 512:(vb + 1) * 512],
                                    p_v[:, (vb % 2) * 512:(vb % 2 + 1) * 512]
                                    ).then_inc(sems["eV_vv"], 1)

            for g in range(NG):
                _evacs(g)
                if g >= 1:
                    _rope(g - 1)
            _rope(NG - 1)

        def p1_tensor(eng):
            eng.wait_ge(sems["qc"], 176)

            def _tp(g):
                for b in range(4):
                    Bi = 4 * g + b
                    pb4 = Bi % 4
                    ptp = p_tp if Bi % 2 == 0 else p_tp2
                    eng.wait_ge(sems["eS_ln"], Bi + 1)
                    if Bi >= 2:
                        eng.wait_ge(sems["eV_xg"], Bi - 1)
                    for j in range(8):
                        i = eng.transpose(ptp[:, j * 128:(j + 1) * 128],
                                          xn_tm[:, pb4 * C + j * 128: pb4 * C + (j + 1) * 128],
                                          ident[:])
                    i.then_inc(sems["eT_tp"], 1)

            def _kvq(g):
                base = (g % 2) * 4096
                for r in range(8):
                    rg = 8 * g + r
                    pr = rg % 2
                    is_k = r < 4
                    jc = r % 4
                    off = (512 + jc * 128) if is_k else (jc * 128)
                    if rg >= 2:
                        eng.wait_ge(sems["eV_e1"], rg - 1)
                    for cc in range(NCH):
                        i = eng.matmul(p_mm[:, pr * 512:(pr + 1) * 512],
                                       wkv_bf[:, cc * 1536 + off: cc * 1536 + off + 128],
                                       xn_grp[:, base + cc * 512: base + (cc + 1) * 512],
                                       start=(cc == 0), stop=(cc == NCH - 1))
                    i.then_inc(sems["eT_mm"], 1)
                    eng.wait_ge(sems["eV_raw"], rg + 1)
                    if rg >= 2:
                        eng.wait_ge(sems["eV_fin"], rg - 1)
                    eng.matmul(p_rot[:, pr * 512:(pr + 1) * 512], permT[:],
                               kraw[:, pr * 512:(pr + 1) * 512],
                               start=True, stop=True).then_inc(sems["eT_rot"], 1)
                for tb in range(4):
                    vb = 4 * g + tb
                    pv_ = vb % 2
                    if vb >= 2:
                        eng.wait_ge(sems["eV_vv"], vb - 1)
                    for cc in range(NCH):
                        i = eng.matmul(p_v[:, pv_ * 512:(pv_ + 1) * 512],
                                       xn_grp[:, base + cc * 512 + tb * 128: base + cc * 512 + (tb + 1) * 128],
                                       wkv_bf[:, cc * 1536 + 1024: cc * 1536 + 1536],
                                       start=(cc == 0), stop=(cc == NCH - 1))
                    i.then_inc(sems["eT_v"], 1)

            for g in range(NG):
                _tp(g)
                if g == 0:
                    qw_r.wait(eng, NCH - 1)
                    qw_r.wait(eng, NCH - 2)
                if g >= 1:
                    eng.wait_ge(sems["eV_xg"], 4 * g)
                    _kvq(g - 1)
            eng.wait_ge(sems["eV_xg"], 4 * NG)
            _kvq(NG - 1)

        p1s.close()
        p2s = contextlib.ExitStack()
        psc = p2s.enter_context(nc.psum_tensor("psc", [128, 3072], F32))
        ppv = p2s.enter_context(nc.psum_tensor("ppv", [128, 512], F32))
        pdn = p2s.enter_context(nc.psum_tensor("pdn", [128, 512], F32))
        pt_buf = p2s.enter_context(nc.sbuf_tensor("pt_buf", [128, 6 * 1024], BF16))
        attn_all = p2s.enter_context(nc.sbuf_tensor("attn_all", [128, 4 * T], BF16))
        dnsp = p2s.enter_context(nc.sbuf_tensor("dnsp", [128, 1024], F32))
        dn_buf = p2s.enter_context(nc.sbuf_tensor("dn_buf", [8, 8 * 512], F32))
        dn_rec = p2s.enter_context(nc.sbuf_tensor("dn_rec", [8, 8 * 512], F32))
        masks4 = p2s.enter_context(nc.sbuf_tensor("masks4", [128, 4 * 1024], BF16))
        bcast_t = p2s.enter_context(nc.sbuf_tensor("bcast_t", [128, 2 * 512], F32))

        STEPS = []
        for qt_ in range(8):
            for pr_ in range(4):
                for kb_ in range(4 * (qt_ + 1)):
                    STEPS.append((qt_, pr_, kb_))
        NSTEP = len(STEPS)
        GEND = {}
        DCTR = {}
        _d = 0
        for si_, st_ in enumerate(STEPS):
            qt_, pr_, kb_ = st_
            GEND[(qt_, pr_)] = si_
            if kb_ >= 4 * (qt_ + 1) - 4:
                _d += 1
                DCTR[si_] = _d

        def _sp_ap(handle, row_off, col_off, rowlen):
            return bass.AP(handle[:].tensor, row_off * rowlen + col_off,
                           [[64 * rowlen, 2], [1, 512]])

        def _pvdn(eng, si):
            qt, pr, kb = STEPS[si]
            nch = 4 * (qt + 1)
            idx2 = 4 * qt + pr
            eng.wait_ge(sems["eS_exp"], si + 1)
            if si in DCTR:
                eng.wait_ge(sems["eV_msk"], DCTR[si])
            if kb == 0 and idx2 >= 1:
                eng.wait_ge(sems["eV_av"], idx2)
                eng.wait_ge(sems["eV_dn"], idx2)
            sp4 = si % 6
            pts = pt_buf[:, sp4 * 1024:(sp4 + 1) * 1024]
            st_, sp_f = (kb == 0), (kb == nch - 1)
            h0 = 2 * pr
            eng.matmul(ppv[0:64, :],
                       v_tm[:, kb * 512 + h0 * 64: kb * 512 + (h0 + 1) * 64],
                       pts[:, 0:512], start=st_, stop=sp_f,
                       tile_position=(0, 0), skip_group_check=True)
            eng.matmul(ppv[64:128, :],
                       v_tm[:, kb * 512 + (h0 + 1) * 64: kb * 512 + (h0 + 2) * 64],
                       pts[:, 512:1024], start=st_, stop=sp_f,
                       tile_position=(0, 64), skip_group_check=True)
            eng.matmul(pdn[0:1, :], ones128[:],
                       pts[:, 0:512], start=st_, stop=sp_f,
                       tile_position=(0, 0), skip_group_check=True)
            eng.matmul(pdn[64:65, :], ones128[:],
                       pts[:, 512:1024], start=st_, stop=sp_f,
                       tile_position=(0, 64), skip_group_check=True
                       ).then_inc(sems["eT_pv"], 1)

        def p2a_tensor(eng):
            eng.wait_ge(sems["bar1"], 5)
            for si in range(NSTEP):
                qt, pr, kb = STEPS[si]
                jc = pr
                sp = si % 3
                if si >= 3:
                    eng.wait_ge(sems["eS_exp"], si - 2)
                for rt in range(2):
                    i = eng.matmul(
                        psc[:, sp * 1024 + rt * 512: sp * 1024 + (rt + 1) * 512],
                        k_fm[rt * 64:(rt + 1) * 64, jc * T + kb * 128: jc * T + (kb + 1) * 128],
                        q_fm[rt * 64:(rt + 1) * 64, jc * T + qt * 512: jc * T + (qt + 1) * 512],
                        start=True, stop=True, tile_position=(rt * 64, 0),
                        skip_group_check=True)
                i.then_inc(sems["eT_sc"], 1)
                if si >= 1:
                    _pvdn(eng, si - 1)
            _pvdn(eng, NSTEP - 1)
            eng.nop().then_inc(sems["bar2aT"], 1)

        def p2a_scalar(eng):
            eng.wait_ge(sems["bar1"], 5)
            for si in range(NSTEP):
                eng.wait_ge(sems["eT_sc"], si + 1)
                if si >= 6 and si % 2 == 0:
                    eng.wait_ge(sems["eT_pv"], si - 4)
                eng.activation(pt_buf[:, (si % 6) * 1024:(si % 6 + 1) * 1024],
                               psc[:, (si % 3) * 1024:(si % 3 + 1) * 1024],
                               AF.Exp, scale=SCALE).then_inc(sems["eS_exp"], 1)

        def p2a_vector(eng):
            eng.wait_ge(sems["bar1"], 5)
            qmk_r.wait(eng, 0)
            for qt in range(8):
                for pr in range(4):
                    idx2 = 4 * qt + pr
                    nch = 4 * (qt + 1)
                    for si in range(GEND[(qt, pr)] - 3, GEND[(qt, pr)] + 1):
                        _q, _p, kb = STEPS[si]
                        r = kb - (nch - 4)
                        sp = si % 6
                        eng.wait_ge(sems["eS_exp"], si + 1)
                        eng.tensor_tensor(pt_buf[:, sp * 1024:(sp + 1) * 1024],
                                          pt_buf[:, sp * 1024:(sp + 1) * 1024],
                                          masks4[:, r * 1024:(r + 1) * 1024],
                                          OP.mult).then_inc(sems["eV_msk"], 1)
                    eng.wait_ge(sems["eT_pv"], GEND[(qt, pr)] + 1)
                    if idx2 >= 1:
                        qdn_r.wait(eng, idx2 - 1)
                    eng.tensor_copy(dnsp[0:1, 0:512], pdn[0:1, :])
                    eng.tensor_copy(dnsp[64:65, 0:512], pdn[64:65, :]
                                    ).then_inc(sems["eV_dn"], 1)
                    eng.tensor_copy(attn_all[:, pr * T + qt * 512: pr * T + (qt + 1) * 512],
                                    ppv[:]).then_inc(sems["eV_av"], 1)
                    vch.note(sems["eV_av"], idx2 + 1)
                qdn_r.wait(eng, 4 * qt + 3)
                qdn_r.wait(eng, 4 * qt + 2)
                vch.wait(eng)
                eng.reciprocal(dn_rec[:, qt * 512:(qt + 1) * 512],
                               dn_buf[:, qt * 512:(qt + 1) * 512]).then_inc(sems["eV_rc"], 1)
                for j in range(4 * qt, 4 * qt + 4):
                    pr2 = j % 4
                    qbc_r.wait(eng, j)
                    eng.tensor_tensor(attn_all[:, pr2 * T + qt * 512: pr2 * T + (qt + 1) * 512],
                                      attn_all[:, pr2 * T + qt * 512: pr2 * T + (qt + 1) * 512],
                                      bcast_t[:, (j % 2) * 512:(j % 2 + 1) * 512],
                                      OP.mult).then_inc(sems["eV_nrm"], 1)

        def p2a_gpsimd(eng):
            eng.wait_ge(sems["bar1"], 5)
            eng.dma_start(out=masks4[:], in_=masks_in).then_inc(qmk_r.s[0], 16)
            for qt in range(8):
                for pr in range(4):
                    idx2 = 4 * qt + pr
                    eng.wait_ge(sems["eV_dn"], idx2 + 1)
                    s_, p_ = qdn_r.tag(idx2)
                    eng.dma_start(out=dn_buf[2 * pr: 2 * pr + 2, qt * 512:(qt + 1) * 512],
                                  in_=_sp_ap(dnsp, 0, 0, 1024)).then_inc(s_, 16)
                eng.wait_ge(sems["eV_rc"], qt + 1)
                for j in range(4 * qt, 4 * qt + 4):
                    pr2 = j % 4
                    if j >= 2:
                        eng.wait_ge(sems["eV_nrm"], j - 1)
                    s_, p_ = qbc_r.tag(j)
                    for half in range(2):
                        row = 2 * pr2 + half
                        srcap = bass.AP(dn_rec[:].tensor, row * (8 * 512) + qt * 512,
                                        [[8 * 512, 1], [0, 64], [1, 512]])
                        eng.dma_start(out=bcast_t[half * 64:(half + 1) * 64,
                                                  (j % 2) * 512:(j % 2 + 1) * 512],
                                      in_=srcap).then_inc(s_, 16)

        p2s.close()
        pav.close()
        p3s = contextlib.ExitStack()
        p_tp3 = p3s.enter_context(nc.psum_tensor("p_tp3", [128, 1024], BF16))
        p_tp4 = p3s.enter_context(nc.psum_tensor("p_tp4", [128, 1024], BF16))
        p_big = p3s.enter_context(nc.psum_tensor("p_big", [128, 3072], F32))
        xn_own = p3s.enter_context(nc.sbuf_tensor("xn_own", [128, 8 * 2048], BF16))
        w1sl = p3s.enter_context(nc.sbuf_tensor("w1sl", [128, 4 * 1024], BF16))
        y_att = p3s.enter_context(nc.sbuf_tensor("y_att", [128, 16 * 1024], BF16))
        x_tb = p3s.enter_context(nc.sbuf_tensor("x_tb", [128, 1024], F32))
        out_tb = p3s.enter_context(nc.sbuf_tensor("out_tb", [128, 1024], F32))
        e_s = p3s.enter_context(nc.sbuf_tensor("e_s", [128, 2 * 8], F32))
        r_sml = p3s.enter_context(nc.sbuf_tensor("r_sml", [128, 16], F32))
        y_ffn_tm = p3s.enter_context(nc.sbuf_tensor("y_ffn_tm", [128, 8 * 1024], BF16))
        attn_tb = p3s.enter_context(nc.sbuf_tensor("attn_tb", [128, 2 * 1024], BF16))
        wo_bf = p3s.enter_context(nc.sbuf_tensor("wo_bf", [128, 8 * 1024], BF16))
        p3b = contextlib.ExitStack()
        h_half = p3b.enter_context(nc.sbuf_tensor("h_half", [128, 32 * 1024], BF16))
        w2sl = p3b.enter_context(nc.sbuf_tensor("w2sl", [128, 2 * 4096], BF16))
        y_fm_s = p3b.enter_context(nc.sbuf_tensor("y_fm_s", [128, 2 * 1024], BF16))
        p3b.close()

        def p2b_sync(eng):
            eng.wait_ge(sems["bar2aT"], 1)
            pid = eng.partition_id()
            with eng.If(pid % 2 == 0):
                for cc in range(8):
                    eng.dma_start(out=xn_own[:, cc * 2048:(cc + 1) * 2048],
                                  in_=xn_spill[cc * 128:(cc + 1) * 128, 0:2048]
                                  ).then_inc(sems["qxn"], 16)
            with eng.Else():
                for cc in range(8):
                    eng.dma_start(out=xn_own[:, cc * 2048:(cc + 1) * 2048],
                                  in_=xn_spill[cc * 128:(cc + 1) * 128, 2048:4096]
                                  ).then_inc(sems["qxn"], 16)
            eng.wait_ge(sems["ecc"], 1)
            with eng.If(pid % 2 == 0):
                for tb in range(16):
                    if tb >= 2:
                        eng.wait_ge(sems["eT_y"], tb - 1)
                    s_, p_ = qat_r.tag(tb)
                    srcap = bass.AP(cc_out_ap.tensor, tb * 128,
                                    [[4096, 128], [128 * 4096, 8], [1, 128]])
                    eng.dma_start(out=attn_tb[:, (tb % 2) * 1024:(tb % 2 + 1) * 1024],
                                  in_=srcap).then_inc(s_, p_)
            with eng.Else():
                for tb in range(16):
                    if tb >= 2:
                        eng.wait_ge(sems["eT_y"], tb - 1)
                    s_, p_ = qat_r.tag(tb)
                    srcap = bass.AP(cc_out_ap.tensor, 2048 + tb * 128,
                                    [[4096, 128], [128 * 4096, 8], [1, 128]])
                    eng.dma_start(out=attn_tb[:, (tb % 2) * 1024:(tb % 2 + 1) * 1024],
                                  in_=srcap).then_inc(s_, p_)
            for u3 in range(16):
                if u3 >= 1:
                    eng.wait_ge(sems["eV_out"], u3)
                s_, p_ = qxo_r.tag(u3)
                eng.dma_start(out=x_tb[:],
                              in_=x_own[u3 * 128:(u3 + 1) * 128, :]).then_inc(s_, p_)
                if u3 >= 1:
                    s2_, p2_ = qst_r.tag(u3 - 1)
                    eng.dma_start(out=y_out[(u3 - 1) * 128:u3 * 128, :],
                                  in_=out_tb[:]).then_inc(s2_, p2_)
            eng.wait_ge(sems["eV_out"], 16)
            s2_, p2_ = qst_r.tag(15)
            eng.dma_start(out=y_out[15 * 128:16 * 128, :], in_=out_tb[:]).then_inc(s2_, p2_)

        def p2b_gpsimd(eng):
            eng.wait_ge(sems["bar2aT"], 1)
            bo_rep = bass.AP(bo_in.tensor, 0, [[C, 1], [0, 128], [1, C]])
            b2_rep = bass.AP(b2_in.tensor, 0, [[C, 1], [0, 128], [1, C]])
            eng.dma_start(out=bb[:], in_=bo_rep).then_inc(sems["qbb"], 16)
            eng.dma_start(out=out_tb[:], in_=b2_rep).then_inc(sems["qbb"], 16)
            for u1 in range(4):
                s_, p_ = qwf1_r.tag(u1)
                srcap = bass.AP(w1_in.tensor, u1 * 128,
                                [[FF, 128], [128 * FF, 8], [1, 128]])
                eng.dma_start(out=w1sl[:, u1 * 1024:(u1 + 1) * 1024],
                              in_=srcap).then_inc(s_, p_)
            eng.wait_ge(sems["eV_nrm"], 32)
            for jc in range(4):
                eng.dma_start(out=cc_in_ap[jc * 128:(jc + 1) * 128, :],
                              in_=attn_all[:, jc * T:(jc + 1) * T]).then_inc(sems["qcc"], 16)
            eng.wait_ge(sems["qcc"], 64)
            eng.collective_compute(
                "AllGather", OP.bypass,
                replica_groups=[[0, 1], [2, 3], [4, 5], [6, 7]],
                ins=[cc_in_ap], outs=[cc_out_ap],
            ).then_inc(sems["ecc"], 1)
            for cc in range(NCH):
                s_, p_ = qwo_r.tag(cc)
                eng.dma_start(out=wo_bf[:, cc * 1024:(cc + 1) * 1024],
                              in_=wo_in[cc * 128:(cc + 1) * 128, :]).then_inc(s_, p_)
            def _w1load(eng, u1):
                fc = u1 % 32
                eng.wait_ge(sems["eT_h"], u1 - 3)
                s_, p_ = qwf1_r.tag(u1)
                srcap = bass.AP(w1_in.tensor, fc * 128,
                                [[FF, 128], [128 * FF, 8], [1, 128]])
                eng.dma_start(out=w1sl[:, (u1 % 4) * 1024:(u1 % 4 + 1) * 1024],
                              in_=srcap).then_inc(s_, p_)

            def _w2load(eng, u2):
                jc = u2 % 8
                if u2 >= 2:
                    eng.wait_ge(sems["eT_f"], u2 - 1)
                s_, p_ = qwf2_r.tag(u2)
                srcap = bass.AP(w2_in.tensor, jc * 128,
                                [[C, 128], [128 * C, 32], [1, 128]])
                eng.dma_start(out=w2sl[:, (u2 % 2) * 4096:(u2 % 2 + 1) * 4096],
                              in_=srcap).then_inc(s_, p_)

            for u1 in range(4, 32):
                _w1load(eng, u1)
            for u2 in range(0, 8):
                _w2load(eng, u2)
            for u1 in range(32, 64):
                _w1load(eng, u1)
            for u2 in range(8, 16):
                _w2load(eng, u2)
            qst_r.wait(eng, 15)
            qst_r.wait(eng, 14)

        def p2b_scalar(eng):
            for tb in range(16):
                eng.wait_ge(sems["eT_r"], tb + 1)
                if tb >= 2:
                    eng.wait_ge(sems["eV_rg"], tb - 1)
                eng.activation(e_s[:, (tb % 2) * 8:(tb % 2 + 1) * 8],
                               p_big[:, 2048 + (tb % 2) * 512: 2048 + (tb % 2) * 512 + 8],
                               AF.Exp).then_inc(sems["eS_re"], 1)
            for ph in range(2):
                for fc in range(32):
                    u1 = ph * 32 + fc
                    if ph == 0 and fc == 0:
                        eng.wait_ge(sems["qcc"], 64)
                    eng.wait_ge(sems["eT_h"], u1 + 1)
                    if ph == 1 and fc == 0:
                        eng.wait_ge(sems["eT_f"], 8)
                    eng.activation(h_half[:, fc * 1024:(fc + 1) * 1024],
                                   p_big[:, (fc % 2) * 1024:(fc % 2 + 1) * 1024],
                                   AF.Gelu_apprx_tanh,
                                   bias=b1r[:, fc:fc + 1]).then_inc(sems["eS_ge"], 1)

        def p2b_vector(eng):
            eng.wait_ge(sems["qbb"], 32)
            vch.inc(eng.tensor_tensor(bb[:], bb[:], out_tb[:], OP.add))
            vch.wait(eng)
            eng.nop().then_inc(sems["eV_bb"], 1)
            for tb in range(16):
                eng.wait_ge(sems["eS_re"], tb + 1)
                ec = e_s[:, (tb % 2) * 8:(tb % 2 + 1) * 8]
                mx1 = r_sml[:, 0:1]; meq = r_sml[:, 2:10]; mx2 = r_sml[:, 1:2]
                s12 = r_sml[:, 10:11]; r12 = r_sml[:, 11:12]
                ga = r_sml[:, 12:13]; gb = r_sml[:, 13:14]
                vch.inc(eng.reduce_max(mx1, ec, AX)); vch.wait(eng)
                vch.inc(eng.tensor_scalar(meq, ec, mx1, None, OP.is_equal)); vch.wait(eng)
                vch.inc(eng.tensor_scalar(meq, meq, -1.0, 1.0, OP.mult, OP.add)); vch.wait(eng)
                vch.inc(eng.tensor_tensor(meq, ec, meq, OP.mult)); vch.wait(eng)
                vch.inc(eng.reduce_max(mx2, meq, AX)); vch.wait(eng)
                vch.inc(eng.tensor_tensor(s12, mx1, mx2, OP.add)); vch.wait(eng)
                vch.inc(eng.reciprocal(r12, s12)); vch.wait(eng)
                vch.inc(eng.tensor_tensor(ga, mx1, r12, OP.mult)); vch.wait(eng)
                vch.inc(eng.tensor_tensor(gb, mx2, r12, OP.mult)); vch.wait(eng)
                vch.inc(eng.tensor_tensor(g_s[:, tb:tb + 1], ga, gb, OP.add))
                vch.wait(eng)
                eng.nop().then_inc(sems["eV_rg"], 1)
            eng.wait_ge(sems["eV_bb"], 1)
            for tb in range(16):
                eng.wait_ge(sems["eT_y"], tb + 1)
                eng.tensor_tensor(y_att[:, tb * 1024:(tb + 1) * 1024],
                                  p_big[:, (tb % 2) * 1024:(tb % 2 + 1) * 1024],
                                  bb[:], OP.add).then_inc(sems["eV_ye"], 1)
            for ph in range(2):

                def _w2evac(eng, jc2):
                    u2c = ph * 8 + jc2
                    eng.wait_ge(sems["eT_f"], u2c + 1)
                    eng.tensor_copy(y_fm_s[:, (jc2 % 2) * 1024:(jc2 % 2 + 1) * 1024],
                                    p_big[:, (jc2 % 2) * 1024:(jc2 % 2 + 1) * 1024]
                                    ).then_inc(sems["eV_fe"], 1)
                    eng.wait_ge(sems["eT_ft"], u2c + 1)
                    if jc2 == 0:
                        eng.wait_ge(sems["eV_rg"], 16)
                    ptp = p_tp3 if jc2 % 2 == 0 else p_tp4
                    for tbi in range(8):
                        i = eng.tensor_scalar_mul(
                            bass.AP(y_ffn_tm[:].tensor, tbi * 1024 + jc2 * 128,
                                    [[8 * 1024, 128], [1, 128]]),
                            ptp[:, tbi * 128:(tbi + 1) * 128],
                            g_s[:, ph * 8 + tbi: ph * 8 + tbi + 1])
                    i.then_inc(sems["eV_ft"], 1)
                    vch.note(sems["eV_ft"], u2c + 1)

                for jc in range(8):
                    _w2evac(eng, jc)
                for tbl in range(8):
                    u3 = ph * 8 + tbl
                    eng.wait_ge(sems["eV_ye"], u3 + 1)
                    qxo_r.wait(eng, u3)
                    if u3 >= 1:
                        qst_r.wait(eng, u3 - 1)
                    vch.wait(eng)
                    vch.inc(eng.tensor_tensor(
                        out_tb[:], y_ffn_tm[:, tbl * 1024:(tbl + 1) * 1024],
                        x_tb[:], OP.add))
                    vch.wait(eng)
                    eng.tensor_tensor(out_tb[:], out_tb[:],
                                      y_att[:, u3 * 1024:(u3 + 1) * 1024],
                                      OP.add).then_inc(sems["eV_out"], 1)

        def p2b_tensor(eng):
            eng.wait_ge(sems["eS_exp"], NSTEP)
            eng.wait_ge(sems["eV_av"], 32)
            eng.wait_ge(sems["eV_dn"], 32)
            eng.wait_ge(sems["qxn"], 128)
            for tb in range(16):
                if tb >= 2:
                    eng.wait_ge(sems["eS_re"], tb - 1)
                for cc in range(NCH):
                    i = eng.matmul(p_big[:, 2048 + (tb % 2) * 512: 2048 + (tb % 2) * 512 + 8],
                                   xn_own[:, cc * 2048 + tb * 128: cc * 2048 + (tb + 1) * 128],
                                   rw_bf[:, cc * 8:(cc + 1) * 8],
                                   start=(cc == 0), stop=(cc == NCH - 1),
                                   skip_group_check=True)
                i.then_inc(sems["eT_r"], 1)
            def _w1(eng, ph):
                for fc in range(32):
                    u1 = ph * 32 + fc
                    qwf1_r.wait(eng, u1)
                    if u1 >= 2:
                        eng.wait_ge(sems["eS_ge"], u1 - 1)
                    for cc in range(NCH):
                        for tt in range(2):
                            i = eng.matmul(
                                p_big[:, (fc % 2) * 1024 + tt * 512:(fc % 2) * 1024 + (tt + 1) * 512],
                                w1sl[:, (u1 % 4) * 1024 + cc * 128:(u1 % 4) * 1024 + (cc + 1) * 128],
                                xn_own[:, cc * 2048 + ph * 1024 + tt * 512: cc * 2048 + ph * 1024 + (tt + 1) * 512],
                                start=(cc == 0), stop=(cc == NCH - 1), skip_group_check=True)
                    i.then_inc(sems["eT_h"], 1)

            _w1(eng, 0)
            qwo_r.wait(eng, NCH - 1)
            qwo_r.wait(eng, NCH - 2)
            eng.wait_ge(sems["eS_ge"], 32)
            for tb in range(16):
                qat_r.wait(eng, tb)
                if tb >= 2:
                    eng.wait_ge(sems["eV_ye"], tb - 1)
                for gc in range(8):
                    for sl in range(2):
                        i = eng.matmul(
                            p_big[:, (tb % 2) * 1024 + sl * 512:(tb % 2) * 1024 + (sl + 1) * 512],
                            attn_tb[:, (tb % 2) * 1024 + gc * 128:(tb % 2) * 1024 + (gc + 1) * 128],
                            wo_bf[:, gc * 1024 + sl * 512: gc * 1024 + (sl + 1) * 512],
                            start=(gc == 0), stop=(gc == NCH - 1), skip_group_check=True)
                i.then_inc(sems["eT_y"], 1)
            eng.wait_ge(sems["eV_ye"], 16)
            for ph in range(2):
                if ph == 1:
                    eng.wait_ge(sems["eV_fe"], 8)
                    _w1(eng, 1)
                eng.wait_ge(sems["eS_ge"], (ph + 1) * 32)
                for jc in range(8):
                    u2 = ph * 8 + jc
                    qwf2_r.wait(eng, u2)
                    if u2 >= 2:
                        eng.wait_ge(sems["eV_fe"], u2 - 1)
                    for fc in range(32):
                        for tt in range(2):
                            i = eng.matmul(
                                p_big[:, (jc % 2) * 1024 + tt * 512:(jc % 2) * 1024 + (tt + 1) * 512],
                                w2sl[:, (u2 % 2) * 4096 + fc * 128:(u2 % 2) * 4096 + (fc + 1) * 128],
                                h_half[:, fc * 1024 + tt * 512: fc * 1024 + (tt + 1) * 512],
                                start=(fc == 0), stop=(fc == 31), skip_group_check=True)
                    i.then_inc(sems["eT_f"], 1)
                    eng.wait_ge(sems["eV_fe"], u2 + 1)
                    if u2 >= 2:
                        eng.wait_ge(sems["eV_ft"], u2 - 1)
                    ptp = p_tp3 if jc % 2 == 0 else p_tp4
                    for tbi in range(8):
                        i = eng.transpose(ptp[:, tbi * 128:(tbi + 1) * 128],
                                          y_fm_s[:, (jc % 2) * 1024 + tbi * 128:(jc % 2) * 1024 + (tbi + 1) * 128],
                                          ident[:])
                    i.then_inc(sems["eT_ft"], 1)


        with nc.Block() as blk:
            full = (debug_phase == 0 or debug_phase >= 3)

            @blk.sync
            def _(eng):
                p1_sync(eng)
                eng.nop().then_inc(sems["bar1"], 1)
                if full:
                    p2b_sync(eng)

            @blk.gpsimd
            def _(eng):
                p1_gpsimd(eng)
                eng.nop().then_inc(sems["bar1"], 1)
                if full:
                    p2a_gpsimd(eng)
                    p2b_gpsimd(eng)
                if debug_phase == 2:
                    p2a_gpsimd(eng)
                    eng.wait_ge(sems["ecc"], 1)
                    eng.dma_start(out=dbg["att"], in_=attn_all[:]).then_inc(sems["qst"], 16)
                    eng.dma_start(out=dbg["ccout"], in_=cc_out_ap).then_inc(sems["qst"], 16)
                    eng.wait_ge(sems["qst"], 32)
                if debug_phase == 1:
                    eng.wait_ge(sems["eV_fin"], 8 * NG)
                    eng.wait_ge(sems["eV_vv"], 4 * NG)
                    for jc in range(4):
                        eng.dma_start(out=dbg["k"][:, jc * T:(jc + 1) * T],
                                      in_=k_fm[:, jc * T:(jc + 1) * T]).then_inc(sems["qst"], 16)
                        eng.dma_start(out=dbg["q"][:, jc * T:(jc + 1) * T],
                                      in_=q_fm[:, jc * T:(jc + 1) * T]).then_inc(sems["qst"], 16)
                    for i8 in range(8):
                        eng.dma_start(out=dbg["v"][:, i8 * 2048:(i8 + 1) * 2048],
                                      in_=v_tm[:, i8 * 2048:(i8 + 1) * 2048]).then_inc(sems["qst"], 16)
                    eng.wait_ge(sems["qst"], 16 * 16)
                    qpc2_r.wait(eng, W1U + W2U - 1)
                    qpc2_r.wait(eng, W1U + W2U - 2)

            @blk.scalar
            def _(eng):
                p1_scalar(eng)
                eng.nop().then_inc(sems["bar1"], 1)
                if debug_phase >= 2 or full:
                    p2a_scalar(eng)
                if full:
                    p2b_scalar(eng)

            @blk.vector
            def _(eng):
                p1_vector(eng)
                eng.nop().then_inc(sems["bar1"], 1)
                if debug_phase >= 2 or full:
                    p2a_vector(eng)
                if full:
                    p2b_vector(eng)

            @blk.tensor
            def _(eng):
                p1_tensor(eng)
                eng.nop().then_inc(sems["bar1"], 1)
                if debug_phase >= 2 or full:
                    p2a_tensor(eng)
                if full:
                    p2b_tensor(eng)

        p3s.close()

    return nc, dbg



def _host_constants():
    half = D // 2
    inv_freq = (1.0 / 10000.0 ** (np.arange(half, dtype=np.float32) / half)).astype(np.float32)
    t = np.arange(T, dtype=np.float32)
    freqs = t[:, None] * inv_freq[None, :]
    cos = np.repeat(np.cos(freqs), 2, axis=-1)
    sin = np.repeat(np.sin(freqs), 2, axis=-1)
    cos_fm = np.concatenate([cos.T, cos.T], axis=0).astype(np.float32)
    sin_fm = np.concatenate([sin.T, sin.T], axis=0).astype(np.float32)
    ident = np.eye(128, dtype=np.float32)
    P = np.zeros((128, 128), dtype=np.float32)
    for i in range(64):
        P[2 * i, 2 * i + 1] = -1.0
        P[2 * i + 1, 2 * i] = 1.0
    permT = P.T.copy()
    masks = np.zeros((128, 4 * 1024), dtype=np.float32)
    ii = np.arange(128)[:, None]
    jj = np.arange(512)[None, :]
    for r in range(4):
        m = (ii + 128 * r <= jj).astype(np.float32)
        masks[:, r * 1024:r * 1024 + 512] = m
        masks[:, r * 1024 + 512:(r + 1) * 1024] = m
    return cos_fm, sin_fm, ident, permT, masks


def make_inputs(full):
    x = np.asarray(full["L0_attn_x"], dtype=np.float32)
    Wqkv = np.asarray(full["L0_attn_Wqkv"], dtype=np.float32)
    Wo = np.asarray(full["L0_attn_Wo"], dtype=np.float32)
    bo = np.asarray(full["L0_attn_bo"], dtype=np.float32)
    rw = np.asarray(full["L0_ffn_router_w"], dtype=np.float32)
    w1 = np.asarray(full["L0_ffn_w1"], dtype=np.float32)
    b1 = np.asarray(full["L0_ffn_b1"], dtype=np.float32)
    w2 = np.asarray(full["L0_ffn_w2"], dtype=np.float32)
    b2 = np.asarray(full["L0_ffn_b2"], dtype=np.float32)
    cos_fm, sin_fm, ident, permT, masks = _host_constants()
    b1r = np.ascontiguousarray(b1.reshape(32, 128).T)
    in_maps = []
    for c in range(8):
        b = c // 2
        hh = c % 2
        cols = np.concatenate([np.arange(512 * hh, 512 * hh + 512),
                               1024 + np.arange(512 * hh, 512 * hh + 512),
                               2048 + np.arange(512 * hh, 512 * hh + 512)])
        in_maps.append({
            "x_in": np.ascontiguousarray(x[b]),
            "x_own": np.ascontiguousarray(x[b, hh * 2048:(hh + 1) * 2048]),
            "wqkv": np.ascontiguousarray(Wqkv[:, cols]),
            "wo_in": Wo,
            "w1_in": w1,
            "w2_in": w2,
            "rw_in": rw,
            "b1r_in": b1r,
            "bo_in": bo.reshape(1, C),
            "b2_in": b2.reshape(1, C),
            "cos_in": cos_fm,
            "sin_in": sin_fm,
            "ident_in": ident,
            "permT_in": permT,
            "masks_in": masks.astype(_bf16),
        })
    return in_maps


_NC_CACHE = {}


def kernel(**inputs):
    in_maps = make_inputs(inputs)
    if "nc" not in _NC_CACHE:
        _NC_CACHE["nc"] = build_program(debug_phase=0)[0]
    nc = _NC_CACHE["nc"]
    res = run_bass_kernel_spmd(nc, in_maps, core_ids=list(range(8)))
    out = np.empty((B, T, C), dtype=np.float32)
    for c in range(8):
        b, hh = c // 2, c % 2
        out[b, hh * 2048:(hh + 1) * 2048, :] = res.results[c]["y_out"]
    return out


# version 16
# speedup vs baseline: 1.0309x; 1.0309x over previous
import contextlib
import numpy as np
import ml_dtypes
_bf16 = ml_dtypes.bfloat16
import concourse.bass as bass
import concourse.mybir as mybir
from concourse.bass_utils import run_bass_kernel_spmd

F32 = mybir.dt.float32
BF16 = mybir.dt.bfloat16
AF = mybir.ActivationFunctionType
OP = mybir.AluOpType
AX = mybir.AxisListType.X

B, T, C, H, D, FF = 4, 4096, 1024, 16, 64, 4096
NG = 8
NB = 32
NCH = 8
SCALE = float(D) ** -0.5

DEBUG = False


def build_program(debug_phase=0):
    nc = bass.Bass("TRN2", target_bir_lowering=False)
    x_in = nc.dram_tensor("x_in", [T, C], F32, kind="ExternalInput").ap()
    x_own = nc.dram_tensor("x_own", [2048, C], F32, kind="ExternalInput").ap()
    wqkv = nc.dram_tensor("wqkv", [C, 1536], F32, kind="ExternalInput").ap()
    wo_in = nc.dram_tensor("wo_in", [C, C], F32, kind="ExternalInput").ap()
    w1_in = nc.dram_tensor("w1_in", [C, FF], F32, kind="ExternalInput").ap()
    w2_in = nc.dram_tensor("w2_in", [FF, C], F32, kind="ExternalInput").ap()
    rw_in = nc.dram_tensor("rw_in", [C, 8], F32, kind="ExternalInput").ap()
    b1r_in = nc.dram_tensor("b1r_in", [128, 32], F32, kind="ExternalInput").ap()
    bo_in = nc.dram_tensor("bo_in", [1, C], F32, kind="ExternalInput").ap()
    b2_in = nc.dram_tensor("b2_in", [1, C], F32, kind="ExternalInput").ap()
    cos_in = nc.dram_tensor("cos_in", [128, T], F32, kind="ExternalInput").ap()
    sin_in = nc.dram_tensor("sin_in", [128, T], F32, kind="ExternalInput").ap()
    ident_in = nc.dram_tensor("ident_in", [128, 128], F32, kind="ExternalInput").ap()
    permT_in = nc.dram_tensor("permT_in", [128, 128], F32, kind="ExternalInput").ap()
    masks_in = nc.dram_tensor("masks_in", [128, 4 * 1024], BF16, kind="ExternalInput").ap()
    y_out = nc.dram_tensor("y_out", [2048, C], F32, kind="ExternalOutput").ap()

    xn_spill = nc.dram_tensor("xn_spill", [1024, 4096], BF16, kind="Internal").ap()
    cc_in = nc.dram_tensor("cc_in", [512, 4096], BF16, kind="Internal")
    cc_out = nc.dram_tensor("cc_out", [1024, 4096], BF16, kind="Internal")
    cc_in_ap = cc_in.ap()
    cc_out_ap = cc_out.ap()

    dbg = {}
    if debug_phase >= 1:
        dbg["k"] = nc.dram_tensor("dbg_k", [128, 4 * T], BF16, kind="ExternalOutput").ap()
        dbg["q"] = nc.dram_tensor("dbg_q", [128, 4 * T], BF16, kind="ExternalOutput").ap()
        dbg["v"] = nc.dram_tensor("dbg_v", [128, 32 * 512], BF16, kind="ExternalOutput").ap()
    if debug_phase >= 2:
        dbg["att"] = nc.dram_tensor("dbg_att", [128, 4 * T], BF16, kind="ExternalOutput").ap()
        dbg["ccout"] = nc.dram_tensor("dbg_ccout", [1024, 4096], BF16, kind="ExternalOutput").ap()

    est = contextlib.ExitStack()
    with est:
        ident = est.enter_context(nc.sbuf_tensor("ident", [128, 128], BF16))
        permT = est.enter_context(nc.sbuf_tensor("permT", [128, 128], BF16))
        ones128 = est.enter_context(nc.sbuf_tensor("ones128", [128, 1], BF16))
        b1r = est.enter_context(nc.sbuf_tensor("b1r", [128, 32], F32))
        rw_fm = est.enter_context(nc.sbuf_tensor("rw_fm", [128, 8 * 8], F32))
        bb = est.enter_context(nc.sbuf_tensor("bb", [128, C], F32))
        g_s = est.enter_context(nc.sbuf_tensor("g_s", [128, 16], F32))
        eps_s = est.enter_context(nc.sbuf_tensor("eps_s", [128, 1], F32))
        rw_bf = est.enter_context(nc.sbuf_tensor("rw_bf", [128, 8 * 8], BF16))

        sems = {}
        def sem(name):
            sems[name] = est.enter_context(nc.semaphore(name))
            return sems[name]
        class DmaRing:
            def __init__(self, name, per=16, ways=2):
                self.ways = ways
                self.s = [est.enter_context(nc.semaphore(name + str(k)))
                          for k in range(ways)]
                self.per = per
            def tag(self, i):
                return self.s[i % self.ways], self.per
            def tag6(self, i):
                return self.s[i % self.ways], self.per
            def wait(self, eng, i):
                eng.wait_ge(self.s[i % self.ways], self.per * (i // self.ways + 1))

        class Chain:
            def __init__(self, name):
                self.h = est.enter_context(nc.semaphore(name))
                self.n = 0
                self.last = None
            def inc(self, instr):
                instr.then_inc(self.h, 1)
                self.n += 1
                self.last = (self.h, self.n)
                return instr
            def note(self, semh, val):
                self.last = (semh, val)
            def wait(self, eng):
                if self.last is not None:
                    eng.wait_ge(self.last[0], self.last[1])

        vch = Chain("vch")
        sch = Chain("sch")
        tch = Chain("tch")
        gch = Chain("gch")
        qx_r = DmaRing("qx", 16, ways=4)
        qat_r = DmaRing("qat_r", 16)
        qwo_r = DmaRing("qwo_r", 16)
        qwf1_r = DmaRing("qwf1_r", 16, ways=4)
        qwf2_r = DmaRing("qwf2_r", 16)
        qxo_r = DmaRing("qxo_r", 16)
        qst_r = DmaRing("qst_r", 16)
        qdn_r = DmaRing("qdn_r", 16)
        qbc_r = DmaRing("qbc_r", 32)
        qmk_r = DmaRing("qmk_r", 16)
        qxs_r = DmaRing("qxs_r", 128)
        qw_r = DmaRing("qw", 16)
        qtab_r = DmaRing("qtab", 32)
        qpc1_r = DmaRing("qpc1", 16)
        qpc2_r = DmaRing("qpc2", 16)
        for n in ["qc",
                  "eS_sq", "eV_cf1", "eS_rstd", "eV_cf2", "eS_ln",
                  "eT_tp", "eV_xg", "eT_mm", "eV_raw", "eT_rot", "eV_e1", "eV_fin",
                  "eT_v", "eV_vv", "eV_wc", "eV_pc", "bar1",
                  "eT_sc", "eS_exp", "eV_msk", "eT_pv", "eV_av", "eV_dn", "eV_rc", "eV_nrm",
                  "qcc", "ecc", "bar2aT", "qxn", "qbb", "eV_bb", "eV_mk",
                  "eS_sq2", "eV_cf12", "eS_rstd2", "eV_cf22", "eS_ln2",
                  "eT_tp2", "eV_xg2", "eV_woc", "eT_y", "eV_ye",
                  "eT_r", "eS_re", "eV_rg",
                  "eT_h", "eS_ge", "eT_f", "eV_w1c", "eV_w2c",
                  "eV_fe", "eT_ft", "eV_ft", "eV_out", "qst"]:
            sem(n)

        pav = contextlib.ExitStack()
        k_fm = pav.enter_context(nc.sbuf_tensor("k_fm", [128, 4 * T], BF16))
        q_fm = pav.enter_context(nc.sbuf_tensor("q_fm", [128, 4 * T], BF16))
        v_tm = pav.enter_context(nc.sbuf_tensor("v_tm", [128, 32 * 512], BF16))
        p1s = contextlib.ExitStack()
        p_tp = p1s.enter_context(nc.psum_tensor("p_tp", [128, 1024], BF16))
        p_tp2 = p1s.enter_context(nc.psum_tensor("p_tp2", [128, 1024], BF16))
        p_mm = p1s.enter_context(nc.psum_tensor("p_mm", [128, 2 * 512], F32))
        p_rot = p1s.enter_context(nc.psum_tensor("p_rot", [128, 2 * 512], F32))
        p_v = p1s.enter_context(nc.psum_tensor("p_v", [128, 2 * 512], F32))
        x_blk = p1s.enter_context(nc.sbuf_tensor("x_blk", [128, 4 * C], F32))
        xn_tm = p1s.enter_context(nc.sbuf_tensor("xn_tm", [128, 4 * C], BF16))
        xn_grp = p1s.enter_context(nc.sbuf_tensor("xn_grp", [128, 2 * 8 * 512], BF16))
        wkv_bf = p1s.enter_context(nc.sbuf_tensor("wkv_bf", [128, 8 * 1536], BF16))
        tab_c = p1s.enter_context(nc.sbuf_tensor("tab_c", [128, 2 * 512], F32))
        tab_s = p1s.enter_context(nc.sbuf_tensor("tab_s", [128, 2 * 512], F32))
        kraw = p1s.enter_context(nc.sbuf_tensor("kraw", [128, 2 * 512], BF16))
        tmp1 = p1s.enter_context(nc.sbuf_tensor("tmp1", [128, 2 * 512], BF16))
        tmp2 = p1s.enter_context(nc.sbuf_tensor("tmp2", [128, 2 * 512], BF16))
        sum_s = p1s.enter_context(nc.sbuf_tensor("sum_s", [128, 4], F32))
        sq_s = p1s.enter_context(nc.sbuf_tensor("sq_s", [128, 4], F32))
        m_s = p1s.enter_context(nc.sbuf_tensor("m_s", [128, 4], F32))
        var_s = p1s.enter_context(nc.sbuf_tensor("var_s", [128, 4], F32))
        rstd_s = p1s.enter_context(nc.sbuf_tensor("rstd_s", [128, 4], F32))
        bneg_s = p1s.enter_context(nc.sbuf_tensor("bneg_s", [128, 4], F32))
        scr = p1s.enter_context(nc.sbuf_tensor("scr", [128, C], BF16))

        W1U = 16
        W2U = 32


        def p1_sync(eng):
            for Bi in range(NB):
                if Bi >= 4:
                    eng.wait_ge(sems["eS_ln"], Bi - 3)
                s_, p_ = qx_r.tag(Bi)
                eng.dma_start(out=x_blk[:, (Bi % 4) * C:(Bi % 4 + 1) * C],
                              in_=x_in[Bi * 128:(Bi + 1) * 128, :]).then_inc(s_, p_)

        def p1_gpsimd(eng):
            eng.dma_start(out=ident[:], in_=ident_in).then_inc(sems["qc"], 16)
            eng.dma_start(out=permT[:], in_=permT_in).then_inc(sems["qc"], 16)
            eng.dma_start(out=b1r[:], in_=b1r_in).then_inc(sems["qc"], 16)
            for cc in range(NCH):
                eng.dma_start(out=rw_fm[:, cc * 8:(cc + 1) * 8],
                              in_=rw_in[cc * 128:(cc + 1) * 128, :]).then_inc(sems["qc"], 16)
            for cc in range(NCH):
                s_, p_ = qw_r.tag(cc)
                eng.dma_start(out=wkv_bf[:, cc * 1536:(cc + 1) * 1536],
                              in_=wqkv[cc * 128:(cc + 1) * 128, :]).then_inc(s_, p_)
            for g in range(NG):
                if g >= 2:
                    eng.wait_ge(sems["eV_fin"], 8 * (g - 1))
                s_, _ = qtab_r.tag(g)
                eng.dma_start(out=tab_c[:, (g % 2) * 512:(g % 2 + 1) * 512],
                              in_=cos_in[:, g * 512:(g + 1) * 512]).then_inc(s_, 16)
                eng.dma_start(out=tab_s[:, (g % 2) * 512:(g % 2 + 1) * 512],
                              in_=sin_in[:, g * 512:(g + 1) * 512]).then_inc(s_, 16)
                if g >= 1:
                    eng.wait_ge(sems["eV_xg"], 4 * g)
                    sx_, px_ = qxs_r.tag(g - 1)
                    gb = ((g - 1) % 2) * 4096
                    for cc in range(NCH):
                        eng.dma_start(out=xn_spill[cc * 128:(cc + 1) * 128,
                                                   (g - 1) * 512: g * 512],
                                      in_=xn_grp[:, gb + cc * 512: gb + (cc + 1) * 512]).then_inc(sx_, 16)
            eng.wait_ge(sems["eV_xg"], 4 * NG)
            sx_, px_ = qxs_r.tag(NG - 1)
            gb = ((NG - 1) % 2) * 4096
            for cc in range(NCH):
                eng.dma_start(out=xn_spill[cc * 128:(cc + 1) * 128,
                                           (NG - 1) * 512: NG * 512],
                              in_=xn_grp[:, gb + cc * 512: gb + (cc + 1) * 512]).then_inc(sx_, 16)
            qxs_r.wait(eng, NG - 1)
            qxs_r.wait(eng, NG - 2)
        def p1_scalar(eng):
            eng.wait_ge(sems["eV_cf2"], 1)
            for Bi in range(NB):
                pb = Bi % 4
                qx_r.wait(eng, Bi)
                if Bi >= 4:
                    eng.wait_ge(sems["eT_tp"], Bi - 3)
                eng.activation(scr[:], x_blk[:, pb * C:(pb + 1) * C], AF.Square,
                               accum_out=sq_s[:, pb:pb + 1])
                sch.inc(eng.activation(xn_tm[:, pb * C:(pb + 1) * C],
                                       x_blk[:, pb * C:(pb + 1) * C], AF.Copy,
                                       accum_out=sum_s[:, pb:pb + 1]))
                sch.wait(eng)
                sch.inc(eng.mul(m_s[:, pb:pb + 1], sum_s[:, pb:pb + 1], 1.0 / C))
                sch.wait(eng)
                sch.inc(eng.activation(var_s[:, pb:pb + 1], m_s[:, pb:pb + 1], AF.Square))
                sch.wait(eng)
                sch.inc(eng.mul(var_s[:, pb:pb + 1], var_s[:, pb:pb + 1], -1.0))
                sch.wait(eng)
                sch.inc(eng.activation(var_s[:, pb:pb + 1], sq_s[:, pb:pb + 1], AF.Identity,
                                       scale=1.0 / C, bias=var_s[:, pb:pb + 1]))
                sch.wait(eng)
                sch.inc(eng.activation(var_s[:, pb:pb + 1], var_s[:, pb:pb + 1], AF.Ln,
                                       bias=eps_s[:, 0:1]))
                sch.wait(eng)
                sch.inc(eng.activation(rstd_s[:, pb:pb + 1], var_s[:, pb:pb + 1], AF.Exp,
                                       scale=-0.5))
                sch.wait(eng)
                sch.inc(eng.activation(bneg_s[:, pb:pb + 1], m_s[:, pb:pb + 1], AF.Identity,
                                       scale=rstd_s[:, pb:pb + 1]))
                sch.wait(eng)
                sch.inc(eng.mul(bneg_s[:, pb:pb + 1], bneg_s[:, pb:pb + 1], -1.0))
                sch.wait(eng)
                eng.activation(xn_tm[:, pb * C:(pb + 1) * C], x_blk[:, pb * C:(pb + 1) * C],
                               AF.Identity, bias=bneg_s[:, pb:pb + 1],
                               scale=rstd_s[:, pb:pb + 1]).then_inc(sems["eS_ln"], 1)

        def p1_vector(eng):
            eng.memset(eps_s[:], 1e-5)
            eng.memset(ones128[:], 1.0).then_inc(sems["eV_cf2"], 1)
            eng.wait_ge(sems["qc"], 176)
            eng.tensor_copy(rw_bf[:], rw_fm[:])

            def _evacs(g):
                for b in range(4):
                    Bi = 4 * g + b
                    eng.wait_ge(sems["eT_tp"], Bi + 1)
                    if b == 0 and g >= 2:
                        qxs_r.wait(eng, g - 2)
                    ptp = p_tp if Bi % 2 == 0 else p_tp2
                    srcv = ptp[:].rearrange("p (cc t) -> p cc t", cc=8)
                    dstv = bass.AP(xn_grp[:].tensor, (g % 2) * 4096 + b * 128,
                                   [[8192, 128], [512, 8], [1, 128]])
                    eng.tensor_copy(dstv, srcv).then_inc(sems["eV_xg"], 1)

            def _rope(g):
                for r in range(8):
                    rg = 8 * g + r
                    pr = rg % 2
                    is_k = r < 4
                    jc = r % 4
                    eng.wait_ge(sems["eT_mm"], rg + 1)
                    eng.tensor_copy(kraw[:, pr * 512:(pr + 1) * 512],
                                    p_mm[:, pr * 512:(pr + 1) * 512]).then_inc(sems["eV_raw"], 1)
                    qtab_r.wait(eng, g)
                    eng.tensor_tensor(tmp1[:, pr * 512:(pr + 1) * 512],
                                      p_mm[:, pr * 512:(pr + 1) * 512],
                                      tab_c[:, (g % 2) * 512:(g % 2 + 1) * 512],
                                      OP.mult).then_inc(sems["eV_e1"], 1)
                    vch.note(sems["eV_e1"], rg + 1)
                    eng.wait_ge(sems["eT_rot"], rg + 1)
                    vch.inc(eng.tensor_tensor(tmp2[:, pr * 512:(pr + 1) * 512],
                                              p_rot[:, pr * 512:(pr + 1) * 512],
                                              tab_s[:, (g % 2) * 512:(g % 2 + 1) * 512], OP.mult))
                    vch.wait(eng)
                    dst = (k_fm if is_k else q_fm)[:, jc * T + g * 512: jc * T + (g + 1) * 512]
                    eng.tensor_tensor(dst, tmp1[:, pr * 512:(pr + 1) * 512],
                                      tmp2[:, pr * 512:(pr + 1) * 512], OP.add
                                      ).then_inc(sems["eV_fin"], 1)
                for tb in range(4):
                    vb = 4 * g + tb
                    eng.wait_ge(sems["eT_v"], vb + 1)
                    eng.tensor_copy(v_tm[:, vb * 512:(vb + 1) * 512],
                                    p_v[:, (vb % 2) * 512:(vb % 2 + 1) * 512]
                                    ).then_inc(sems["eV_vv"], 1)

            for g in range(NG):
                _evacs(g)
                if g >= 1:
                    _rope(g - 1)
            _rope(NG - 1)

        def p1_tensor(eng):
            eng.wait_ge(sems["qc"], 176)

            def _tp(g):
                for b in range(4):
                    Bi = 4 * g + b
                    pb4 = Bi % 4
                    ptp = p_tp if Bi % 2 == 0 else p_tp2
                    eng.wait_ge(sems["eS_ln"], Bi + 1)
                    if Bi >= 2:
                        eng.wait_ge(sems["eV_xg"], Bi - 1)
                    for j in range(8):
                        i = eng.transpose(ptp[:, j * 128:(j + 1) * 128],
                                          xn_tm[:, pb4 * C + j * 128: pb4 * C + (j + 1) * 128],
                                          ident[:])
                    i.then_inc(sems["eT_tp"], 1)

            def _kvq(g):
                base = (g % 2) * 4096
                for r in range(8):
                    rg = 8 * g + r
                    pr = rg % 2
                    is_k = r < 4
                    jc = r % 4
                    off = (512 + jc * 128) if is_k else (jc * 128)
                    if rg >= 2:
                        eng.wait_ge(sems["eV_e1"], rg - 1)
                    for cc in range(NCH):
                        i = eng.matmul(p_mm[:, pr * 512:(pr + 1) * 512],
                                       wkv_bf[:, cc * 1536 + off: cc * 1536 + off + 128],
                                       xn_grp[:, base + cc * 512: base + (cc + 1) * 512],
                                       start=(cc == 0), stop=(cc == NCH - 1))
                    i.then_inc(sems["eT_mm"], 1)
                    eng.wait_ge(sems["eV_raw"], rg + 1)
                    if rg >= 2:
                        eng.wait_ge(sems["eV_fin"], rg - 1)
                    eng.matmul(p_rot[:, pr * 512:(pr + 1) * 512], permT[:],
                               kraw[:, pr * 512:(pr + 1) * 512],
                               start=True, stop=True).then_inc(sems["eT_rot"], 1)
                for tb in range(4):
                    vb = 4 * g + tb
                    pv_ = vb % 2
                    if vb >= 2:
                        eng.wait_ge(sems["eV_vv"], vb - 1)
                    for cc in range(NCH):
                        i = eng.matmul(p_v[:, pv_ * 512:(pv_ + 1) * 512],
                                       xn_grp[:, base + cc * 512 + tb * 128: base + cc * 512 + (tb + 1) * 128],
                                       wkv_bf[:, cc * 1536 + 1024: cc * 1536 + 1536],
                                       start=(cc == 0), stop=(cc == NCH - 1))
                    i.then_inc(sems["eT_v"], 1)

            for g in range(NG):
                _tp(g)
                if g == 0:
                    qw_r.wait(eng, NCH - 1)
                    qw_r.wait(eng, NCH - 2)
                if g >= 1:
                    eng.wait_ge(sems["eV_xg"], 4 * g)
                    _kvq(g - 1)
            eng.wait_ge(sems["eV_xg"], 4 * NG)
            _kvq(NG - 1)

        p1s.close()
        p2s = contextlib.ExitStack()
        psc = p2s.enter_context(nc.psum_tensor("psc", [128, 2048], F32))
        ppv = p2s.enter_context(nc.psum_tensor("ppv", [128, 1024], F32))
        pdn = p2s.enter_context(nc.psum_tensor("pdn", [128, 1024], F32))
        pt_buf = p2s.enter_context(nc.sbuf_tensor("pt_buf", [128, 6 * 1024], BF16))
        attn_all = p2s.enter_context(nc.sbuf_tensor("attn_all", [128, 4 * T], BF16))
        dnsp = p2s.enter_context(nc.sbuf_tensor("dnsp", [128, 1024], F32))
        dn_buf = p2s.enter_context(nc.sbuf_tensor("dn_buf", [8, 8 * 512], F32))
        dn_rec = p2s.enter_context(nc.sbuf_tensor("dn_rec", [8, 8 * 512], F32))
        masks4 = p2s.enter_context(nc.sbuf_tensor("masks4", [128, 4 * 1024], BF16))
        bcast_t = p2s.enter_context(nc.sbuf_tensor("bcast_t", [128, 2 * 512], F32))

        STEPS = []
        for qt_ in range(8):
            for pr_ in range(4):
                for kb_ in range(4 * (qt_ + 1)):
                    STEPS.append((qt_, pr_, kb_))
        NSTEP = len(STEPS)
        GEND = {}
        DCTR = {}
        _d = 0
        for si_, st_ in enumerate(STEPS):
            qt_, pr_, kb_ = st_
            GEND[(qt_, pr_)] = si_
            if kb_ >= 4 * (qt_ + 1) - 4:
                _d += 1
                DCTR[si_] = _d

        def _sp_ap(handle, row_off, col_off, rowlen):
            return bass.AP(handle[:].tensor, row_off * rowlen + col_off,
                           [[64 * rowlen, 2], [1, 512]])

        def _pvdn(eng, si):
            qt, pr, kb = STEPS[si]
            nch = 4 * (qt + 1)
            idx2 = 4 * qt + pr
            eng.wait_ge(sems["eS_exp"], si + 1)
            if si in DCTR:
                eng.wait_ge(sems["eV_msk"], DCTR[si])
            if kb == 0 and idx2 >= 2:
                eng.wait_ge(sems["eV_av"], idx2 - 1)
                eng.wait_ge(sems["eV_dn"], idx2 - 1)
            sp4 = si % 6
            pc = (idx2 % 2) * 512
            pts = pt_buf[:, sp4 * 1024:(sp4 + 1) * 1024]
            st_, sp_f = (kb == 0), (kb == nch - 1)
            h0 = 2 * pr
            eng.matmul(ppv[0:64, pc:pc + 512],
                       v_tm[:, kb * 512 + h0 * 64: kb * 512 + (h0 + 1) * 64],
                       pts[:, 0:512], start=st_, stop=sp_f,
                       tile_position=(0, 0), skip_group_check=True)
            eng.matmul(ppv[64:128, pc:pc + 512],
                       v_tm[:, kb * 512 + (h0 + 1) * 64: kb * 512 + (h0 + 2) * 64],
                       pts[:, 512:1024], start=st_, stop=sp_f,
                       tile_position=(0, 64), skip_group_check=True)
            eng.matmul(pdn[0:1, pc:pc + 512], ones128[:],
                       pts[:, 0:512], start=st_, stop=sp_f,
                       tile_position=(0, 0), skip_group_check=True)
            eng.matmul(pdn[64:65, pc:pc + 512], ones128[:],
                       pts[:, 512:1024], start=st_, stop=sp_f,
                       tile_position=(0, 64), skip_group_check=True
                       ).then_inc(sems["eT_pv"], 1)

        def p2a_tensor(eng):
            eng.wait_ge(sems["bar1"], 5)
            for si in range(NSTEP):
                qt, pr, kb = STEPS[si]
                jc = pr
                sp = si % 2
                if si >= 2:
                    eng.wait_ge(sems["eS_exp"], si - 1)
                for rt in range(2):
                    i = eng.matmul(
                        psc[:, sp * 1024 + rt * 512: sp * 1024 + (rt + 1) * 512],
                        k_fm[rt * 64:(rt + 1) * 64, jc * T + kb * 128: jc * T + (kb + 1) * 128],
                        q_fm[rt * 64:(rt + 1) * 64, jc * T + qt * 512: jc * T + (qt + 1) * 512],
                        start=True, stop=True, tile_position=(rt * 64, 0),
                        skip_group_check=True)
                i.then_inc(sems["eT_sc"], 1)
                if si >= 1:
                    _pvdn(eng, si - 1)
            _pvdn(eng, NSTEP - 1)
            eng.nop().then_inc(sems["bar2aT"], 1)

        def p2a_scalar(eng):
            eng.wait_ge(sems["bar1"], 5)
            for si in range(NSTEP):
                eng.wait_ge(sems["eT_sc"], si + 1)
                if si >= 6 and si % 2 == 0:
                    eng.wait_ge(sems["eT_pv"], si - 4)
                eng.activation(pt_buf[:, (si % 6) * 1024:(si % 6 + 1) * 1024],
                               psc[:, (si % 2) * 1024:(si % 2 + 1) * 1024],
                               AF.Exp, scale=SCALE).then_inc(sems["eS_exp"], 1)

        def p2a_vector(eng):
            eng.wait_ge(sems["bar1"], 5)
            qmk_r.wait(eng, 0)
            for qt in range(8):
                for pr in range(4):
                    idx2 = 4 * qt + pr
                    nch = 4 * (qt + 1)
                    for si in range(GEND[(qt, pr)] - 3, GEND[(qt, pr)] + 1):
                        _q, _p, kb = STEPS[si]
                        r = kb - (nch - 4)
                        sp = si % 6
                        eng.wait_ge(sems["eS_exp"], si + 1)
                        eng.tensor_tensor(pt_buf[:, sp * 1024:(sp + 1) * 1024],
                                          pt_buf[:, sp * 1024:(sp + 1) * 1024],
                                          masks4[:, r * 1024:(r + 1) * 1024],
                                          OP.mult).then_inc(sems["eV_msk"], 1)
                    eng.wait_ge(sems["eT_pv"], GEND[(qt, pr)] + 1)
                    pc = (idx2 % 2) * 512
                    if idx2 >= 2:
                        qdn_r.wait(eng, idx2 - 2)
                    eng.tensor_copy(dnsp[0:1, pc:pc + 512], pdn[0:1, pc:pc + 512])
                    eng.tensor_copy(dnsp[64:65, pc:pc + 512], pdn[64:65, pc:pc + 512]
                                    ).then_inc(sems["eV_dn"], 1)
                    eng.tensor_copy(attn_all[:, pr * T + qt * 512: pr * T + (qt + 1) * 512],
                                    ppv[:, pc:pc + 512]).then_inc(sems["eV_av"], 1)
                    vch.note(sems["eV_av"], idx2 + 1)
                qdn_r.wait(eng, 4 * qt + 3)
                qdn_r.wait(eng, 4 * qt + 2)
                vch.wait(eng)
                eng.reciprocal(dn_rec[:, qt * 512:(qt + 1) * 512],
                               dn_buf[:, qt * 512:(qt + 1) * 512]).then_inc(sems["eV_rc"], 1)
                for j in range(4 * qt, 4 * qt + 4):
                    pr2 = j % 4
                    qbc_r.wait(eng, j)
                    eng.tensor_tensor(attn_all[:, pr2 * T + qt * 512: pr2 * T + (qt + 1) * 512],
                                      attn_all[:, pr2 * T + qt * 512: pr2 * T + (qt + 1) * 512],
                                      bcast_t[:, (j % 2) * 512:(j % 2 + 1) * 512],
                                      OP.mult).then_inc(sems["eV_nrm"], 1)

        def p2a_gpsimd(eng):
            eng.wait_ge(sems["bar1"], 5)
            eng.dma_start(out=masks4[:], in_=masks_in).then_inc(qmk_r.s[0], 16)
            for qt in range(8):
                for pr in range(4):
                    idx2 = 4 * qt + pr
                    eng.wait_ge(sems["eV_dn"], idx2 + 1)
                    s_, p_ = qdn_r.tag(idx2)
                    eng.dma_start(out=dn_buf[2 * pr: 2 * pr + 2, qt * 512:(qt + 1) * 512],
                                  in_=_sp_ap(dnsp, 0, (idx2 % 2) * 512, 1024)).then_inc(s_, 16)
                eng.wait_ge(sems["eV_rc"], qt + 1)
                for j in range(4 * qt, 4 * qt + 4):
                    pr2 = j % 4
                    if j >= 2:
                        eng.wait_ge(sems["eV_nrm"], j - 1)
                    s_, p_ = qbc_r.tag(j)
                    for half in range(2):
                        row = 2 * pr2 + half
                        srcap = bass.AP(dn_rec[:].tensor, row * (8 * 512) + qt * 512,
                                        [[8 * 512, 1], [0, 64], [1, 512]])
                        eng.dma_start(out=bcast_t[half * 64:(half + 1) * 64,
                                                  (j % 2) * 512:(j % 2 + 1) * 512],
                                      in_=srcap).then_inc(s_, 16)

        p2s.close()
        pav.close()
        p3s = contextlib.ExitStack()
        p_tp3 = p3s.enter_context(nc.psum_tensor("p_tp3", [128, 1024], BF16))
        p_tp4 = p3s.enter_context(nc.psum_tensor("p_tp4", [128, 1024], BF16))
        p_big = p3s.enter_context(nc.psum_tensor("p_big", [128, 3072], F32))
        xn_own = p3s.enter_context(nc.sbuf_tensor("xn_own", [128, 8 * 2048], BF16))
        w1sl = p3s.enter_context(nc.sbuf_tensor("w1sl", [128, 4 * 1024], BF16))
        y_att = p3s.enter_context(nc.sbuf_tensor("y_att", [128, 16 * 1024], BF16))
        x_tb = p3s.enter_context(nc.sbuf_tensor("x_tb", [128, 1024], F32))
        out_tb = p3s.enter_context(nc.sbuf_tensor("out_tb", [128, 1024], F32))
        e_s = p3s.enter_context(nc.sbuf_tensor("e_s", [128, 2 * 8], F32))
        r_sml = p3s.enter_context(nc.sbuf_tensor("r_sml", [128, 16], F32))
        y_ffn_tm = p3s.enter_context(nc.sbuf_tensor("y_ffn_tm", [128, 8 * 1024], BF16))
        attn_tb = p3s.enter_context(nc.sbuf_tensor("attn_tb", [128, 2 * 1024], BF16))
        wo_bf = p3s.enter_context(nc.sbuf_tensor("wo_bf", [128, 8 * 1024], BF16))
        p3b = contextlib.ExitStack()
        h_half = p3b.enter_context(nc.sbuf_tensor("h_half", [128, 32 * 1024], BF16))
        w2sl = p3b.enter_context(nc.sbuf_tensor("w2sl", [128, 2 * 4096], BF16))
        y_fm_s = p3b.enter_context(nc.sbuf_tensor("y_fm_s", [128, 2 * 1024], BF16))
        p3b.close()

        def p2b_sync(eng):
            eng.wait_ge(sems["bar2aT"], 1)
            pid = eng.partition_id()
            with eng.If(pid % 2 == 0):
                for cc in range(8):
                    eng.dma_start(out=xn_own[:, cc * 2048:(cc + 1) * 2048],
                                  in_=xn_spill[cc * 128:(cc + 1) * 128, 0:2048]
                                  ).then_inc(sems["qxn"], 16)
            with eng.Else():
                for cc in range(8):
                    eng.dma_start(out=xn_own[:, cc * 2048:(cc + 1) * 2048],
                                  in_=xn_spill[cc * 128:(cc + 1) * 128, 2048:4096]
                                  ).then_inc(sems["qxn"], 16)
            eng.wait_ge(sems["ecc"], 1)
            with eng.If(pid % 2 == 0):
                for tb in range(16):
                    if tb >= 2:
                        eng.wait_ge(sems["eT_y"], tb - 1)
                    s_, p_ = qat_r.tag(tb)
                    srcap = bass.AP(cc_out_ap.tensor, tb * 128,
                                    [[4096, 128], [128 * 4096, 8], [1, 128]])
                    eng.dma_start(out=attn_tb[:, (tb % 2) * 1024:(tb % 2 + 1) * 1024],
                                  in_=srcap).then_inc(s_, p_)
            with eng.Else():
                for tb in range(16):
                    if tb >= 2:
                        eng.wait_ge(sems["eT_y"], tb - 1)
                    s_, p_ = qat_r.tag(tb)
                    srcap = bass.AP(cc_out_ap.tensor, 2048 + tb * 128,
                                    [[4096, 128], [128 * 4096, 8], [1, 128]])
                    eng.dma_start(out=attn_tb[:, (tb % 2) * 1024:(tb % 2 + 1) * 1024],
                                  in_=srcap).then_inc(s_, p_)
            for u3 in range(16):
                if u3 >= 1:
                    eng.wait_ge(sems["eV_out"], u3)
                s_, p_ = qxo_r.tag(u3)
                eng.dma_start(out=x_tb[:],
                              in_=x_own[u3 * 128:(u3 + 1) * 128, :]).then_inc(s_, p_)
                if u3 >= 1:
                    s2_, p2_ = qst_r.tag(u3 - 1)
                    eng.dma_start(out=y_out[(u3 - 1) * 128:u3 * 128, :],
                                  in_=out_tb[:]).then_inc(s2_, p2_)
            eng.wait_ge(sems["eV_out"], 16)
            s2_, p2_ = qst_r.tag(15)
            eng.dma_start(out=y_out[15 * 128:16 * 128, :], in_=out_tb[:]).then_inc(s2_, p2_)

        def p2b_gpsimd(eng):
            eng.wait_ge(sems["bar2aT"], 1)
            bo_rep = bass.AP(bo_in.tensor, 0, [[C, 1], [0, 128], [1, C]])
            b2_rep = bass.AP(b2_in.tensor, 0, [[C, 1], [0, 128], [1, C]])
            eng.dma_start(out=bb[:], in_=bo_rep).then_inc(sems["qbb"], 16)
            eng.dma_start(out=out_tb[:], in_=b2_rep).then_inc(sems["qbb"], 16)
            for u1 in range(4):
                s_, p_ = qwf1_r.tag(u1)
                srcap = bass.AP(w1_in.tensor, u1 * 128,
                                [[FF, 128], [128 * FF, 8], [1, 128]])
                eng.dma_start(out=w1sl[:, u1 * 1024:(u1 + 1) * 1024],
                              in_=srcap).then_inc(s_, p_)
            eng.wait_ge(sems["eV_nrm"], 32)
            for jc in range(4):
                eng.dma_start(out=cc_in_ap[jc * 128:(jc + 1) * 128, :],
                              in_=attn_all[:, jc * T:(jc + 1) * T]).then_inc(sems["qcc"], 16)
            eng.wait_ge(sems["qcc"], 64)
            eng.collective_compute(
                "AllGather", OP.bypass,
                replica_groups=[[0, 1], [2, 3], [4, 5], [6, 7]],
                ins=[cc_in_ap], outs=[cc_out_ap],
            ).then_inc(sems["ecc"], 1)
            for cc in range(NCH):
                s_, p_ = qwo_r.tag(cc)
                eng.dma_start(out=wo_bf[:, cc * 1024:(cc + 1) * 1024],
                              in_=wo_in[cc * 128:(cc + 1) * 128, :]).then_inc(s_, p_)
            def _w1load(eng, u1):
                fc = u1 % 32
                eng.wait_ge(sems["eT_h"], u1 - 3)
                s_, p_ = qwf1_r.tag(u1)
                srcap = bass.AP(w1_in.tensor, fc * 128,
                                [[FF, 128], [128 * FF, 8], [1, 128]])
                eng.dma_start(out=w1sl[:, (u1 % 4) * 1024:(u1 % 4 + 1) * 1024],
                              in_=srcap).then_inc(s_, p_)

            def _w2load(eng, u2):
                jc = u2 % 8
                if u2 >= 2:
                    eng.wait_ge(sems["eT_f"], u2 - 1)
                s_, p_ = qwf2_r.tag(u2)
                srcap = bass.AP(w2_in.tensor, jc * 128,
                                [[C, 128], [128 * C, 32], [1, 128]])
                eng.dma_start(out=w2sl[:, (u2 % 2) * 4096:(u2 % 2 + 1) * 4096],
                              in_=srcap).then_inc(s_, p_)

            for u1 in range(4, 32):
                _w1load(eng, u1)
            for u2 in range(0, 8):
                _w2load(eng, u2)
            for u1 in range(32, 64):
                _w1load(eng, u1)
            for u2 in range(8, 16):
                _w2load(eng, u2)
            qst_r.wait(eng, 15)
            qst_r.wait(eng, 14)

        def p2b_scalar(eng):
            for tb in range(16):
                eng.wait_ge(sems["eT_r"], tb + 1)
                if tb >= 2:
                    eng.wait_ge(sems["eV_rg"], tb - 1)
                eng.activation(e_s[:, (tb % 2) * 8:(tb % 2 + 1) * 8],
                               p_big[:, 2048 + (tb % 2) * 512: 2048 + (tb % 2) * 512 + 8],
                               AF.Exp).then_inc(sems["eS_re"], 1)
            for ph in range(2):
                for fc in range(32):
                    u1 = ph * 32 + fc
                    if ph == 0 and fc == 0:
                        eng.wait_ge(sems["qcc"], 64)
                    eng.wait_ge(sems["eT_h"], u1 + 1)
                    if ph == 1 and fc == 0:
                        eng.wait_ge(sems["eT_f"], 8)
                    eng.activation(h_half[:, fc * 1024:(fc + 1) * 1024],
                                   p_big[:, (fc % 2) * 1024:(fc % 2 + 1) * 1024],
                                   AF.Gelu_apprx_tanh,
                                   bias=b1r[:, fc:fc + 1]).then_inc(sems["eS_ge"], 1)

        def p2b_vector(eng):
            eng.wait_ge(sems["qbb"], 32)
            vch.inc(eng.tensor_tensor(bb[:], bb[:], out_tb[:], OP.add))
            vch.wait(eng)
            eng.nop().then_inc(sems["eV_bb"], 1)
            for tb in range(16):
                eng.wait_ge(sems["eS_re"], tb + 1)
                ec = e_s[:, (tb % 2) * 8:(tb % 2 + 1) * 8]
                mx1 = r_sml[:, 0:1]; meq = r_sml[:, 2:10]; mx2 = r_sml[:, 1:2]
                s12 = r_sml[:, 10:11]; r12 = r_sml[:, 11:12]
                ga = r_sml[:, 12:13]; gb = r_sml[:, 13:14]
                vch.inc(eng.reduce_max(mx1, ec, AX)); vch.wait(eng)
                vch.inc(eng.tensor_scalar(meq, ec, mx1, None, OP.is_equal)); vch.wait(eng)
                vch.inc(eng.tensor_scalar(meq, meq, -1.0, 1.0, OP.mult, OP.add)); vch.wait(eng)
                vch.inc(eng.tensor_tensor(meq, ec, meq, OP.mult)); vch.wait(eng)
                vch.inc(eng.reduce_max(mx2, meq, AX)); vch.wait(eng)
                vch.inc(eng.tensor_tensor(s12, mx1, mx2, OP.add)); vch.wait(eng)
                vch.inc(eng.reciprocal(r12, s12)); vch.wait(eng)
                vch.inc(eng.tensor_tensor(ga, mx1, r12, OP.mult)); vch.wait(eng)
                vch.inc(eng.tensor_tensor(gb, mx2, r12, OP.mult)); vch.wait(eng)
                vch.inc(eng.tensor_tensor(g_s[:, tb:tb + 1], ga, gb, OP.add))
                vch.wait(eng)
                eng.nop().then_inc(sems["eV_rg"], 1)
            eng.wait_ge(sems["eV_bb"], 1)
            for tb in range(16):
                eng.wait_ge(sems["eT_y"], tb + 1)
                eng.tensor_tensor(y_att[:, tb * 1024:(tb + 1) * 1024],
                                  p_big[:, (tb % 2) * 1024:(tb % 2 + 1) * 1024],
                                  bb[:], OP.add).then_inc(sems["eV_ye"], 1)
            for ph in range(2):

                def _w2evac(eng, jc2):
                    u2c = ph * 8 + jc2
                    eng.wait_ge(sems["eT_f"], u2c + 1)
                    eng.tensor_copy(y_fm_s[:, (jc2 % 2) * 1024:(jc2 % 2 + 1) * 1024],
                                    p_big[:, (jc2 % 2) * 1024:(jc2 % 2 + 1) * 1024]
                                    ).then_inc(sems["eV_fe"], 1)
                    eng.wait_ge(sems["eT_ft"], u2c + 1)
                    if jc2 == 0:
                        eng.wait_ge(sems["eV_rg"], 16)
                    ptp = p_tp3 if jc2 % 2 == 0 else p_tp4
                    for tbi in range(8):
                        i = eng.tensor_scalar_mul(
                            bass.AP(y_ffn_tm[:].tensor, tbi * 1024 + jc2 * 128,
                                    [[8 * 1024, 128], [1, 128]]),
                            ptp[:, tbi * 128:(tbi + 1) * 128],
                            g_s[:, ph * 8 + tbi: ph * 8 + tbi + 1])
                    i.then_inc(sems["eV_ft"], 1)
                    vch.note(sems["eV_ft"], u2c + 1)

                for jc in range(8):
                    _w2evac(eng, jc)
                for tbl in range(8):
                    u3 = ph * 8 + tbl
                    eng.wait_ge(sems["eV_ye"], u3 + 1)
                    qxo_r.wait(eng, u3)
                    if u3 >= 1:
                        qst_r.wait(eng, u3 - 1)
                    vch.wait(eng)
                    vch.inc(eng.tensor_tensor(
                        out_tb[:], y_ffn_tm[:, tbl * 1024:(tbl + 1) * 1024],
                        x_tb[:], OP.add))
                    vch.wait(eng)
                    eng.tensor_tensor(out_tb[:], out_tb[:],
                                      y_att[:, u3 * 1024:(u3 + 1) * 1024],
                                      OP.add).then_inc(sems["eV_out"], 1)

        def p2b_tensor(eng):
            eng.wait_ge(sems["eS_exp"], NSTEP)
            eng.wait_ge(sems["eV_av"], 32)
            eng.wait_ge(sems["eV_dn"], 32)
            eng.wait_ge(sems["qxn"], 128)
            for tb in range(16):
                if tb >= 2:
                    eng.wait_ge(sems["eS_re"], tb - 1)
                for cc in range(NCH):
                    i = eng.matmul(p_big[:, 2048 + (tb % 2) * 512: 2048 + (tb % 2) * 512 + 8],
                                   xn_own[:, cc * 2048 + tb * 128: cc * 2048 + (tb + 1) * 128],
                                   rw_bf[:, cc * 8:(cc + 1) * 8],
                                   start=(cc == 0), stop=(cc == NCH - 1),
                                   skip_group_check=True)
                i.then_inc(sems["eT_r"], 1)
            def _w1(eng, ph):
                for fc in range(32):
                    u1 = ph * 32 + fc
                    qwf1_r.wait(eng, u1)
                    if u1 >= 2:
                        eng.wait_ge(sems["eS_ge"], u1 - 1)
                    for cc in range(NCH):
                        for tt in range(2):
                            i = eng.matmul(
                                p_big[:, (fc % 2) * 1024 + tt * 512:(fc % 2) * 1024 + (tt + 1) * 512],
                                w1sl[:, (u1 % 4) * 1024 + cc * 128:(u1 % 4) * 1024 + (cc + 1) * 128],
                                xn_own[:, cc * 2048 + ph * 1024 + tt * 512: cc * 2048 + ph * 1024 + (tt + 1) * 512],
                                start=(cc == 0), stop=(cc == NCH - 1), skip_group_check=True)
                    i.then_inc(sems["eT_h"], 1)

            _w1(eng, 0)
            qwo_r.wait(eng, NCH - 1)
            qwo_r.wait(eng, NCH - 2)
            eng.wait_ge(sems["eS_ge"], 32)
            for tb in range(16):
                qat_r.wait(eng, tb)
                if tb >= 2:
                    eng.wait_ge(sems["eV_ye"], tb - 1)
                for gc in range(8):
                    for sl in range(2):
                        i = eng.matmul(
                            p_big[:, (tb % 2) * 1024 + sl * 512:(tb % 2) * 1024 + (sl + 1) * 512],
                            attn_tb[:, (tb % 2) * 1024 + gc * 128:(tb % 2) * 1024 + (gc + 1) * 128],
                            wo_bf[:, gc * 1024 + sl * 512: gc * 1024 + (sl + 1) * 512],
                            start=(gc == 0), stop=(gc == NCH - 1), skip_group_check=True)
                i.then_inc(sems["eT_y"], 1)
            eng.wait_ge(sems["eV_ye"], 16)
            for ph in range(2):
                if ph == 1:
                    eng.wait_ge(sems["eV_fe"], 8)
                    _w1(eng, 1)
                eng.wait_ge(sems["eS_ge"], (ph + 1) * 32)
                for jc in range(8):
                    u2 = ph * 8 + jc
                    qwf2_r.wait(eng, u2)
                    if u2 >= 2:
                        eng.wait_ge(sems["eV_fe"], u2 - 1)
                    for fc in range(32):
                        for tt in range(2):
                            i = eng.matmul(
                                p_big[:, (jc % 2) * 1024 + tt * 512:(jc % 2) * 1024 + (tt + 1) * 512],
                                w2sl[:, (u2 % 2) * 4096 + fc * 128:(u2 % 2) * 4096 + (fc + 1) * 128],
                                h_half[:, fc * 1024 + tt * 512: fc * 1024 + (tt + 1) * 512],
                                start=(fc == 0), stop=(fc == 31), skip_group_check=True)
                    i.then_inc(sems["eT_f"], 1)
                    eng.wait_ge(sems["eV_fe"], u2 + 1)
                    if u2 >= 2:
                        eng.wait_ge(sems["eV_ft"], u2 - 1)
                    ptp = p_tp3 if jc % 2 == 0 else p_tp4
                    for tbi in range(8):
                        i = eng.transpose(ptp[:, tbi * 128:(tbi + 1) * 128],
                                          y_fm_s[:, (jc % 2) * 1024 + tbi * 128:(jc % 2) * 1024 + (tbi + 1) * 128],
                                          ident[:])
                    i.then_inc(sems["eT_ft"], 1)


        with nc.Block() as blk:
            full = (debug_phase == 0 or debug_phase >= 3)

            @blk.sync
            def _(eng):
                p1_sync(eng)
                eng.nop().then_inc(sems["bar1"], 1)
                if full:
                    p2b_sync(eng)

            @blk.gpsimd
            def _(eng):
                p1_gpsimd(eng)
                eng.nop().then_inc(sems["bar1"], 1)
                if full:
                    p2a_gpsimd(eng)
                    p2b_gpsimd(eng)
                if debug_phase == 2:
                    p2a_gpsimd(eng)
                    eng.wait_ge(sems["ecc"], 1)
                    eng.dma_start(out=dbg["att"], in_=attn_all[:]).then_inc(sems["qst"], 16)
                    eng.dma_start(out=dbg["ccout"], in_=cc_out_ap).then_inc(sems["qst"], 16)
                    eng.wait_ge(sems["qst"], 32)
                if debug_phase == 1:
                    eng.wait_ge(sems["eV_fin"], 8 * NG)
                    eng.wait_ge(sems["eV_vv"], 4 * NG)
                    for jc in range(4):
                        eng.dma_start(out=dbg["k"][:, jc * T:(jc + 1) * T],
                                      in_=k_fm[:, jc * T:(jc + 1) * T]).then_inc(sems["qst"], 16)
                        eng.dma_start(out=dbg["q"][:, jc * T:(jc + 1) * T],
                                      in_=q_fm[:, jc * T:(jc + 1) * T]).then_inc(sems["qst"], 16)
                    for i8 in range(8):
                        eng.dma_start(out=dbg["v"][:, i8 * 2048:(i8 + 1) * 2048],
                                      in_=v_tm[:, i8 * 2048:(i8 + 1) * 2048]).then_inc(sems["qst"], 16)
                    eng.wait_ge(sems["qst"], 16 * 16)
                    qpc2_r.wait(eng, W1U + W2U - 1)
                    qpc2_r.wait(eng, W1U + W2U - 2)

            @blk.scalar
            def _(eng):
                p1_scalar(eng)
                eng.nop().then_inc(sems["bar1"], 1)
                if debug_phase >= 2 or full:
                    p2a_scalar(eng)
                if full:
                    p2b_scalar(eng)

            @blk.vector
            def _(eng):
                p1_vector(eng)
                eng.nop().then_inc(sems["bar1"], 1)
                if debug_phase >= 2 or full:
                    p2a_vector(eng)
                if full:
                    p2b_vector(eng)

            @blk.tensor
            def _(eng):
                p1_tensor(eng)
                eng.nop().then_inc(sems["bar1"], 1)
                if debug_phase >= 2 or full:
                    p2a_tensor(eng)
                if full:
                    p2b_tensor(eng)

        p3s.close()

    return nc, dbg



def _host_constants():
    half = D // 2
    inv_freq = (1.0 / 10000.0 ** (np.arange(half, dtype=np.float32) / half)).astype(np.float32)
    t = np.arange(T, dtype=np.float32)
    freqs = t[:, None] * inv_freq[None, :]
    cos = np.repeat(np.cos(freqs), 2, axis=-1)
    sin = np.repeat(np.sin(freqs), 2, axis=-1)
    cos_fm = np.concatenate([cos.T, cos.T], axis=0).astype(np.float32)
    sin_fm = np.concatenate([sin.T, sin.T], axis=0).astype(np.float32)
    ident = np.eye(128, dtype=np.float32)
    P = np.zeros((128, 128), dtype=np.float32)
    for i in range(64):
        P[2 * i, 2 * i + 1] = -1.0
        P[2 * i + 1, 2 * i] = 1.0
    permT = P.T.copy()
    masks = np.zeros((128, 4 * 1024), dtype=np.float32)
    ii = np.arange(128)[:, None]
    jj = np.arange(512)[None, :]
    for r in range(4):
        m = (ii + 128 * r <= jj).astype(np.float32)
        masks[:, r * 1024:r * 1024 + 512] = m
        masks[:, r * 1024 + 512:(r + 1) * 1024] = m
    return cos_fm, sin_fm, ident, permT, masks


def make_inputs(full):
    x = np.asarray(full["L0_attn_x"], dtype=np.float32)
    Wqkv = np.asarray(full["L0_attn_Wqkv"], dtype=np.float32)
    Wo = np.asarray(full["L0_attn_Wo"], dtype=np.float32)
    bo = np.asarray(full["L0_attn_bo"], dtype=np.float32)
    rw = np.asarray(full["L0_ffn_router_w"], dtype=np.float32)
    w1 = np.asarray(full["L0_ffn_w1"], dtype=np.float32)
    b1 = np.asarray(full["L0_ffn_b1"], dtype=np.float32)
    w2 = np.asarray(full["L0_ffn_w2"], dtype=np.float32)
    b2 = np.asarray(full["L0_ffn_b2"], dtype=np.float32)
    cos_fm, sin_fm, ident, permT, masks = _host_constants()
    b1r = np.ascontiguousarray(b1.reshape(32, 128).T)
    in_maps = []
    for c in range(8):
        b = c // 2
        hh = c % 2
        cols = np.concatenate([np.arange(512 * hh, 512 * hh + 512),
                               1024 + np.arange(512 * hh, 512 * hh + 512),
                               2048 + np.arange(512 * hh, 512 * hh + 512)])
        in_maps.append({
            "x_in": np.ascontiguousarray(x[b]),
            "x_own": np.ascontiguousarray(x[b, hh * 2048:(hh + 1) * 2048]),
            "wqkv": np.ascontiguousarray(Wqkv[:, cols]),
            "wo_in": Wo,
            "w1_in": w1,
            "w2_in": w2,
            "rw_in": rw,
            "b1r_in": b1r,
            "bo_in": bo.reshape(1, C),
            "b2_in": b2.reshape(1, C),
            "cos_in": cos_fm,
            "sin_in": sin_fm,
            "ident_in": ident,
            "permT_in": permT,
            "masks_in": masks.astype(_bf16),
        })
    return in_maps


_NC_CACHE = {}


def kernel(**inputs):
    in_maps = make_inputs(inputs)
    if "nc" not in _NC_CACHE:
        _NC_CACHE["nc"] = build_program(debug_phase=0)[0]
    nc = _NC_CACHE["nc"]
    res = run_bass_kernel_spmd(nc, in_maps, core_ids=list(range(8)))
    out = np.empty((B, T, C), dtype=np.float32)
    for c in range(8):
        b, hh = c // 2, c % 2
        out[b, hh * 2048:(hh + 1) * 2048, :] = res.results[c]["y_out"]
    return out


# version 17
# speedup vs baseline: 1.0501x; 1.0186x over previous
import contextlib
import numpy as np
import ml_dtypes
_bf16 = ml_dtypes.bfloat16
import concourse.bass as bass
import concourse.mybir as mybir
from concourse.bass_utils import run_bass_kernel_spmd

F32 = mybir.dt.float32
BF16 = mybir.dt.bfloat16
AF = mybir.ActivationFunctionType
OP = mybir.AluOpType
AX = mybir.AxisListType.X

B, T, C, H, D, FF = 4, 4096, 1024, 16, 64, 4096
NG = 8
NB = 32
NCH = 8
SCALE = float(D) ** -0.5

DEBUG = False


def build_program(debug_phase=0):
    nc = bass.Bass("TRN2", target_bir_lowering=False)
    x_in = nc.dram_tensor("x_in", [T, C], F32, kind="ExternalInput").ap()
    x_own = nc.dram_tensor("x_own", [2048, C], F32, kind="ExternalInput").ap()
    wqkv = nc.dram_tensor("wqkv", [C, 1536], F32, kind="ExternalInput").ap()
    wo_in = nc.dram_tensor("wo_in", [C, C], F32, kind="ExternalInput").ap()
    w1_in = nc.dram_tensor("w1_in", [C, FF], F32, kind="ExternalInput").ap()
    w2_in = nc.dram_tensor("w2_in", [FF, C], F32, kind="ExternalInput").ap()
    rw_in = nc.dram_tensor("rw_in", [C, 8], F32, kind="ExternalInput").ap()
    b1r_in = nc.dram_tensor("b1r_in", [128, 32], F32, kind="ExternalInput").ap()
    bo_in = nc.dram_tensor("bo_in", [1, C], F32, kind="ExternalInput").ap()
    b2_in = nc.dram_tensor("b2_in", [1, C], F32, kind="ExternalInput").ap()
    cos_in = nc.dram_tensor("cos_in", [128, T], F32, kind="ExternalInput").ap()
    sin_in = nc.dram_tensor("sin_in", [128, T], F32, kind="ExternalInput").ap()
    ident_in = nc.dram_tensor("ident_in", [128, 128], F32, kind="ExternalInput").ap()
    permT_in = nc.dram_tensor("permT_in", [128, 128], F32, kind="ExternalInput").ap()
    masks_in = nc.dram_tensor("masks_in", [128, 4 * 1024], BF16, kind="ExternalInput").ap()
    y_out = nc.dram_tensor("y_out", [2048, C], F32, kind="ExternalOutput").ap()

    xn_spill = nc.dram_tensor("xn_spill", [1024, 4096], BF16, kind="Internal").ap()
    cc_in = nc.dram_tensor("cc_in", [512, 4096], BF16, kind="Internal")
    cc_out = nc.dram_tensor("cc_out", [1024, 4096], BF16, kind="Internal")
    cc_in_ap = cc_in.ap()
    cc_out_ap = cc_out.ap()

    dbg = {}
    if debug_phase >= 1:
        dbg["k"] = nc.dram_tensor("dbg_k", [128, 4 * T], BF16, kind="ExternalOutput").ap()
        dbg["q"] = nc.dram_tensor("dbg_q", [128, 4 * T], BF16, kind="ExternalOutput").ap()
        dbg["v"] = nc.dram_tensor("dbg_v", [128, 32 * 512], BF16, kind="ExternalOutput").ap()
    if debug_phase >= 2:
        dbg["att"] = nc.dram_tensor("dbg_att", [128, 4 * T], BF16, kind="ExternalOutput").ap()
        dbg["ccout"] = nc.dram_tensor("dbg_ccout", [1024, 4096], BF16, kind="ExternalOutput").ap()

    est = contextlib.ExitStack()
    with est:
        ident = est.enter_context(nc.sbuf_tensor("ident", [128, 128], BF16))
        permT = est.enter_context(nc.sbuf_tensor("permT", [128, 128], BF16))
        ones128 = est.enter_context(nc.sbuf_tensor("ones128", [128, 1], BF16))
        b1r = est.enter_context(nc.sbuf_tensor("b1r", [128, 32], F32))
        rw_fm = est.enter_context(nc.sbuf_tensor("rw_fm", [128, 8 * 8], F32))
        bb = est.enter_context(nc.sbuf_tensor("bb", [128, C], F32))
        g_s = est.enter_context(nc.sbuf_tensor("g_s", [128, 16], F32))
        eps_s = est.enter_context(nc.sbuf_tensor("eps_s", [128, 1], F32))
        rw_bf = est.enter_context(nc.sbuf_tensor("rw_bf", [128, 8 * 8], BF16))

        sems = {}
        def sem(name):
            sems[name] = est.enter_context(nc.semaphore(name))
            return sems[name]
        class DmaRing:
            def __init__(self, name, per=16, ways=2):
                self.ways = ways
                self.s = [est.enter_context(nc.semaphore(name + str(k)))
                          for k in range(ways)]
                self.per = per
            def tag(self, i):
                return self.s[i % self.ways], self.per
            def tag6(self, i):
                return self.s[i % self.ways], self.per
            def wait(self, eng, i):
                eng.wait_ge(self.s[i % self.ways], self.per * (i // self.ways + 1))

        class Chain:
            def __init__(self, name):
                self.h = est.enter_context(nc.semaphore(name))
                self.n = 0
                self.last = None
            def inc(self, instr):
                instr.then_inc(self.h, 1)
                self.n += 1
                self.last = (self.h, self.n)
                return instr
            def note(self, semh, val):
                self.last = (semh, val)
            def wait(self, eng):
                if self.last is not None:
                    eng.wait_ge(self.last[0], self.last[1])

        vch = Chain("vch")
        sch = Chain("sch")
        tch = Chain("tch")
        gch = Chain("gch")
        qx_r = DmaRing("qx", 16, ways=4)
        qat_r = DmaRing("qat_r", 16)
        qwo_r = DmaRing("qwo_r", 16)
        qwf1_r = DmaRing("qwf1_r", 16, ways=4)
        qwf2_r = DmaRing("qwf2_r", 16)
        qxo_r = DmaRing("qxo_r", 16)
        qst_r = DmaRing("qst_r", 16)
        qdn_r = DmaRing("qdn_r", 16)
        qbc_r = DmaRing("qbc_r", 32)
        qmk_r = DmaRing("qmk_r", 16)
        qxs_r = DmaRing("qxs_r", 128)
        qw_r = DmaRing("qw", 16)
        qtab_r = DmaRing("qtab", 32)
        qpc1_r = DmaRing("qpc1", 16)
        qpc2_r = DmaRing("qpc2", 16)
        for n in ["qc",
                  "eS_sq", "eV_cf1", "eS_rstd", "eV_cf2", "eS_ln",
                  "eT_tp", "eV_xg", "eT_mm", "eV_raw", "eT_rot", "eV_e1", "eV_fin",
                  "eT_v", "eV_vv", "eV_wc", "eV_pc", "bar1",
                  "eT_sc", "eS_exp", "eV_msk", "eT_pv", "eV_av", "eV_dn", "eV_rc", "eV_nrm",
                  "qcc", "ecc", "bar2aT", "qxn", "qbb", "eV_bb", "eV_mk",
                  "eS_sq2", "eV_cf12", "eS_rstd2", "eV_cf22", "eS_ln2",
                  "eT_tp2", "eV_xg2", "eV_woc", "eT_y", "eV_ye",
                  "eT_r", "eS_re", "eV_rg",
                  "eT_h", "eS_ge", "eT_f", "eV_w1c", "eV_w2c",
                  "eV_fe", "eT_ft", "eV_ft", "eV_out", "qst"]:
            sem(n)

        pav = contextlib.ExitStack()
        k_fm = pav.enter_context(nc.sbuf_tensor("k_fm", [128, 4 * T], BF16))
        q_fm = pav.enter_context(nc.sbuf_tensor("q_fm", [128, 4 * T], BF16))
        v_tm = pav.enter_context(nc.sbuf_tensor("v_tm", [128, 32 * 512], BF16))
        p1s = contextlib.ExitStack()
        p_tp = p1s.enter_context(nc.psum_tensor("p_tp", [128, 1024], BF16))
        p_tp2 = p1s.enter_context(nc.psum_tensor("p_tp2", [128, 1024], BF16))
        p_mm = p1s.enter_context(nc.psum_tensor("p_mm", [128, 2 * 512], F32))
        p_rot = p1s.enter_context(nc.psum_tensor("p_rot", [128, 2 * 512], F32))
        p_v = p1s.enter_context(nc.psum_tensor("p_v", [128, 2 * 512], F32))
        x_blk = p1s.enter_context(nc.sbuf_tensor("x_blk", [128, 4 * C], F32))
        xn_tm = p1s.enter_context(nc.sbuf_tensor("xn_tm", [128, 4 * C], BF16))
        xn_grp = p1s.enter_context(nc.sbuf_tensor("xn_grp", [128, 2 * 8 * 512], BF16))
        wkv_bf = p1s.enter_context(nc.sbuf_tensor("wkv_bf", [128, 8 * 1536], BF16))
        tab_c = p1s.enter_context(nc.sbuf_tensor("tab_c", [128, 2 * 512], F32))
        tab_s = p1s.enter_context(nc.sbuf_tensor("tab_s", [128, 2 * 512], F32))
        kraw = p1s.enter_context(nc.sbuf_tensor("kraw", [128, 2 * 512], BF16))
        tmp1 = p1s.enter_context(nc.sbuf_tensor("tmp1", [128, 2 * 512], BF16))
        tmp2 = p1s.enter_context(nc.sbuf_tensor("tmp2", [128, 2 * 512], BF16))
        sum_s = p1s.enter_context(nc.sbuf_tensor("sum_s", [128, 4], F32))
        sq_s = p1s.enter_context(nc.sbuf_tensor("sq_s", [128, 4], F32))
        m_s = p1s.enter_context(nc.sbuf_tensor("m_s", [128, 4], F32))
        var_s = p1s.enter_context(nc.sbuf_tensor("var_s", [128, 4], F32))
        rstd_s = p1s.enter_context(nc.sbuf_tensor("rstd_s", [128, 4], F32))
        bneg_s = p1s.enter_context(nc.sbuf_tensor("bneg_s", [128, 4], F32))
        scr = p1s.enter_context(nc.sbuf_tensor("scr", [128, C], BF16))

        W1U = 16
        W2U = 32


        def p1_sync(eng):
            for Bi in range(NB):
                if Bi >= 4:
                    eng.wait_ge(sems["eS_ln"], Bi - 3)
                s_, p_ = qx_r.tag(Bi)
                eng.dma_start(out=x_blk[:, (Bi % 4) * C:(Bi % 4 + 1) * C],
                              in_=x_in[Bi * 128:(Bi + 1) * 128, :]).then_inc(s_, p_)

        def p1_gpsimd(eng):
            eng.dma_start(out=ident[:], in_=ident_in).then_inc(sems["qc"], 16)
            eng.dma_start(out=permT[:], in_=permT_in).then_inc(sems["qc"], 16)
            eng.dma_start(out=b1r[:], in_=b1r_in).then_inc(sems["qc"], 16)
            for cc in range(NCH):
                eng.dma_start(out=rw_fm[:, cc * 8:(cc + 1) * 8],
                              in_=rw_in[cc * 128:(cc + 1) * 128, :]).then_inc(sems["qc"], 16)
            for cc in range(NCH):
                s_, p_ = qw_r.tag(cc)
                eng.dma_start(out=wkv_bf[:, cc * 1536:(cc + 1) * 1536],
                              in_=wqkv[cc * 128:(cc + 1) * 128, :]).then_inc(s_, p_)
            for g in range(NG):
                if g >= 2:
                    eng.wait_ge(sems["eV_fin"], 8 * (g - 1))
                s_, _ = qtab_r.tag(g)
                eng.dma_start(out=tab_c[:, (g % 2) * 512:(g % 2 + 1) * 512],
                              in_=cos_in[:, g * 512:(g + 1) * 512]).then_inc(s_, 16)
                eng.dma_start(out=tab_s[:, (g % 2) * 512:(g % 2 + 1) * 512],
                              in_=sin_in[:, g * 512:(g + 1) * 512]).then_inc(s_, 16)
                if g >= 1:
                    eng.wait_ge(sems["eV_xg"], 4 * g)
                    sx_, px_ = qxs_r.tag(g - 1)
                    gb = ((g - 1) % 2) * 4096
                    for cc in range(NCH):
                        eng.dma_start(out=xn_spill[cc * 128:(cc + 1) * 128,
                                                   (g - 1) * 512: g * 512],
                                      in_=xn_grp[:, gb + cc * 512: gb + (cc + 1) * 512]).then_inc(sx_, 16)
            eng.wait_ge(sems["eV_xg"], 4 * NG)
            sx_, px_ = qxs_r.tag(NG - 1)
            gb = ((NG - 1) % 2) * 4096
            for cc in range(NCH):
                eng.dma_start(out=xn_spill[cc * 128:(cc + 1) * 128,
                                           (NG - 1) * 512: NG * 512],
                              in_=xn_grp[:, gb + cc * 512: gb + (cc + 1) * 512]).then_inc(sx_, 16)
            qxs_r.wait(eng, NG - 1)
            qxs_r.wait(eng, NG - 2)
        def p1_scalar(eng):
            eng.wait_ge(sems["eV_cf2"], 1)
            for Bi in range(NB):
                pb = Bi % 4
                qx_r.wait(eng, Bi)
                if Bi >= 4:
                    eng.wait_ge(sems["eT_tp"], Bi - 3)
                eng.activation(scr[:], x_blk[:, pb * C:(pb + 1) * C], AF.Square,
                               accum_out=sq_s[:, pb:pb + 1])
                sch.inc(eng.activation(xn_tm[:, pb * C:(pb + 1) * C],
                                       x_blk[:, pb * C:(pb + 1) * C], AF.Copy,
                                       accum_out=sum_s[:, pb:pb + 1]))
                sch.wait(eng)
                sch.inc(eng.mul(m_s[:, pb:pb + 1], sum_s[:, pb:pb + 1], 1.0 / C))
                sch.wait(eng)
                sch.inc(eng.activation(var_s[:, pb:pb + 1], m_s[:, pb:pb + 1], AF.Square))
                sch.wait(eng)
                sch.inc(eng.mul(var_s[:, pb:pb + 1], var_s[:, pb:pb + 1], -1.0))
                sch.wait(eng)
                sch.inc(eng.activation(var_s[:, pb:pb + 1], sq_s[:, pb:pb + 1], AF.Identity,
                                       scale=1.0 / C, bias=var_s[:, pb:pb + 1]))
                sch.wait(eng)
                sch.inc(eng.activation(var_s[:, pb:pb + 1], var_s[:, pb:pb + 1], AF.Ln,
                                       bias=eps_s[:, 0:1]))
                sch.wait(eng)
                sch.inc(eng.activation(rstd_s[:, pb:pb + 1], var_s[:, pb:pb + 1], AF.Exp,
                                       scale=-0.5))
                sch.wait(eng)
                sch.inc(eng.activation(bneg_s[:, pb:pb + 1], m_s[:, pb:pb + 1], AF.Identity,
                                       scale=rstd_s[:, pb:pb + 1]))
                sch.wait(eng)
                sch.inc(eng.mul(bneg_s[:, pb:pb + 1], bneg_s[:, pb:pb + 1], -1.0))
                sch.wait(eng)
                eng.activation(xn_tm[:, pb * C:(pb + 1) * C], x_blk[:, pb * C:(pb + 1) * C],
                               AF.Identity, bias=bneg_s[:, pb:pb + 1],
                               scale=rstd_s[:, pb:pb + 1]).then_inc(sems["eS_ln"], 1)

        def p1_vector(eng):
            eng.memset(eps_s[:], 1e-5)
            eng.memset(ones128[:], 1.0).then_inc(sems["eV_cf2"], 1)
            eng.wait_ge(sems["qc"], 176)
            eng.tensor_copy(rw_bf[:], rw_fm[:])

            def _evacs(g):
                for b in range(4):
                    Bi = 4 * g + b
                    eng.wait_ge(sems["eT_tp"], Bi + 1)
                    if b == 0 and g >= 2:
                        qxs_r.wait(eng, g - 2)
                    ptp = p_tp if Bi % 2 == 0 else p_tp2
                    srcv = ptp[:].rearrange("p (cc t) -> p cc t", cc=8)
                    dstv = bass.AP(xn_grp[:].tensor, (g % 2) * 4096 + b * 128,
                                   [[8192, 128], [512, 8], [1, 128]])
                    eng.tensor_copy(dstv, srcv).then_inc(sems["eV_xg"], 1)

            def _rope(g):
                for r in range(8):
                    rg = 8 * g + r
                    pr = rg % 2
                    is_k = r < 4
                    jc = r % 4
                    eng.wait_ge(sems["eT_mm"], rg + 1)
                    eng.tensor_copy(kraw[:, pr * 512:(pr + 1) * 512],
                                    p_mm[:, pr * 512:(pr + 1) * 512]).then_inc(sems["eV_raw"], 1)
                    qtab_r.wait(eng, g)
                    eng.tensor_tensor(tmp1[:, pr * 512:(pr + 1) * 512],
                                      p_mm[:, pr * 512:(pr + 1) * 512],
                                      tab_c[:, (g % 2) * 512:(g % 2 + 1) * 512],
                                      OP.mult).then_inc(sems["eV_e1"], 1)
                    vch.note(sems["eV_e1"], rg + 1)
                    eng.wait_ge(sems["eT_rot"], rg + 1)
                    vch.inc(eng.tensor_tensor(tmp2[:, pr * 512:(pr + 1) * 512],
                                              p_rot[:, pr * 512:(pr + 1) * 512],
                                              tab_s[:, (g % 2) * 512:(g % 2 + 1) * 512], OP.mult))
                    vch.wait(eng)
                    dst = (k_fm if is_k else q_fm)[:, jc * T + g * 512: jc * T + (g + 1) * 512]
                    eng.tensor_tensor(dst, tmp1[:, pr * 512:(pr + 1) * 512],
                                      tmp2[:, pr * 512:(pr + 1) * 512], OP.add
                                      ).then_inc(sems["eV_fin"], 1)
                for tb in range(4):
                    vb = 4 * g + tb
                    eng.wait_ge(sems["eT_v"], vb + 1)
                    eng.tensor_copy(v_tm[:, vb * 512:(vb + 1) * 512],
                                    p_v[:, (vb % 2) * 512:(vb % 2 + 1) * 512]
                                    ).then_inc(sems["eV_vv"], 1)

            for g in range(NG):
                _evacs(g)
                if g >= 1:
                    _rope(g - 1)
            _rope(NG - 1)

        def p1_tensor(eng):
            eng.wait_ge(sems["qc"], 176)

            def _tp(g):
                for b in range(4):
                    Bi = 4 * g + b
                    pb4 = Bi % 4
                    ptp = p_tp if Bi % 2 == 0 else p_tp2
                    eng.wait_ge(sems["eS_ln"], Bi + 1)
                    if Bi >= 2:
                        eng.wait_ge(sems["eV_xg"], Bi - 1)
                    for j in range(8):
                        i = eng.transpose(ptp[:, j * 128:(j + 1) * 128],
                                          xn_tm[:, pb4 * C + j * 128: pb4 * C + (j + 1) * 128],
                                          ident[:])
                    i.then_inc(sems["eT_tp"], 1)

            def _kvq(g):
                base = (g % 2) * 4096
                for r in range(8):
                    rg = 8 * g + r
                    pr = rg % 2
                    is_k = r < 4
                    jc = r % 4
                    off = (512 + jc * 128) if is_k else (jc * 128)
                    if rg >= 2:
                        eng.wait_ge(sems["eV_e1"], rg - 1)
                    for cc in range(NCH):
                        i = eng.matmul(p_mm[:, pr * 512:(pr + 1) * 512],
                                       wkv_bf[:, cc * 1536 + off: cc * 1536 + off + 128],
                                       xn_grp[:, base + cc * 512: base + (cc + 1) * 512],
                                       start=(cc == 0), stop=(cc == NCH - 1))
                    i.then_inc(sems["eT_mm"], 1)
                    eng.wait_ge(sems["eV_raw"], rg + 1)
                    if rg >= 2:
                        eng.wait_ge(sems["eV_fin"], rg - 1)
                    eng.matmul(p_rot[:, pr * 512:(pr + 1) * 512], permT[:],
                               kraw[:, pr * 512:(pr + 1) * 512],
                               start=True, stop=True).then_inc(sems["eT_rot"], 1)
                for tb in range(4):
                    vb = 4 * g + tb
                    pv_ = vb % 2
                    if vb >= 2:
                        eng.wait_ge(sems["eV_vv"], vb - 1)
                    for cc in range(NCH):
                        i = eng.matmul(p_v[:, pv_ * 512:(pv_ + 1) * 512],
                                       xn_grp[:, base + cc * 512 + tb * 128: base + cc * 512 + (tb + 1) * 128],
                                       wkv_bf[:, cc * 1536 + 1024: cc * 1536 + 1536],
                                       start=(cc == 0), stop=(cc == NCH - 1))
                    i.then_inc(sems["eT_v"], 1)

            for g in range(NG):
                _tp(g)
                if g == 0:
                    qw_r.wait(eng, NCH - 1)
                    qw_r.wait(eng, NCH - 2)
                if g >= 1:
                    eng.wait_ge(sems["eV_xg"], 4 * g)
                    _kvq(g - 1)
            eng.wait_ge(sems["eV_xg"], 4 * NG)
            _kvq(NG - 1)

        p1s.close()
        p2s = contextlib.ExitStack()
        psc = p2s.enter_context(nc.psum_tensor("psc", [128, 3072], F32))
        ppv = p2s.enter_context(nc.psum_tensor("ppv", [128, 512], F32))
        pdn = p2s.enter_context(nc.psum_tensor("pdn", [128, 512], F32))
        pt_buf = p2s.enter_context(nc.sbuf_tensor("pt_buf", [128, 6 * 1024], BF16))
        attn_all = p2s.enter_context(nc.sbuf_tensor("attn_all", [128, 4 * T], BF16))
        dnsp = p2s.enter_context(nc.sbuf_tensor("dnsp", [128, 1024], F32))
        dn_buf = p2s.enter_context(nc.sbuf_tensor("dn_buf", [8, 8 * 512], F32))
        dn_rec = p2s.enter_context(nc.sbuf_tensor("dn_rec", [8, 8 * 512], F32))
        masks4 = p2s.enter_context(nc.sbuf_tensor("masks4", [128, 4 * 1024], BF16))
        bcast_t = p2s.enter_context(nc.sbuf_tensor("bcast_t", [128, 2 * 512], F32))

        STEPS = []
        for qt_ in range(8):
            for pr_ in range(4):
                for kb_ in range(4 * (qt_ + 1)):
                    STEPS.append((qt_, pr_, kb_))
        NSTEP = len(STEPS)
        GEND = {}
        DCTR = {}
        _d = 0
        for si_, st_ in enumerate(STEPS):
            qt_, pr_, kb_ = st_
            GEND[(qt_, pr_)] = si_
            if kb_ >= 4 * (qt_ + 1) - 4:
                _d += 1
                DCTR[si_] = _d

        def _sp_ap(handle, row_off, col_off, rowlen):
            return bass.AP(handle[:].tensor, row_off * rowlen + col_off,
                           [[64 * rowlen, 2], [1, 512]])

        def _pvdn(eng, si):
            qt, pr, kb = STEPS[si]
            nch = 4 * (qt + 1)
            idx2 = 4 * qt + pr
            eng.wait_ge(sems["eS_exp"], si + 1)
            if si in DCTR:
                eng.wait_ge(sems["eV_msk"], DCTR[si])
            if kb == 0 and idx2 >= 1:
                eng.wait_ge(sems["eV_av"], idx2)
                eng.wait_ge(sems["eV_dn"], idx2)
            sp4 = si % 6
            pts = pt_buf[:, sp4 * 1024:(sp4 + 1) * 1024]
            st_, sp_f = (kb == 0), (kb == nch - 1)
            h0 = 2 * pr
            eng.matmul(ppv[0:64, :],
                       v_tm[:, kb * 512 + h0 * 64: kb * 512 + (h0 + 1) * 64],
                       pts[:, 0:512], start=st_, stop=sp_f,
                       tile_position=(0, 0), skip_group_check=True)
            eng.matmul(ppv[64:128, :],
                       v_tm[:, kb * 512 + (h0 + 1) * 64: kb * 512 + (h0 + 2) * 64],
                       pts[:, 512:1024], start=st_, stop=sp_f,
                       tile_position=(0, 64), skip_group_check=True)
            eng.matmul(pdn[0:1, :], ones128[:],
                       pts[:, 0:512], start=st_, stop=sp_f,
                       tile_position=(0, 0), skip_group_check=True)
            eng.matmul(pdn[64:65, :], ones128[:],
                       pts[:, 512:1024], start=st_, stop=sp_f,
                       tile_position=(0, 64), skip_group_check=True
                       ).then_inc(sems["eT_pv"], 1)

        def p2a_tensor(eng):
            eng.wait_ge(sems["bar1"], 5)
            for si in range(NSTEP):
                qt, pr, kb = STEPS[si]
                jc = pr
                sp = si % 3
                if si >= 3:
                    eng.wait_ge(sems["eS_exp"], si - 2)
                for rt in range(2):
                    i = eng.matmul(
                        psc[:, sp * 1024 + rt * 512: sp * 1024 + (rt + 1) * 512],
                        k_fm[rt * 64:(rt + 1) * 64, jc * T + kb * 128: jc * T + (kb + 1) * 128],
                        q_fm[rt * 64:(rt + 1) * 64, jc * T + qt * 512: jc * T + (qt + 1) * 512],
                        start=True, stop=True, tile_position=(rt * 64, 0),
                        skip_group_check=True)
                i.then_inc(sems["eT_sc"], 1)
                if si >= 2:
                    _pvdn(eng, si - 2)
            _pvdn(eng, NSTEP - 2)
            _pvdn(eng, NSTEP - 1)
            eng.nop().then_inc(sems["bar2aT"], 1)

        def p2a_scalar(eng):
            eng.wait_ge(sems["bar1"], 5)
            for si in range(NSTEP):
                eng.wait_ge(sems["eT_sc"], si + 1)
                if si >= 6 and si % 2 == 0:
                    eng.wait_ge(sems["eT_pv"], si - 4)
                eng.activation(pt_buf[:, (si % 6) * 1024:(si % 6 + 1) * 1024],
                               psc[:, (si % 3) * 1024:(si % 3 + 1) * 1024],
                               AF.Exp, scale=SCALE).then_inc(sems["eS_exp"], 1)

        def p2a_vector(eng):
            eng.wait_ge(sems["bar1"], 5)
            qmk_r.wait(eng, 0)
            for qt in range(8):
                for pr in range(4):
                    idx2 = 4 * qt + pr
                    nch = 4 * (qt + 1)
                    for si in range(GEND[(qt, pr)] - 3, GEND[(qt, pr)] + 1):
                        _q, _p, kb = STEPS[si]
                        r = kb - (nch - 4)
                        sp = si % 6
                        eng.wait_ge(sems["eS_exp"], si + 1)
                        eng.tensor_tensor(pt_buf[:, sp * 1024:(sp + 1) * 1024],
                                          pt_buf[:, sp * 1024:(sp + 1) * 1024],
                                          masks4[:, r * 1024:(r + 1) * 1024],
                                          OP.mult).then_inc(sems["eV_msk"], 1)
                    eng.wait_ge(sems["eT_pv"], GEND[(qt, pr)] + 1)
                    if idx2 >= 1:
                        qdn_r.wait(eng, idx2 - 1)
                    eng.tensor_copy(dnsp[0:1, 0:512], pdn[0:1, :])
                    eng.tensor_copy(dnsp[64:65, 0:512], pdn[64:65, :]
                                    ).then_inc(sems["eV_dn"], 1)
                    eng.tensor_copy(attn_all[:, pr * T + qt * 512: pr * T + (qt + 1) * 512],
                                    ppv[:]).then_inc(sems["eV_av"], 1)
                    vch.note(sems["eV_av"], idx2 + 1)
                qdn_r.wait(eng, 4 * qt + 3)
                qdn_r.wait(eng, 4 * qt + 2)
                vch.wait(eng)
                eng.reciprocal(dn_rec[:, qt * 512:(qt + 1) * 512],
                               dn_buf[:, qt * 512:(qt + 1) * 512]).then_inc(sems["eV_rc"], 1)
                for j in range(4 * qt, 4 * qt + 4):
                    pr2 = j % 4
                    qbc_r.wait(eng, j)
                    eng.tensor_tensor(attn_all[:, pr2 * T + qt * 512: pr2 * T + (qt + 1) * 512],
                                      attn_all[:, pr2 * T + qt * 512: pr2 * T + (qt + 1) * 512],
                                      bcast_t[:, (j % 2) * 512:(j % 2 + 1) * 512],
                                      OP.mult).then_inc(sems["eV_nrm"], 1)

        def p2a_gpsimd(eng):
            eng.wait_ge(sems["bar1"], 5)
            eng.dma_start(out=masks4[:], in_=masks_in).then_inc(qmk_r.s[0], 16)
            for qt in range(8):
                for pr in range(4):
                    idx2 = 4 * qt + pr
                    eng.wait_ge(sems["eV_dn"], idx2 + 1)
                    s_, p_ = qdn_r.tag(idx2)
                    eng.dma_start(out=dn_buf[2 * pr: 2 * pr + 2, qt * 512:(qt + 1) * 512],
                                  in_=_sp_ap(dnsp, 0, 0, 1024)).then_inc(s_, 16)
                eng.wait_ge(sems["eV_rc"], qt + 1)
                for j in range(4 * qt, 4 * qt + 4):
                    pr2 = j % 4
                    if j >= 2:
                        eng.wait_ge(sems["eV_nrm"], j - 1)
                    s_, p_ = qbc_r.tag(j)
                    for half in range(2):
                        row = 2 * pr2 + half
                        srcap = bass.AP(dn_rec[:].tensor, row * (8 * 512) + qt * 512,
                                        [[8 * 512, 1], [0, 64], [1, 512]])
                        eng.dma_start(out=bcast_t[half * 64:(half + 1) * 64,
                                                  (j % 2) * 512:(j % 2 + 1) * 512],
                                      in_=srcap).then_inc(s_, 16)

        p2s.close()
        pav.close()
        p3s = contextlib.ExitStack()
        p_tp3 = p3s.enter_context(nc.psum_tensor("p_tp3", [128, 1024], BF16))
        p_tp4 = p3s.enter_context(nc.psum_tensor("p_tp4", [128, 1024], BF16))
        p_big = p3s.enter_context(nc.psum_tensor("p_big", [128, 3072], F32))
        xn_own = p3s.enter_context(nc.sbuf_tensor("xn_own", [128, 8 * 2048], BF16))
        w1sl = p3s.enter_context(nc.sbuf_tensor("w1sl", [128, 4 * 1024], BF16))
        y_att = p3s.enter_context(nc.sbuf_tensor("y_att", [128, 16 * 1024], BF16))
        x_tb = p3s.enter_context(nc.sbuf_tensor("x_tb", [128, 1024], F32))
        out_tb = p3s.enter_context(nc.sbuf_tensor("out_tb", [128, 1024], F32))
        e_s = p3s.enter_context(nc.sbuf_tensor("e_s", [128, 2 * 8], F32))
        r_sml = p3s.enter_context(nc.sbuf_tensor("r_sml", [128, 16], F32))
        y_ffn_tm = p3s.enter_context(nc.sbuf_tensor("y_ffn_tm", [128, 8 * 1024], BF16))
        attn_tb = p3s.enter_context(nc.sbuf_tensor("attn_tb", [128, 2 * 1024], BF16))
        wo_bf = p3s.enter_context(nc.sbuf_tensor("wo_bf", [128, 8 * 1024], BF16))
        p3b = contextlib.ExitStack()
        h_half = p3b.enter_context(nc.sbuf_tensor("h_half", [128, 32 * 1024], BF16))
        w2sl = p3b.enter_context(nc.sbuf_tensor("w2sl", [128, 2 * 4096], BF16))
        y_fm_s = p3b.enter_context(nc.sbuf_tensor("y_fm_s", [128, 2 * 1024], BF16))
        p3b.close()

        def p2b_sync(eng):
            eng.wait_ge(sems["bar2aT"], 1)
            pid = eng.partition_id()
            with eng.If(pid % 2 == 0):
                for cc in range(8):
                    eng.dma_start(out=xn_own[:, cc * 2048:(cc + 1) * 2048],
                                  in_=xn_spill[cc * 128:(cc + 1) * 128, 0:2048]
                                  ).then_inc(sems["qxn"], 16)
            with eng.Else():
                for cc in range(8):
                    eng.dma_start(out=xn_own[:, cc * 2048:(cc + 1) * 2048],
                                  in_=xn_spill[cc * 128:(cc + 1) * 128, 2048:4096]
                                  ).then_inc(sems["qxn"], 16)
            eng.wait_ge(sems["ecc"], 1)
            with eng.If(pid % 2 == 0):
                for tb in range(16):
                    if tb >= 2:
                        eng.wait_ge(sems["eT_y"], tb - 1)
                    s_, p_ = qat_r.tag(tb)
                    srcap = bass.AP(cc_out_ap.tensor, tb * 128,
                                    [[4096, 128], [128 * 4096, 8], [1, 128]])
                    eng.dma_start(out=attn_tb[:, (tb % 2) * 1024:(tb % 2 + 1) * 1024],
                                  in_=srcap).then_inc(s_, p_)
            with eng.Else():
                for tb in range(16):
                    if tb >= 2:
                        eng.wait_ge(sems["eT_y"], tb - 1)
                    s_, p_ = qat_r.tag(tb)
                    srcap = bass.AP(cc_out_ap.tensor, 2048 + tb * 128,
                                    [[4096, 128], [128 * 4096, 8], [1, 128]])
                    eng.dma_start(out=attn_tb[:, (tb % 2) * 1024:(tb % 2 + 1) * 1024],
                                  in_=srcap).then_inc(s_, p_)
            for u3 in range(16):
                if u3 >= 1:
                    eng.wait_ge(sems["eV_out"], u3)
                s_, p_ = qxo_r.tag(u3)
                eng.dma_start(out=x_tb[:],
                              in_=x_own[u3 * 128:(u3 + 1) * 128, :]).then_inc(s_, p_)
                if u3 >= 1:
                    s2_, p2_ = qst_r.tag(u3 - 1)
                    eng.dma_start(out=y_out[(u3 - 1) * 128:u3 * 128, :],
                                  in_=out_tb[:]).then_inc(s2_, p2_)
            eng.wait_ge(sems["eV_out"], 16)
            s2_, p2_ = qst_r.tag(15)
            eng.dma_start(out=y_out[15 * 128:16 * 128, :], in_=out_tb[:]).then_inc(s2_, p2_)

        def p2b_gpsimd(eng):
            eng.wait_ge(sems["bar2aT"], 1)
            bo_rep = bass.AP(bo_in.tensor, 0, [[C, 1], [0, 128], [1, C]])
            b2_rep = bass.AP(b2_in.tensor, 0, [[C, 1], [0, 128], [1, C]])
            eng.dma_start(out=bb[:], in_=bo_rep).then_inc(sems["qbb"], 16)
            eng.dma_start(out=out_tb[:], in_=b2_rep).then_inc(sems["qbb"], 16)
            for u1 in range(4):
                s_, p_ = qwf1_r.tag(u1)
                srcap = bass.AP(w1_in.tensor, u1 * 128,
                                [[FF, 128], [128 * FF, 8], [1, 128]])
                eng.dma_start(out=w1sl[:, u1 * 1024:(u1 + 1) * 1024],
                              in_=srcap).then_inc(s_, p_)
            eng.wait_ge(sems["eV_nrm"], 32)
            for jc in range(4):
                eng.dma_start(out=cc_in_ap[jc * 128:(jc + 1) * 128, :],
                              in_=attn_all[:, jc * T:(jc + 1) * T]).then_inc(sems["qcc"], 16)
            eng.wait_ge(sems["qcc"], 64)
            eng.collective_compute(
                "AllGather", OP.bypass,
                replica_groups=[[0, 1], [2, 3], [4, 5], [6, 7]],
                ins=[cc_in_ap], outs=[cc_out_ap],
            ).then_inc(sems["ecc"], 1)
            for cc in range(NCH):
                s_, p_ = qwo_r.tag(cc)
                eng.dma_start(out=wo_bf[:, cc * 1024:(cc + 1) * 1024],
                              in_=wo_in[cc * 128:(cc + 1) * 128, :]).then_inc(s_, p_)
            def _w1load(eng, u1):
                fc = u1 % 32
                eng.wait_ge(sems["eT_h"], u1 - 3)
                s_, p_ = qwf1_r.tag(u1)
                srcap = bass.AP(w1_in.tensor, fc * 128,
                                [[FF, 128], [128 * FF, 8], [1, 128]])
                eng.dma_start(out=w1sl[:, (u1 % 4) * 1024:(u1 % 4 + 1) * 1024],
                              in_=srcap).then_inc(s_, p_)

            def _w2load(eng, u2):
                jc = u2 % 8
                if u2 >= 2:
                    eng.wait_ge(sems["eT_f"], u2 - 1)
                s_, p_ = qwf2_r.tag(u2)
                srcap = bass.AP(w2_in.tensor, jc * 128,
                                [[C, 128], [128 * C, 32], [1, 128]])
                eng.dma_start(out=w2sl[:, (u2 % 2) * 4096:(u2 % 2 + 1) * 4096],
                              in_=srcap).then_inc(s_, p_)

            for u1 in range(4, 32):
                _w1load(eng, u1)
            for u2 in range(0, 8):
                _w2load(eng, u2)
            for u1 in range(32, 64):
                _w1load(eng, u1)
            for u2 in range(8, 16):
                _w2load(eng, u2)
            qst_r.wait(eng, 15)
            qst_r.wait(eng, 14)

        def p2b_scalar(eng):
            for tb in range(16):
                eng.wait_ge(sems["eT_r"], tb + 1)
                if tb >= 2:
                    eng.wait_ge(sems["eV_rg"], tb - 1)
                eng.activation(e_s[:, (tb % 2) * 8:(tb % 2 + 1) * 8],
                               p_big[:, 2048 + (tb % 2) * 512: 2048 + (tb % 2) * 512 + 8],
                               AF.Exp).then_inc(sems["eS_re"], 1)
            for ph in range(2):
                for fc in range(32):
                    u1 = ph * 32 + fc
                    if ph == 0 and fc == 0:
                        eng.wait_ge(sems["qcc"], 64)
                    eng.wait_ge(sems["eT_h"], u1 + 1)
                    if ph == 1 and fc == 0:
                        eng.wait_ge(sems["eT_f"], 8)
                    eng.activation(h_half[:, fc * 1024:(fc + 1) * 1024],
                                   p_big[:, (fc % 2) * 1024:(fc % 2 + 1) * 1024],
                                   AF.Gelu_apprx_tanh,
                                   bias=b1r[:, fc:fc + 1]).then_inc(sems["eS_ge"], 1)

        def p2b_vector(eng):
            eng.wait_ge(sems["qbb"], 32)
            vch.inc(eng.tensor_tensor(bb[:], bb[:], out_tb[:], OP.add))
            vch.wait(eng)
            eng.nop().then_inc(sems["eV_bb"], 1)
            for tb in range(16):
                eng.wait_ge(sems["eS_re"], tb + 1)
                ec = e_s[:, (tb % 2) * 8:(tb % 2 + 1) * 8]
                mx1 = r_sml[:, 0:1]; meq = r_sml[:, 2:10]; mx2 = r_sml[:, 1:2]
                s12 = r_sml[:, 10:11]; r12 = r_sml[:, 11:12]
                ga = r_sml[:, 12:13]; gb = r_sml[:, 13:14]
                vch.inc(eng.reduce_max(mx1, ec, AX)); vch.wait(eng)
                vch.inc(eng.tensor_scalar(meq, ec, mx1, None, OP.is_equal)); vch.wait(eng)
                vch.inc(eng.tensor_scalar(meq, meq, -1.0, 1.0, OP.mult, OP.add)); vch.wait(eng)
                vch.inc(eng.tensor_tensor(meq, ec, meq, OP.mult)); vch.wait(eng)
                vch.inc(eng.reduce_max(mx2, meq, AX)); vch.wait(eng)
                vch.inc(eng.tensor_tensor(s12, mx1, mx2, OP.add)); vch.wait(eng)
                vch.inc(eng.reciprocal(r12, s12)); vch.wait(eng)
                vch.inc(eng.tensor_tensor(ga, mx1, r12, OP.mult)); vch.wait(eng)
                vch.inc(eng.tensor_tensor(gb, mx2, r12, OP.mult)); vch.wait(eng)
                vch.inc(eng.tensor_tensor(g_s[:, tb:tb + 1], ga, gb, OP.add))
                vch.wait(eng)
                eng.nop().then_inc(sems["eV_rg"], 1)
            eng.wait_ge(sems["eV_bb"], 1)
            for tb in range(16):
                eng.wait_ge(sems["eT_y"], tb + 1)
                eng.tensor_tensor(y_att[:, tb * 1024:(tb + 1) * 1024],
                                  p_big[:, (tb % 2) * 1024:(tb % 2 + 1) * 1024],
                                  bb[:], OP.add).then_inc(sems["eV_ye"], 1)
            for ph in range(2):

                def _w2evac(eng, jc2):
                    u2c = ph * 8 + jc2
                    eng.wait_ge(sems["eT_f"], u2c + 1)
                    eng.tensor_copy(y_fm_s[:, (jc2 % 2) * 1024:(jc2 % 2 + 1) * 1024],
                                    p_big[:, (jc2 % 2) * 1024:(jc2 % 2 + 1) * 1024]
                                    ).then_inc(sems["eV_fe"], 1)
                    eng.wait_ge(sems["eT_ft"], u2c + 1)
                    if jc2 == 0:
                        eng.wait_ge(sems["eV_rg"], 16)
                    ptp = p_tp3 if jc2 % 2 == 0 else p_tp4
                    for tbi in range(8):
                        i = eng.tensor_scalar_mul(
                            bass.AP(y_ffn_tm[:].tensor, tbi * 1024 + jc2 * 128,
                                    [[8 * 1024, 128], [1, 128]]),
                            ptp[:, tbi * 128:(tbi + 1) * 128],
                            g_s[:, ph * 8 + tbi: ph * 8 + tbi + 1])
                    i.then_inc(sems["eV_ft"], 1)
                    vch.note(sems["eV_ft"], u2c + 1)

                for jc in range(8):
                    _w2evac(eng, jc)
                for tbl in range(8):
                    u3 = ph * 8 + tbl
                    eng.wait_ge(sems["eV_ye"], u3 + 1)
                    qxo_r.wait(eng, u3)
                    if u3 >= 1:
                        qst_r.wait(eng, u3 - 1)
                    vch.wait(eng)
                    vch.inc(eng.tensor_tensor(
                        out_tb[:], y_ffn_tm[:, tbl * 1024:(tbl + 1) * 1024],
                        x_tb[:], OP.add))
                    vch.wait(eng)
                    eng.tensor_tensor(out_tb[:], out_tb[:],
                                      y_att[:, u3 * 1024:(u3 + 1) * 1024],
                                      OP.add).then_inc(sems["eV_out"], 1)

        def p2b_tensor(eng):
            eng.wait_ge(sems["eS_exp"], NSTEP)
            eng.wait_ge(sems["eV_av"], 32)
            eng.wait_ge(sems["eV_dn"], 32)
            eng.wait_ge(sems["qxn"], 128)
            for tb in range(16):
                if tb >= 2:
                    eng.wait_ge(sems["eS_re"], tb - 1)
                for cc in range(NCH):
                    i = eng.matmul(p_big[:, 2048 + (tb % 2) * 512: 2048 + (tb % 2) * 512 + 8],
                                   xn_own[:, cc * 2048 + tb * 128: cc * 2048 + (tb + 1) * 128],
                                   rw_bf[:, cc * 8:(cc + 1) * 8],
                                   start=(cc == 0), stop=(cc == NCH - 1),
                                   skip_group_check=True)
                i.then_inc(sems["eT_r"], 1)
            def _w1(eng, ph):
                for fc in range(32):
                    u1 = ph * 32 + fc
                    qwf1_r.wait(eng, u1)
                    if u1 >= 2:
                        eng.wait_ge(sems["eS_ge"], u1 - 1)
                    for cc in range(NCH):
                        for tt in range(2):
                            i = eng.matmul(
                                p_big[:, (fc % 2) * 1024 + tt * 512:(fc % 2) * 1024 + (tt + 1) * 512],
                                w1sl[:, (u1 % 4) * 1024 + cc * 128:(u1 % 4) * 1024 + (cc + 1) * 128],
                                xn_own[:, cc * 2048 + ph * 1024 + tt * 512: cc * 2048 + ph * 1024 + (tt + 1) * 512],
                                start=(cc == 0), stop=(cc == NCH - 1), skip_group_check=True)
                    i.then_inc(sems["eT_h"], 1)

            _w1(eng, 0)
            qwo_r.wait(eng, NCH - 1)
            qwo_r.wait(eng, NCH - 2)
            eng.wait_ge(sems["eS_ge"], 32)
            for tb in range(16):
                qat_r.wait(eng, tb)
                if tb >= 2:
                    eng.wait_ge(sems["eV_ye"], tb - 1)
                for gc in range(8):
                    for sl in range(2):
                        i = eng.matmul(
                            p_big[:, (tb % 2) * 1024 + sl * 512:(tb % 2) * 1024 + (sl + 1) * 512],
                            attn_tb[:, (tb % 2) * 1024 + gc * 128:(tb % 2) * 1024 + (gc + 1) * 128],
                            wo_bf[:, gc * 1024 + sl * 512: gc * 1024 + (sl + 1) * 512],
                            start=(gc == 0), stop=(gc == NCH - 1), skip_group_check=True)
                i.then_inc(sems["eT_y"], 1)
            eng.wait_ge(sems["eV_ye"], 16)
            for ph in range(2):
                if ph == 1:
                    eng.wait_ge(sems["eV_fe"], 8)
                    _w1(eng, 1)
                eng.wait_ge(sems["eS_ge"], (ph + 1) * 32)
                for jc in range(8):
                    u2 = ph * 8 + jc
                    qwf2_r.wait(eng, u2)
                    if u2 >= 2:
                        eng.wait_ge(sems["eV_fe"], u2 - 1)
                    for fc in range(32):
                        for tt in range(2):
                            i = eng.matmul(
                                p_big[:, (jc % 2) * 1024 + tt * 512:(jc % 2) * 1024 + (tt + 1) * 512],
                                w2sl[:, (u2 % 2) * 4096 + fc * 128:(u2 % 2) * 4096 + (fc + 1) * 128],
                                h_half[:, fc * 1024 + tt * 512: fc * 1024 + (tt + 1) * 512],
                                start=(fc == 0), stop=(fc == 31), skip_group_check=True)
                    i.then_inc(sems["eT_f"], 1)
                    eng.wait_ge(sems["eV_fe"], u2 + 1)
                    if u2 >= 2:
                        eng.wait_ge(sems["eV_ft"], u2 - 1)
                    ptp = p_tp3 if jc % 2 == 0 else p_tp4
                    for tbi in range(8):
                        i = eng.transpose(ptp[:, tbi * 128:(tbi + 1) * 128],
                                          y_fm_s[:, (jc % 2) * 1024 + tbi * 128:(jc % 2) * 1024 + (tbi + 1) * 128],
                                          ident[:])
                    i.then_inc(sems["eT_ft"], 1)


        with nc.Block() as blk:
            full = (debug_phase == 0 or debug_phase >= 3)

            @blk.sync
            def _(eng):
                p1_sync(eng)
                eng.nop().then_inc(sems["bar1"], 1)
                if full:
                    p2b_sync(eng)

            @blk.gpsimd
            def _(eng):
                p1_gpsimd(eng)
                eng.nop().then_inc(sems["bar1"], 1)
                if full:
                    p2a_gpsimd(eng)
                    p2b_gpsimd(eng)
                if debug_phase == 2:
                    p2a_gpsimd(eng)
                    eng.wait_ge(sems["ecc"], 1)
                    eng.dma_start(out=dbg["att"], in_=attn_all[:]).then_inc(sems["qst"], 16)
                    eng.dma_start(out=dbg["ccout"], in_=cc_out_ap).then_inc(sems["qst"], 16)
                    eng.wait_ge(sems["qst"], 32)
                if debug_phase == 1:
                    eng.wait_ge(sems["eV_fin"], 8 * NG)
                    eng.wait_ge(sems["eV_vv"], 4 * NG)
                    for jc in range(4):
                        eng.dma_start(out=dbg["k"][:, jc * T:(jc + 1) * T],
                                      in_=k_fm[:, jc * T:(jc + 1) * T]).then_inc(sems["qst"], 16)
                        eng.dma_start(out=dbg["q"][:, jc * T:(jc + 1) * T],
                                      in_=q_fm[:, jc * T:(jc + 1) * T]).then_inc(sems["qst"], 16)
                    for i8 in range(8):
                        eng.dma_start(out=dbg["v"][:, i8 * 2048:(i8 + 1) * 2048],
                                      in_=v_tm[:, i8 * 2048:(i8 + 1) * 2048]).then_inc(sems["qst"], 16)
                    eng.wait_ge(sems["qst"], 16 * 16)
                    qpc2_r.wait(eng, W1U + W2U - 1)
                    qpc2_r.wait(eng, W1U + W2U - 2)

            @blk.scalar
            def _(eng):
                p1_scalar(eng)
                eng.nop().then_inc(sems["bar1"], 1)
                if debug_phase >= 2 or full:
                    p2a_scalar(eng)
                if full:
                    p2b_scalar(eng)

            @blk.vector
            def _(eng):
                p1_vector(eng)
                eng.nop().then_inc(sems["bar1"], 1)
                if debug_phase >= 2 or full:
                    p2a_vector(eng)
                if full:
                    p2b_vector(eng)

            @blk.tensor
            def _(eng):
                p1_tensor(eng)
                eng.nop().then_inc(sems["bar1"], 1)
                if debug_phase >= 2 or full:
                    p2a_tensor(eng)
                if full:
                    p2b_tensor(eng)

        p3s.close()

    return nc, dbg



def _host_constants():
    half = D // 2
    inv_freq = (1.0 / 10000.0 ** (np.arange(half, dtype=np.float32) / half)).astype(np.float32)
    t = np.arange(T, dtype=np.float32)
    freqs = t[:, None] * inv_freq[None, :]
    cos = np.repeat(np.cos(freqs), 2, axis=-1)
    sin = np.repeat(np.sin(freqs), 2, axis=-1)
    cos_fm = np.concatenate([cos.T, cos.T], axis=0).astype(np.float32)
    sin_fm = np.concatenate([sin.T, sin.T], axis=0).astype(np.float32)
    ident = np.eye(128, dtype=np.float32)
    P = np.zeros((128, 128), dtype=np.float32)
    for i in range(64):
        P[2 * i, 2 * i + 1] = -1.0
        P[2 * i + 1, 2 * i] = 1.0
    permT = P.T.copy()
    masks = np.zeros((128, 4 * 1024), dtype=np.float32)
    ii = np.arange(128)[:, None]
    jj = np.arange(512)[None, :]
    for r in range(4):
        m = (ii + 128 * r <= jj).astype(np.float32)
        masks[:, r * 1024:r * 1024 + 512] = m
        masks[:, r * 1024 + 512:(r + 1) * 1024] = m
    return cos_fm, sin_fm, ident, permT, masks


def make_inputs(full):
    x = np.asarray(full["L0_attn_x"], dtype=np.float32)
    Wqkv = np.asarray(full["L0_attn_Wqkv"], dtype=np.float32)
    Wo = np.asarray(full["L0_attn_Wo"], dtype=np.float32)
    bo = np.asarray(full["L0_attn_bo"], dtype=np.float32)
    rw = np.asarray(full["L0_ffn_router_w"], dtype=np.float32)
    w1 = np.asarray(full["L0_ffn_w1"], dtype=np.float32)
    b1 = np.asarray(full["L0_ffn_b1"], dtype=np.float32)
    w2 = np.asarray(full["L0_ffn_w2"], dtype=np.float32)
    b2 = np.asarray(full["L0_ffn_b2"], dtype=np.float32)
    cos_fm, sin_fm, ident, permT, masks = _host_constants()
    b1r = np.ascontiguousarray(b1.reshape(32, 128).T)
    in_maps = []
    for c in range(8):
        b = c // 2
        hh = c % 2
        cols = np.concatenate([np.arange(512 * hh, 512 * hh + 512),
                               1024 + np.arange(512 * hh, 512 * hh + 512),
                               2048 + np.arange(512 * hh, 512 * hh + 512)])
        in_maps.append({
            "x_in": np.ascontiguousarray(x[b]),
            "x_own": np.ascontiguousarray(x[b, hh * 2048:(hh + 1) * 2048]),
            "wqkv": np.ascontiguousarray(Wqkv[:, cols]),
            "wo_in": Wo,
            "w1_in": w1,
            "w2_in": w2,
            "rw_in": rw,
            "b1r_in": b1r,
            "bo_in": bo.reshape(1, C),
            "b2_in": b2.reshape(1, C),
            "cos_in": cos_fm,
            "sin_in": sin_fm,
            "ident_in": ident,
            "permT_in": permT,
            "masks_in": masks.astype(_bf16),
        })
    return in_maps


_NC_CACHE = {}


def kernel(**inputs):
    in_maps = make_inputs(inputs)
    if "nc" not in _NC_CACHE:
        _NC_CACHE["nc"] = build_program(debug_phase=0)[0]
    nc = _NC_CACHE["nc"]
    res = run_bass_kernel_spmd(nc, in_maps, core_ids=list(range(8)))
    out = np.empty((B, T, C), dtype=np.float32)
    for c in range(8):
        b, hh = c // 2, c % 2
        out[b, hh * 2048:(hh + 1) * 2048, :] = res.results[c]["y_out"]
    return out


# version 18
# speedup vs baseline: 1.1704x; 1.1146x over previous
import contextlib
import numpy as np
import ml_dtypes
_bf16 = ml_dtypes.bfloat16
import concourse.bass as bass
import concourse.mybir as mybir
from concourse.bass_utils import run_bass_kernel_spmd

F32 = mybir.dt.float32
BF16 = mybir.dt.bfloat16
AF = mybir.ActivationFunctionType
OP = mybir.AluOpType
AX = mybir.AxisListType.X

B, T, C, H, D, FF = 4, 4096, 1024, 16, 64, 4096
NG = 8
NB = 32
NCH = 8
SCALE = float(D) ** -0.5

DEBUG = False


def build_program(debug_phase=0):
    nc = bass.Bass("TRN2", target_bir_lowering=False)
    x_in = nc.dram_tensor("x_in", [T, C], F32, kind="ExternalInput").ap()
    x_own = nc.dram_tensor("x_own", [2048, C], F32, kind="ExternalInput").ap()
    wqkv = nc.dram_tensor("wqkv", [C, 1536], F32, kind="ExternalInput").ap()
    wo_in = nc.dram_tensor("wo_in", [C, C], F32, kind="ExternalInput").ap()
    w1_in = nc.dram_tensor("w1_in", [C, FF], F32, kind="ExternalInput").ap()
    w2_in = nc.dram_tensor("w2_in", [FF, C], F32, kind="ExternalInput").ap()
    rw_in = nc.dram_tensor("rw_in", [C, 8], F32, kind="ExternalInput").ap()
    b1r_in = nc.dram_tensor("b1r_in", [128, 32], F32, kind="ExternalInput").ap()
    bo_in = nc.dram_tensor("bo_in", [1, C], F32, kind="ExternalInput").ap()
    b2_in = nc.dram_tensor("b2_in", [1, C], F32, kind="ExternalInput").ap()
    cos_in = nc.dram_tensor("cos_in", [128, T], F32, kind="ExternalInput").ap()
    sin_in = nc.dram_tensor("sin_in", [128, T], F32, kind="ExternalInput").ap()
    ident_in = nc.dram_tensor("ident_in", [128, 128], F32, kind="ExternalInput").ap()
    permT_in = nc.dram_tensor("permT_in", [128, 128], F32, kind="ExternalInput").ap()
    masks_in = nc.dram_tensor("masks_in", [128, 4 * 1024], BF16, kind="ExternalInput").ap()
    y_out = nc.dram_tensor("y_out", [2048, C], F32, kind="ExternalOutput").ap()

    xn_spill = nc.dram_tensor("xn_spill", [1024, 4096], BF16, kind="Internal").ap()
    cc_in = nc.dram_tensor("cc_in", [512, 4096], BF16, kind="Internal")
    cc_out = nc.dram_tensor("cc_out", [1024, 4096], BF16, kind="Internal")
    cc_in_ap = cc_in.ap()
    cc_out_ap = cc_out.ap()

    dbg = {}
    if debug_phase >= 1:
        dbg["k"] = nc.dram_tensor("dbg_k", [128, 4 * T], BF16, kind="ExternalOutput").ap()
        dbg["q"] = nc.dram_tensor("dbg_q", [128, 4 * T], BF16, kind="ExternalOutput").ap()
        dbg["v"] = nc.dram_tensor("dbg_v", [128, 32 * 512], BF16, kind="ExternalOutput").ap()
    if debug_phase >= 2:
        dbg["att"] = nc.dram_tensor("dbg_att", [128, 4 * T], BF16, kind="ExternalOutput").ap()
        dbg["ccout"] = nc.dram_tensor("dbg_ccout", [1024, 4096], BF16, kind="ExternalOutput").ap()

    est = contextlib.ExitStack()
    with est:
        ident = est.enter_context(nc.sbuf_tensor("ident", [128, 128], BF16))
        permT = est.enter_context(nc.sbuf_tensor("permT", [128, 128], BF16))
        ones128 = est.enter_context(nc.sbuf_tensor("ones128", [128, 1], BF16))
        b1r = est.enter_context(nc.sbuf_tensor("b1r", [128, 32], F32))
        rw_fm = est.enter_context(nc.sbuf_tensor("rw_fm", [128, 8 * 8], F32))
        bb = est.enter_context(nc.sbuf_tensor("bb", [128, C], F32))
        g_s = est.enter_context(nc.sbuf_tensor("g_s", [128, 16], F32))
        eps_s = est.enter_context(nc.sbuf_tensor("eps_s", [128, 1], F32))
        rw_bf = est.enter_context(nc.sbuf_tensor("rw_bf", [128, 8 * 8], BF16))

        sems = {}
        def sem(name):
            sems[name] = est.enter_context(nc.semaphore(name))
            return sems[name]
        class DmaRing:
            def __init__(self, name, per=16, ways=2):
                self.ways = ways
                self.s = [est.enter_context(nc.semaphore(name + str(k)))
                          for k in range(ways)]
                self.per = per
            def tag(self, i):
                return self.s[i % self.ways], self.per
            def tag6(self, i):
                return self.s[i % self.ways], self.per
            def wait(self, eng, i):
                eng.wait_ge(self.s[i % self.ways], self.per * (i // self.ways + 1))

        class Chain:
            def __init__(self, name):
                self.h = est.enter_context(nc.semaphore(name))
                self.n = 0
                self.last = None
            def inc(self, instr):
                instr.then_inc(self.h, 1)
                self.n += 1
                self.last = (self.h, self.n)
                return instr
            def note(self, semh, val):
                self.last = (semh, val)
            def wait(self, eng):
                if self.last is not None:
                    eng.wait_ge(self.last[0], self.last[1])

        vch = Chain("vch")
        sch = Chain("sch")
        tch = Chain("tch")
        gch = Chain("gch")
        qx_r = DmaRing("qx", 16, ways=4)
        qat_r = DmaRing("qat_r", 16)
        qwo_r = DmaRing("qwo_r", 16)
        qwf1_r = DmaRing("qwf1_r", 16, ways=4)
        qwf2_r = DmaRing("qwf2_r", 16)
        qxo_r = DmaRing("qxo_r", 16)
        qst_r = DmaRing("qst_r", 16)
        qdn_r = DmaRing("qdn_r", 16)
        qbc_r = DmaRing("qbc_r", 32)
        qmk_r = DmaRing("qmk_r", 16)
        qxs_r = DmaRing("qxs_r", 128)
        qw_r = DmaRing("qw", 16)
        qtab_r = DmaRing("qtab", 32)
        qpc1_r = DmaRing("qpc1", 16)
        qpc2_r = DmaRing("qpc2", 16)
        for n in ["qc",
                  "eS_sq", "eV_cf1", "eS_rstd", "eV_cf2", "eS_ln",
                  "eT_tp", "eV_xg", "eT_mm", "eV_raw", "eT_rot", "eV_e1", "eV_fin",
                  "eT_v", "eV_vv", "eV_wc", "eV_pc", "bar1",
                  "eT_sc", "eS_exp", "eV_msk", "eT_pv", "eV_av", "eV_dn", "eV_rc", "eV_nrm",
                  "qcc", "ecc", "bar2aT", "qxn", "qbb", "eV_bb", "eV_mk",
                  "eS_sq2", "eV_cf12", "eS_rstd2", "eV_cf22", "eS_ln2",
                  "eT_tp2", "eV_xg2", "eV_woc", "eT_y", "eV_ye",
                  "eT_r", "eS_re", "eV_rg",
                  "eT_h", "eS_ge", "eT_f", "eV_w1c", "eV_w2c",
                  "eV_fe", "eT_ft", "eV_ft", "eV_out", "qst"]:
            sem(n)

        pav = contextlib.ExitStack()
        k_fm = pav.enter_context(nc.sbuf_tensor("k_fm", [128, 4 * T], BF16))
        q_fm = pav.enter_context(nc.sbuf_tensor("q_fm", [128, 4 * T], BF16))
        v_tm = pav.enter_context(nc.sbuf_tensor("v_tm", [128, 32 * 512], BF16))
        p1s = contextlib.ExitStack()
        p_tp = p1s.enter_context(nc.psum_tensor("p_tp", [128, 1024], BF16))
        p_tp2 = p1s.enter_context(nc.psum_tensor("p_tp2", [128, 1024], BF16))
        p_mm = p1s.enter_context(nc.psum_tensor("p_mm", [128, 2 * 512], F32))
        p_rot = p1s.enter_context(nc.psum_tensor("p_rot", [128, 2 * 512], F32))
        p_v = p1s.enter_context(nc.psum_tensor("p_v", [128, 2 * 512], F32))
        x_blk = p1s.enter_context(nc.sbuf_tensor("x_blk", [128, 4 * C], F32))
        xn_tm = p1s.enter_context(nc.sbuf_tensor("xn_tm", [128, 4 * C], BF16))
        xn_grp = p1s.enter_context(nc.sbuf_tensor("xn_grp", [128, 2 * 8 * 512], BF16))
        wkv_bf = p1s.enter_context(nc.sbuf_tensor("wkv_bf", [128, 8 * 1536], BF16))
        tab_c = p1s.enter_context(nc.sbuf_tensor("tab_c", [128, 2 * 512], F32))
        tab_s = p1s.enter_context(nc.sbuf_tensor("tab_s", [128, 2 * 512], F32))
        kraw = p1s.enter_context(nc.sbuf_tensor("kraw", [128, 2 * 512], BF16))
        tmp1 = p1s.enter_context(nc.sbuf_tensor("tmp1", [128, 2 * 512], BF16))
        tmp2 = p1s.enter_context(nc.sbuf_tensor("tmp2", [128, 2 * 512], BF16))
        sum_s = p1s.enter_context(nc.sbuf_tensor("sum_s", [128, 4], F32))
        sq_s = p1s.enter_context(nc.sbuf_tensor("sq_s", [128, 4], F32))
        m_s = p1s.enter_context(nc.sbuf_tensor("m_s", [128, 4], F32))
        var_s = p1s.enter_context(nc.sbuf_tensor("var_s", [128, 4], F32))
        rstd_s = p1s.enter_context(nc.sbuf_tensor("rstd_s", [128, 4], F32))
        bneg_s = p1s.enter_context(nc.sbuf_tensor("bneg_s", [128, 4], F32))
        scr = p1s.enter_context(nc.sbuf_tensor("scr", [128, C], BF16))

        W1U = 16
        W2U = 32


        def p1_sync(eng):
            for Bi in range(NB):
                if Bi >= 4:
                    eng.wait_ge(sems["eS_ln"], Bi - 3)
                s_, p_ = qx_r.tag(Bi)
                eng.dma_start(out=x_blk[:, (Bi % 4) * C:(Bi % 4 + 1) * C],
                              in_=x_in[Bi * 128:(Bi + 1) * 128, :]).then_inc(s_, p_)

        def p1_gpsimd(eng):
            eng.dma_start(out=ident[:], in_=ident_in).then_inc(sems["qc"], 16)
            eng.dma_start(out=permT[:], in_=permT_in).then_inc(sems["qc"], 16)
            eng.dma_start(out=b1r[:], in_=b1r_in).then_inc(sems["qc"], 16)
            for cc in range(NCH):
                eng.dma_start(out=rw_fm[:, cc * 8:(cc + 1) * 8],
                              in_=rw_in[cc * 128:(cc + 1) * 128, :]).then_inc(sems["qc"], 16)
            for cc in range(NCH):
                s_, p_ = qw_r.tag(cc)
                eng.dma_start(out=wkv_bf[:, cc * 1536:(cc + 1) * 1536],
                              in_=wqkv[cc * 128:(cc + 1) * 128, :]).then_inc(s_, p_)
            for g in range(NG):
                if g >= 2:
                    eng.wait_ge(sems["eV_fin"], 8 * (g - 1))
                s_, _ = qtab_r.tag(g)
                eng.dma_start(out=tab_c[:, (g % 2) * 512:(g % 2 + 1) * 512],
                              in_=cos_in[:, g * 512:(g + 1) * 512]).then_inc(s_, 16)
                eng.dma_start(out=tab_s[:, (g % 2) * 512:(g % 2 + 1) * 512],
                              in_=sin_in[:, g * 512:(g + 1) * 512]).then_inc(s_, 16)
                if g >= 1:
                    eng.wait_ge(sems["eV_xg"], 4 * g)
                    sx_, px_ = qxs_r.tag(g - 1)
                    gb = ((g - 1) % 2) * 4096
                    for cc in range(NCH):
                        eng.dma_start(out=xn_spill[cc * 128:(cc + 1) * 128,
                                                   (g - 1) * 512: g * 512],
                                      in_=xn_grp[:, gb + cc * 512: gb + (cc + 1) * 512]).then_inc(sx_, 16)
            eng.wait_ge(sems["eV_xg"], 4 * NG)
            sx_, px_ = qxs_r.tag(NG - 1)
            gb = ((NG - 1) % 2) * 4096
            for cc in range(NCH):
                eng.dma_start(out=xn_spill[cc * 128:(cc + 1) * 128,
                                           (NG - 1) * 512: NG * 512],
                              in_=xn_grp[:, gb + cc * 512: gb + (cc + 1) * 512]).then_inc(sx_, 16)
            qxs_r.wait(eng, NG - 1)
            qxs_r.wait(eng, NG - 2)
        def p1_scalar(eng):
            eng.wait_ge(sems["eV_cf2"], 1)
            for Bi in range(NB):
                pb = Bi % 4
                qx_r.wait(eng, Bi)
                if Bi >= 4:
                    eng.wait_ge(sems["eT_tp"], Bi - 3)
                eng.activation(scr[:], x_blk[:, pb * C:(pb + 1) * C], AF.Square,
                               accum_out=sq_s[:, pb:pb + 1])
                sch.inc(eng.activation(xn_tm[:, pb * C:(pb + 1) * C],
                                       x_blk[:, pb * C:(pb + 1) * C], AF.Copy,
                                       accum_out=sum_s[:, pb:pb + 1]))
                sch.wait(eng)
                sch.inc(eng.mul(m_s[:, pb:pb + 1], sum_s[:, pb:pb + 1], 1.0 / C))
                sch.wait(eng)
                sch.inc(eng.activation(var_s[:, pb:pb + 1], m_s[:, pb:pb + 1], AF.Square))
                sch.wait(eng)
                sch.inc(eng.mul(var_s[:, pb:pb + 1], var_s[:, pb:pb + 1], -1.0))
                sch.wait(eng)
                sch.inc(eng.activation(var_s[:, pb:pb + 1], sq_s[:, pb:pb + 1], AF.Identity,
                                       scale=1.0 / C, bias=var_s[:, pb:pb + 1]))
                sch.wait(eng)
                sch.inc(eng.activation(var_s[:, pb:pb + 1], var_s[:, pb:pb + 1], AF.Ln,
                                       bias=eps_s[:, 0:1]))
                sch.wait(eng)
                sch.inc(eng.activation(rstd_s[:, pb:pb + 1], var_s[:, pb:pb + 1], AF.Exp,
                                       scale=-0.5))
                sch.wait(eng)
                sch.inc(eng.activation(bneg_s[:, pb:pb + 1], m_s[:, pb:pb + 1], AF.Identity,
                                       scale=rstd_s[:, pb:pb + 1]))
                sch.wait(eng)
                sch.inc(eng.mul(bneg_s[:, pb:pb + 1], bneg_s[:, pb:pb + 1], -1.0))
                sch.wait(eng)
                eng.activation(xn_tm[:, pb * C:(pb + 1) * C], x_blk[:, pb * C:(pb + 1) * C],
                               AF.Identity, bias=bneg_s[:, pb:pb + 1],
                               scale=rstd_s[:, pb:pb + 1]).then_inc(sems["eS_ln"], 1)

        def p1_vector(eng):
            eng.memset(eps_s[:], 1e-5)
            eng.memset(ones128[:], 1.0).then_inc(sems["eV_cf2"], 1)
            eng.wait_ge(sems["qc"], 176)
            eng.tensor_copy(rw_bf[:], rw_fm[:])

            def _evacs(g):
                for b in range(4):
                    Bi = 4 * g + b
                    eng.wait_ge(sems["eT_tp"], Bi + 1)
                    if b == 0 and g >= 2:
                        qxs_r.wait(eng, g - 2)
                    ptp = p_tp if Bi % 2 == 0 else p_tp2
                    srcv = ptp[:].rearrange("p (cc t) -> p cc t", cc=8)
                    dstv = bass.AP(xn_grp[:].tensor, (g % 2) * 4096 + b * 128,
                                   [[8192, 128], [512, 8], [1, 128]])
                    eng.tensor_copy(dstv, srcv).then_inc(sems["eV_xg"], 1)

            def _rope(g):
                for r in range(8):
                    rg = 8 * g + r
                    pr = rg % 2
                    is_k = r < 4
                    jc = r % 4
                    eng.wait_ge(sems["eT_mm"], rg + 1)
                    eng.tensor_copy(kraw[:, pr * 512:(pr + 1) * 512],
                                    p_mm[:, pr * 512:(pr + 1) * 512]).then_inc(sems["eV_raw"], 1)
                    qtab_r.wait(eng, g)
                    eng.tensor_tensor(tmp1[:, pr * 512:(pr + 1) * 512],
                                      p_mm[:, pr * 512:(pr + 1) * 512],
                                      tab_c[:, (g % 2) * 512:(g % 2 + 1) * 512],
                                      OP.mult).then_inc(sems["eV_e1"], 1)
                    vch.note(sems["eV_e1"], rg + 1)
                    eng.wait_ge(sems["eT_rot"], rg + 1)
                    vch.inc(eng.tensor_tensor(tmp2[:, pr * 512:(pr + 1) * 512],
                                              p_rot[:, pr * 512:(pr + 1) * 512],
                                              tab_s[:, (g % 2) * 512:(g % 2 + 1) * 512], OP.mult))
                    vch.wait(eng)
                    dst = (k_fm if is_k else q_fm)[:, jc * T + g * 512: jc * T + (g + 1) * 512]
                    eng.tensor_tensor(dst, tmp1[:, pr * 512:(pr + 1) * 512],
                                      tmp2[:, pr * 512:(pr + 1) * 512], OP.add
                                      ).then_inc(sems["eV_fin"], 1)
                for tb in range(4):
                    vb = 4 * g + tb
                    eng.wait_ge(sems["eT_v"], vb + 1)
                    eng.tensor_copy(v_tm[:, vb * 512:(vb + 1) * 512],
                                    p_v[:, (vb % 2) * 512:(vb % 2 + 1) * 512]
                                    ).then_inc(sems["eV_vv"], 1)

            for g in range(NG):
                _evacs(g)
                if g >= 1:
                    _rope(g - 1)
            _rope(NG - 1)

        def p1_tensor(eng):
            eng.wait_ge(sems["qc"], 176)

            def _tp(g):
                for b in range(4):
                    Bi = 4 * g + b
                    pb4 = Bi % 4
                    ptp = p_tp if Bi % 2 == 0 else p_tp2
                    eng.wait_ge(sems["eS_ln"], Bi + 1)
                    if Bi >= 2:
                        eng.wait_ge(sems["eV_xg"], Bi - 1)
                    for j in range(8):
                        i = eng.transpose(ptp[:, j * 128:(j + 1) * 128],
                                          xn_tm[:, pb4 * C + j * 128: pb4 * C + (j + 1) * 128],
                                          ident[:])
                    i.then_inc(sems["eT_tp"], 1)

            def _kvq(g):
                base = (g % 2) * 4096
                for r in range(8):
                    rg = 8 * g + r
                    pr = rg % 2
                    is_k = r < 4
                    jc = r % 4
                    off = (512 + jc * 128) if is_k else (jc * 128)
                    if rg >= 2:
                        eng.wait_ge(sems["eV_e1"], rg - 1)
                    for cc in range(NCH):
                        i = eng.matmul(p_mm[:, pr * 512:(pr + 1) * 512],
                                       wkv_bf[:, cc * 1536 + off: cc * 1536 + off + 128],
                                       xn_grp[:, base + cc * 512: base + (cc + 1) * 512],
                                       start=(cc == 0), stop=(cc == NCH - 1))
                    i.then_inc(sems["eT_mm"], 1)
                    eng.wait_ge(sems["eV_raw"], rg + 1)
                    if rg >= 2:
                        eng.wait_ge(sems["eV_fin"], rg - 1)
                    eng.matmul(p_rot[:, pr * 512:(pr + 1) * 512], permT[:],
                               kraw[:, pr * 512:(pr + 1) * 512],
                               start=True, stop=True).then_inc(sems["eT_rot"], 1)
                for tb in range(4):
                    vb = 4 * g + tb
                    pv_ = vb % 2
                    if vb >= 2:
                        eng.wait_ge(sems["eV_vv"], vb - 1)
                    for cc in range(NCH):
                        i = eng.matmul(p_v[:, pv_ * 512:(pv_ + 1) * 512],
                                       xn_grp[:, base + cc * 512 + tb * 128: base + cc * 512 + (tb + 1) * 128],
                                       wkv_bf[:, cc * 1536 + 1024: cc * 1536 + 1536],
                                       start=(cc == 0), stop=(cc == NCH - 1))
                    i.then_inc(sems["eT_v"], 1)

            for g in range(NG):
                _tp(g)
                if g == 0:
                    qw_r.wait(eng, NCH - 1)
                    qw_r.wait(eng, NCH - 2)
                if g >= 1:
                    eng.wait_ge(sems["eV_xg"], 4 * g)
                    _kvq(g - 1)
            eng.wait_ge(sems["eV_xg"], 4 * NG)
            _kvq(NG - 1)

        p1s.close()
        p2s = contextlib.ExitStack()
        psc = p2s.enter_context(nc.psum_tensor("psc", [128, 3072], F32))
        ppv = p2s.enter_context(nc.psum_tensor("ppv", [128, 512], F32))
        pdn = p2s.enter_context(nc.psum_tensor("pdn", [128, 512], F32))
        pt_buf = p2s.enter_context(nc.sbuf_tensor("pt_buf", [128, 6 * 1024], BF16))
        attn_all = p2s.enter_context(nc.sbuf_tensor("attn_all", [128, 4 * T], BF16))
        dnsp = p2s.enter_context(nc.sbuf_tensor("dnsp", [128, 1024], F32))
        dn_buf = p2s.enter_context(nc.sbuf_tensor("dn_buf", [2, 1024], F32))
        dn_rec = p2s.enter_context(nc.sbuf_tensor("dn_rec", [2, 1024], F32))
        masks4 = p2s.enter_context(nc.sbuf_tensor("masks4", [128, 4 * 1024], BF16))
        bcast_t = p2s.enter_context(nc.sbuf_tensor("bcast_t", [128, 2 * 512], F32))

        STEPS = []
        for qt_ in range(8):
            for pr_ in range(4):
                for kb_ in range(4 * (qt_ + 1)):
                    STEPS.append((qt_, pr_, kb_))
        NSTEP = len(STEPS)
        GEND = {}
        DCTR = {}
        _d = 0
        for si_, st_ in enumerate(STEPS):
            qt_, pr_, kb_ = st_
            GEND[(qt_, pr_)] = si_
            if kb_ >= 4 * (qt_ + 1) - 4:
                _d += 1
                DCTR[si_] = _d

        def _sp_ap(handle, row_off, col_off, rowlen):
            return bass.AP(handle[:].tensor, row_off * rowlen + col_off,
                           [[64 * rowlen, 2], [1, 512]])

        def _pvdn(eng, si):
            qt, pr, kb = STEPS[si]
            nch = 4 * (qt + 1)
            idx2 = 4 * qt + pr
            eng.wait_ge(sems["eS_exp"], si + 1)
            if si in DCTR:
                eng.wait_ge(sems["eV_msk"], DCTR[si])
            if kb == 0 and idx2 >= 1:
                eng.wait_ge(sems["eV_av"], idx2)
                eng.wait_ge(sems["eV_dn"], idx2)
            sp4 = si % 6
            pts = pt_buf[:, sp4 * 1024:(sp4 + 1) * 1024]
            st_, sp_f = (kb == 0), (kb == nch - 1)
            h0 = 2 * pr
            eng.matmul(ppv[0:64, :],
                       v_tm[:, kb * 512 + h0 * 64: kb * 512 + (h0 + 1) * 64],
                       pts[:, 0:512], start=st_, stop=sp_f,
                       tile_position=(0, 0), skip_group_check=True)
            eng.matmul(ppv[64:128, :],
                       v_tm[:, kb * 512 + (h0 + 1) * 64: kb * 512 + (h0 + 2) * 64],
                       pts[:, 512:1024], start=st_, stop=sp_f,
                       tile_position=(0, 64), skip_group_check=True)
            eng.matmul(pdn[0:1, :], ones128[:],
                       pts[:, 0:512], start=st_, stop=sp_f,
                       tile_position=(0, 0), skip_group_check=True)
            eng.matmul(pdn[64:65, :], ones128[:],
                       pts[:, 512:1024], start=st_, stop=sp_f,
                       tile_position=(0, 64), skip_group_check=True
                       ).then_inc(sems["eT_pv"], 1)

        def p2a_tensor(eng):
            eng.wait_ge(sems["bar1"], 5)
            for si in range(NSTEP):
                qt, pr, kb = STEPS[si]
                jc = pr
                sp = si % 3
                if si >= 3:
                    eng.wait_ge(sems["eS_exp"], si - 2)
                for rt in range(2):
                    i = eng.matmul(
                        psc[:, sp * 1024 + rt * 512: sp * 1024 + (rt + 1) * 512],
                        k_fm[rt * 64:(rt + 1) * 64, jc * T + kb * 128: jc * T + (kb + 1) * 128],
                        q_fm[rt * 64:(rt + 1) * 64, jc * T + qt * 512: jc * T + (qt + 1) * 512],
                        start=True, stop=True, tile_position=(rt * 64, 0),
                        skip_group_check=True)
                i.then_inc(sems["eT_sc"], 1)
                if si >= 2:
                    _pvdn(eng, si - 2)
            _pvdn(eng, NSTEP - 2)
            _pvdn(eng, NSTEP - 1)
            eng.nop().then_inc(sems["bar2aT"], 1)

        def p2a_scalar(eng):
            eng.wait_ge(sems["bar1"], 5)
            for si in range(NSTEP):
                eng.wait_ge(sems["eT_sc"], si + 1)
                if si >= 6 and si % 2 == 0:
                    eng.wait_ge(sems["eT_pv"], si - 4)
                eng.activation(pt_buf[:, (si % 6) * 1024:(si % 6 + 1) * 1024],
                               psc[:, (si % 3) * 1024:(si % 3 + 1) * 1024],
                               AF.Exp, scale=SCALE).then_inc(sems["eS_exp"], 1)

        def p2a_vector(eng):
            eng.wait_ge(sems["bar1"], 5)
            qmk_r.wait(eng, 0)

            def _recip(j):
                if j < 0 or j > 31:
                    return
                qdn_r.wait(eng, j)
                eng.reciprocal(dn_rec[:, (j % 2) * 512:(j % 2 + 1) * 512],
                               dn_buf[:, (j % 2) * 512:(j % 2 + 1) * 512]
                               ).then_inc(sems["eV_rc"], 1)

            def _norm(j):
                if j < 0 or j > 31:
                    return
                qt2, pr2 = j // 4, j % 4
                qbc_r.wait(eng, j)
                eng.tensor_tensor(attn_all[:, pr2 * T + qt2 * 512: pr2 * T + (qt2 + 1) * 512],
                                  attn_all[:, pr2 * T + qt2 * 512: pr2 * T + (qt2 + 1) * 512],
                                  bcast_t[:, (j % 2) * 512:(j % 2 + 1) * 512],
                                  OP.mult).then_inc(sems["eV_nrm"], 1)

            for qt in range(8):
                for pr in range(4):
                    idx2 = 4 * qt + pr
                    nch = 4 * (qt + 1)
                    for si in range(GEND[(qt, pr)] - 3, GEND[(qt, pr)] + 1):
                        _q, _p, kb = STEPS[si]
                        r = kb - (nch - 4)
                        sp = si % 6
                        eng.wait_ge(sems["eS_exp"], si + 1)
                        eng.tensor_tensor(pt_buf[:, sp * 1024:(sp + 1) * 1024],
                                          pt_buf[:, sp * 1024:(sp + 1) * 1024],
                                          masks4[:, r * 1024:(r + 1) * 1024],
                                          OP.mult).then_inc(sems["eV_msk"], 1)
                    eng.wait_ge(sems["eT_pv"], GEND[(qt, pr)] + 1)
                    dc = (idx2 % 2) * 512
                    if idx2 >= 2:
                        qdn_r.wait(eng, idx2 - 2)
                    eng.tensor_copy(dnsp[0:1, dc:dc + 512], pdn[0:1, :])
                    eng.tensor_copy(dnsp[64:65, dc:dc + 512], pdn[64:65, :]
                                    ).then_inc(sems["eV_dn"], 1)
                    eng.tensor_copy(attn_all[:, pr * T + qt * 512: pr * T + (qt + 1) * 512],
                                    ppv[:]).then_inc(sems["eV_av"], 1)
                    vch.note(sems["eV_av"], idx2 + 1)
                    _recip(idx2 - 1)
                    _norm(idx2 - 2)
            _recip(31)
            _norm(30)
            _norm(31)

        def p2a_gpsimd(eng):
            eng.wait_ge(sems["bar1"], 5)
            eng.dma_start(out=masks4[:], in_=masks_in).then_inc(qmk_r.s[0], 16)

            def _bcast(j):
                eng.wait_ge(sems["eV_rc"], j + 1)
                s_, p_ = qbc_r.tag(j)
                for half in range(2):
                    srcap = bass.AP(dn_rec[:].tensor, half * 1024 + (j % 2) * 512,
                                    [[1024, 1], [0, 64], [1, 512]])
                    eng.dma_start(out=bcast_t[half * 64:(half + 1) * 64,
                                              (j % 2) * 512:(j % 2 + 1) * 512],
                                  in_=srcap).then_inc(s_, 16)

            for idx2 in range(32):
                eng.wait_ge(sems["eV_dn"], idx2 + 1)
                if idx2 >= 2:
                    eng.wait_ge(sems["eV_rc"], idx2 - 1)
                s_, p_ = qdn_r.tag(idx2)
                eng.dma_start(out=dn_buf[0:2, (idx2 % 2) * 512:(idx2 % 2 + 1) * 512],
                              in_=_sp_ap(dnsp, 0, (idx2 % 2) * 512, 1024)).then_inc(s_, 16)
                if idx2 >= 1:
                    _bcast(idx2 - 1)
            _bcast(31)

        p2s.close()
        pav.close()
        p3s = contextlib.ExitStack()
        p_tp3 = p3s.enter_context(nc.psum_tensor("p_tp3", [128, 1024], BF16))
        p_tp4 = p3s.enter_context(nc.psum_tensor("p_tp4", [128, 1024], BF16))
        p_big = p3s.enter_context(nc.psum_tensor("p_big", [128, 3072], F32))
        xn_own = p3s.enter_context(nc.sbuf_tensor("xn_own", [128, 8 * 2048], BF16))
        w1sl = p3s.enter_context(nc.sbuf_tensor("w1sl", [128, 4 * 1024], BF16))
        y_att = p3s.enter_context(nc.sbuf_tensor("y_att", [128, 16 * 1024], BF16))
        x_tb = p3s.enter_context(nc.sbuf_tensor("x_tb", [128, 1024], F32))
        out_tb = p3s.enter_context(nc.sbuf_tensor("out_tb", [128, 1024], F32))
        e_s = p3s.enter_context(nc.sbuf_tensor("e_s", [128, 2 * 8], F32))
        r_sml = p3s.enter_context(nc.sbuf_tensor("r_sml", [128, 16], F32))
        y_ffn_tm = p3s.enter_context(nc.sbuf_tensor("y_ffn_tm", [128, 8 * 1024], BF16))
        attn_tb = p3s.enter_context(nc.sbuf_tensor("attn_tb", [128, 2 * 1024], BF16))
        wo_bf = p3s.enter_context(nc.sbuf_tensor("wo_bf", [128, 8 * 1024], BF16))
        p3b = contextlib.ExitStack()
        h_half = p3b.enter_context(nc.sbuf_tensor("h_half", [128, 32 * 1024], BF16))
        w2sl = p3b.enter_context(nc.sbuf_tensor("w2sl", [128, 2 * 4096], BF16))
        y_fm_s = p3b.enter_context(nc.sbuf_tensor("y_fm_s", [128, 2 * 1024], BF16))
        p3b.close()

        def p2b_sync(eng):
            eng.wait_ge(sems["bar2aT"], 1)
            pid = eng.partition_id()
            with eng.If(pid % 2 == 0):
                for cc in range(8):
                    eng.dma_start(out=xn_own[:, cc * 2048:(cc + 1) * 2048],
                                  in_=xn_spill[cc * 128:(cc + 1) * 128, 0:2048]
                                  ).then_inc(sems["qxn"], 16)
            with eng.Else():
                for cc in range(8):
                    eng.dma_start(out=xn_own[:, cc * 2048:(cc + 1) * 2048],
                                  in_=xn_spill[cc * 128:(cc + 1) * 128, 2048:4096]
                                  ).then_inc(sems["qxn"], 16)
            eng.wait_ge(sems["ecc"], 1)
            with eng.If(pid % 2 == 0):
                for tb in range(16):
                    if tb >= 2:
                        eng.wait_ge(sems["eT_y"], tb - 1)
                    s_, p_ = qat_r.tag(tb)
                    srcap = bass.AP(cc_out_ap.tensor, tb * 128,
                                    [[4096, 128], [128 * 4096, 8], [1, 128]])
                    eng.dma_start(out=attn_tb[:, (tb % 2) * 1024:(tb % 2 + 1) * 1024],
                                  in_=srcap).then_inc(s_, p_)
            with eng.Else():
                for tb in range(16):
                    if tb >= 2:
                        eng.wait_ge(sems["eT_y"], tb - 1)
                    s_, p_ = qat_r.tag(tb)
                    srcap = bass.AP(cc_out_ap.tensor, 2048 + tb * 128,
                                    [[4096, 128], [128 * 4096, 8], [1, 128]])
                    eng.dma_start(out=attn_tb[:, (tb % 2) * 1024:(tb % 2 + 1) * 1024],
                                  in_=srcap).then_inc(s_, p_)
            for u3 in range(16):
                if u3 >= 1:
                    eng.wait_ge(sems["eV_out"], u3)
                s_, p_ = qxo_r.tag(u3)
                eng.dma_start(out=x_tb[:],
                              in_=x_own[u3 * 128:(u3 + 1) * 128, :]).then_inc(s_, p_)
                if u3 >= 1:
                    s2_, p2_ = qst_r.tag(u3 - 1)
                    eng.dma_start(out=y_out[(u3 - 1) * 128:u3 * 128, :],
                                  in_=out_tb[:]).then_inc(s2_, p2_)
            eng.wait_ge(sems["eV_out"], 16)
            s2_, p2_ = qst_r.tag(15)
            eng.dma_start(out=y_out[15 * 128:16 * 128, :], in_=out_tb[:]).then_inc(s2_, p2_)

        def p2b_gpsimd(eng):
            eng.wait_ge(sems["bar2aT"], 1)
            bo_rep = bass.AP(bo_in.tensor, 0, [[C, 1], [0, 128], [1, C]])
            b2_rep = bass.AP(b2_in.tensor, 0, [[C, 1], [0, 128], [1, C]])
            eng.dma_start(out=bb[:], in_=bo_rep).then_inc(sems["qbb"], 16)
            eng.dma_start(out=out_tb[:], in_=b2_rep).then_inc(sems["qbb"], 16)
            for u1 in range(4):
                s_, p_ = qwf1_r.tag(u1)
                srcap = bass.AP(w1_in.tensor, u1 * 128,
                                [[FF, 128], [128 * FF, 8], [1, 128]])
                eng.dma_start(out=w1sl[:, u1 * 1024:(u1 + 1) * 1024],
                              in_=srcap).then_inc(s_, p_)
            eng.wait_ge(sems["eV_nrm"], 32)
            for jc in range(4):
                eng.dma_start(out=cc_in_ap[jc * 128:(jc + 1) * 128, :],
                              in_=attn_all[:, jc * T:(jc + 1) * T]).then_inc(sems["qcc"], 16)
            eng.wait_ge(sems["qcc"], 64)
            eng.collective_compute(
                "AllGather", OP.bypass,
                replica_groups=[[0, 1], [2, 3], [4, 5], [6, 7]],
                ins=[cc_in_ap], outs=[cc_out_ap],
            ).then_inc(sems["ecc"], 1)
            for cc in range(NCH):
                s_, p_ = qwo_r.tag(cc)
                eng.dma_start(out=wo_bf[:, cc * 1024:(cc + 1) * 1024],
                              in_=wo_in[cc * 128:(cc + 1) * 128, :]).then_inc(s_, p_)
            def _w1load(eng, u1):
                fc = u1 % 32
                eng.wait_ge(sems["eT_h"], u1 - 3)
                s_, p_ = qwf1_r.tag(u1)
                srcap = bass.AP(w1_in.tensor, fc * 128,
                                [[FF, 128], [128 * FF, 8], [1, 128]])
                eng.dma_start(out=w1sl[:, (u1 % 4) * 1024:(u1 % 4 + 1) * 1024],
                              in_=srcap).then_inc(s_, p_)

            def _w2load(eng, u2):
                jc = u2 % 8
                if u2 >= 2:
                    eng.wait_ge(sems["eT_f"], u2 - 1)
                s_, p_ = qwf2_r.tag(u2)
                srcap = bass.AP(w2_in.tensor, jc * 128,
                                [[C, 128], [128 * C, 32], [1, 128]])
                eng.dma_start(out=w2sl[:, (u2 % 2) * 4096:(u2 % 2 + 1) * 4096],
                              in_=srcap).then_inc(s_, p_)

            for u1 in range(4, 32):
                _w1load(eng, u1)
            for u2 in range(0, 8):
                _w2load(eng, u2)
            for u1 in range(32, 64):
                _w1load(eng, u1)
            for u2 in range(8, 16):
                _w2load(eng, u2)
            qst_r.wait(eng, 15)
            qst_r.wait(eng, 14)

        def p2b_scalar(eng):
            for tb in range(16):
                eng.wait_ge(sems["eT_r"], tb + 1)
                if tb >= 2:
                    eng.wait_ge(sems["eV_rg"], tb - 1)
                eng.activation(e_s[:, (tb % 2) * 8:(tb % 2 + 1) * 8],
                               p_big[:, 2048 + (tb % 2) * 512: 2048 + (tb % 2) * 512 + 8],
                               AF.Exp).then_inc(sems["eS_re"], 1)
            for ph in range(2):
                for fc in range(32):
                    u1 = ph * 32 + fc
                    if ph == 0 and fc == 0:
                        eng.wait_ge(sems["qcc"], 64)
                    eng.wait_ge(sems["eT_h"], u1 + 1)
                    if ph == 1 and fc == 0:
                        eng.wait_ge(sems["eT_f"], 8)
                    eng.activation(h_half[:, fc * 1024:(fc + 1) * 1024],
                                   p_big[:, (fc % 2) * 1024:(fc % 2 + 1) * 1024],
                                   AF.Gelu_apprx_tanh,
                                   bias=b1r[:, fc:fc + 1]).then_inc(sems["eS_ge"], 1)

        def p2b_vector(eng):
            eng.wait_ge(sems["qbb"], 32)
            vch.inc(eng.tensor_tensor(bb[:], bb[:], out_tb[:], OP.add))
            vch.wait(eng)
            eng.nop().then_inc(sems["eV_bb"], 1)
            for tb in range(16):
                eng.wait_ge(sems["eS_re"], tb + 1)
                ec = e_s[:, (tb % 2) * 8:(tb % 2 + 1) * 8]
                mx1 = r_sml[:, 0:1]; meq = r_sml[:, 2:10]; mx2 = r_sml[:, 1:2]
                s12 = r_sml[:, 10:11]; r12 = r_sml[:, 11:12]
                ga = r_sml[:, 12:13]; gb = r_sml[:, 13:14]
                vch.inc(eng.reduce_max(mx1, ec, AX)); vch.wait(eng)
                vch.inc(eng.tensor_scalar(meq, ec, mx1, None, OP.is_equal)); vch.wait(eng)
                vch.inc(eng.tensor_scalar(meq, meq, -1.0, 1.0, OP.mult, OP.add)); vch.wait(eng)
                vch.inc(eng.tensor_tensor(meq, ec, meq, OP.mult)); vch.wait(eng)
                vch.inc(eng.reduce_max(mx2, meq, AX)); vch.wait(eng)
                vch.inc(eng.tensor_tensor(s12, mx1, mx2, OP.add)); vch.wait(eng)
                vch.inc(eng.reciprocal(r12, s12)); vch.wait(eng)
                vch.inc(eng.tensor_tensor(ga, mx1, r12, OP.mult)); vch.wait(eng)
                vch.inc(eng.tensor_tensor(gb, mx2, r12, OP.mult)); vch.wait(eng)
                vch.inc(eng.tensor_tensor(g_s[:, tb:tb + 1], ga, gb, OP.add))
                vch.wait(eng)
                eng.nop().then_inc(sems["eV_rg"], 1)
            eng.wait_ge(sems["eV_bb"], 1)
            for tb in range(16):
                eng.wait_ge(sems["eT_y"], tb + 1)
                eng.tensor_tensor(y_att[:, tb * 1024:(tb + 1) * 1024],
                                  p_big[:, (tb % 2) * 1024:(tb % 2 + 1) * 1024],
                                  bb[:], OP.add).then_inc(sems["eV_ye"], 1)
            for ph in range(2):

                def _w2evac(eng, jc2):
                    u2c = ph * 8 + jc2
                    eng.wait_ge(sems["eT_f"], u2c + 1)
                    eng.tensor_copy(y_fm_s[:, (jc2 % 2) * 1024:(jc2 % 2 + 1) * 1024],
                                    p_big[:, (jc2 % 2) * 1024:(jc2 % 2 + 1) * 1024]
                                    ).then_inc(sems["eV_fe"], 1)
                    eng.wait_ge(sems["eT_ft"], u2c + 1)
                    if jc2 == 0:
                        eng.wait_ge(sems["eV_rg"], 16)
                    ptp = p_tp3 if jc2 % 2 == 0 else p_tp4
                    for tbi in range(8):
                        i = eng.tensor_scalar_mul(
                            bass.AP(y_ffn_tm[:].tensor, tbi * 1024 + jc2 * 128,
                                    [[8 * 1024, 128], [1, 128]]),
                            ptp[:, tbi * 128:(tbi + 1) * 128],
                            g_s[:, ph * 8 + tbi: ph * 8 + tbi + 1])
                    i.then_inc(sems["eV_ft"], 1)
                    vch.note(sems["eV_ft"], u2c + 1)

                for jc in range(8):
                    _w2evac(eng, jc)
                for tbl in range(8):
                    u3 = ph * 8 + tbl
                    eng.wait_ge(sems["eV_ye"], u3 + 1)
                    qxo_r.wait(eng, u3)
                    if u3 >= 1:
                        qst_r.wait(eng, u3 - 1)
                    vch.wait(eng)
                    vch.inc(eng.tensor_tensor(
                        out_tb[:], y_ffn_tm[:, tbl * 1024:(tbl + 1) * 1024],
                        x_tb[:], OP.add))
                    vch.wait(eng)
                    eng.tensor_tensor(out_tb[:], out_tb[:],
                                      y_att[:, u3 * 1024:(u3 + 1) * 1024],
                                      OP.add).then_inc(sems["eV_out"], 1)

        def p2b_tensor(eng):
            eng.wait_ge(sems["eS_exp"], NSTEP)
            eng.wait_ge(sems["eV_av"], 32)
            eng.wait_ge(sems["eV_dn"], 32)
            eng.wait_ge(sems["qxn"], 128)
            for tb in range(16):
                if tb >= 2:
                    eng.wait_ge(sems["eS_re"], tb - 1)
                for cc in range(NCH):
                    i = eng.matmul(p_big[:, 2048 + (tb % 2) * 512: 2048 + (tb % 2) * 512 + 8],
                                   xn_own[:, cc * 2048 + tb * 128: cc * 2048 + (tb + 1) * 128],
                                   rw_bf[:, cc * 8:(cc + 1) * 8],
                                   start=(cc == 0), stop=(cc == NCH - 1),
                                   skip_group_check=True)
                i.then_inc(sems["eT_r"], 1)
            def _w1(eng, ph):
                for fc in range(32):
                    u1 = ph * 32 + fc
                    qwf1_r.wait(eng, u1)
                    if u1 >= 2:
                        eng.wait_ge(sems["eS_ge"], u1 - 1)
                    for cc in range(NCH):
                        for tt in range(2):
                            i = eng.matmul(
                                p_big[:, (fc % 2) * 1024 + tt * 512:(fc % 2) * 1024 + (tt + 1) * 512],
                                w1sl[:, (u1 % 4) * 1024 + cc * 128:(u1 % 4) * 1024 + (cc + 1) * 128],
                                xn_own[:, cc * 2048 + ph * 1024 + tt * 512: cc * 2048 + ph * 1024 + (tt + 1) * 512],
                                start=(cc == 0), stop=(cc == NCH - 1), skip_group_check=True)
                    i.then_inc(sems["eT_h"], 1)

            _w1(eng, 0)
            qwo_r.wait(eng, NCH - 1)
            qwo_r.wait(eng, NCH - 2)
            eng.wait_ge(sems["eS_ge"], 32)
            for tb in range(16):
                qat_r.wait(eng, tb)
                if tb >= 2:
                    eng.wait_ge(sems["eV_ye"], tb - 1)
                for gc in range(8):
                    for sl in range(2):
                        i = eng.matmul(
                            p_big[:, (tb % 2) * 1024 + sl * 512:(tb % 2) * 1024 + (sl + 1) * 512],
                            attn_tb[:, (tb % 2) * 1024 + gc * 128:(tb % 2) * 1024 + (gc + 1) * 128],
                            wo_bf[:, gc * 1024 + sl * 512: gc * 1024 + (sl + 1) * 512],
                            start=(gc == 0), stop=(gc == NCH - 1), skip_group_check=True)
                i.then_inc(sems["eT_y"], 1)
            eng.wait_ge(sems["eV_ye"], 16)
            for ph in range(2):
                if ph == 1:
                    eng.wait_ge(sems["eV_fe"], 8)
                    _w1(eng, 1)
                eng.wait_ge(sems["eS_ge"], (ph + 1) * 32)
                for jc in range(8):
                    u2 = ph * 8 + jc
                    qwf2_r.wait(eng, u2)
                    if u2 >= 2:
                        eng.wait_ge(sems["eV_fe"], u2 - 1)
                    for fc in range(32):
                        for tt in range(2):
                            i = eng.matmul(
                                p_big[:, (jc % 2) * 1024 + tt * 512:(jc % 2) * 1024 + (tt + 1) * 512],
                                w2sl[:, (u2 % 2) * 4096 + fc * 128:(u2 % 2) * 4096 + (fc + 1) * 128],
                                h_half[:, fc * 1024 + tt * 512: fc * 1024 + (tt + 1) * 512],
                                start=(fc == 0), stop=(fc == 31), skip_group_check=True)
                    i.then_inc(sems["eT_f"], 1)
                    eng.wait_ge(sems["eV_fe"], u2 + 1)
                    if u2 >= 2:
                        eng.wait_ge(sems["eV_ft"], u2 - 1)
                    ptp = p_tp3 if jc % 2 == 0 else p_tp4
                    for tbi in range(8):
                        i = eng.transpose(ptp[:, tbi * 128:(tbi + 1) * 128],
                                          y_fm_s[:, (jc % 2) * 1024 + tbi * 128:(jc % 2) * 1024 + (tbi + 1) * 128],
                                          ident[:])
                    i.then_inc(sems["eT_ft"], 1)


        with nc.Block() as blk:
            full = (debug_phase == 0 or debug_phase >= 3)

            @blk.sync
            def _(eng):
                p1_sync(eng)
                eng.nop().then_inc(sems["bar1"], 1)
                if full:
                    p2b_sync(eng)

            @blk.gpsimd
            def _(eng):
                p1_gpsimd(eng)
                eng.nop().then_inc(sems["bar1"], 1)
                if full:
                    p2a_gpsimd(eng)
                    p2b_gpsimd(eng)
                if debug_phase == 2:
                    p2a_gpsimd(eng)
                    eng.wait_ge(sems["ecc"], 1)
                    eng.dma_start(out=dbg["att"], in_=attn_all[:]).then_inc(sems["qst"], 16)
                    eng.dma_start(out=dbg["ccout"], in_=cc_out_ap).then_inc(sems["qst"], 16)
                    eng.wait_ge(sems["qst"], 32)
                if debug_phase == 1:
                    eng.wait_ge(sems["eV_fin"], 8 * NG)
                    eng.wait_ge(sems["eV_vv"], 4 * NG)
                    for jc in range(4):
                        eng.dma_start(out=dbg["k"][:, jc * T:(jc + 1) * T],
                                      in_=k_fm[:, jc * T:(jc + 1) * T]).then_inc(sems["qst"], 16)
                        eng.dma_start(out=dbg["q"][:, jc * T:(jc + 1) * T],
                                      in_=q_fm[:, jc * T:(jc + 1) * T]).then_inc(sems["qst"], 16)
                    for i8 in range(8):
                        eng.dma_start(out=dbg["v"][:, i8 * 2048:(i8 + 1) * 2048],
                                      in_=v_tm[:, i8 * 2048:(i8 + 1) * 2048]).then_inc(sems["qst"], 16)
                    eng.wait_ge(sems["qst"], 16 * 16)
                    qpc2_r.wait(eng, W1U + W2U - 1)
                    qpc2_r.wait(eng, W1U + W2U - 2)

            @blk.scalar
            def _(eng):
                p1_scalar(eng)
                eng.nop().then_inc(sems["bar1"], 1)
                if debug_phase >= 2 or full:
                    p2a_scalar(eng)
                if full:
                    p2b_scalar(eng)

            @blk.vector
            def _(eng):
                p1_vector(eng)
                eng.nop().then_inc(sems["bar1"], 1)
                if debug_phase >= 2 or full:
                    p2a_vector(eng)
                if full:
                    p2b_vector(eng)

            @blk.tensor
            def _(eng):
                p1_tensor(eng)
                eng.nop().then_inc(sems["bar1"], 1)
                if debug_phase >= 2 or full:
                    p2a_tensor(eng)
                if full:
                    p2b_tensor(eng)

        p3s.close()

    return nc, dbg



def _host_constants():
    half = D // 2
    inv_freq = (1.0 / 10000.0 ** (np.arange(half, dtype=np.float32) / half)).astype(np.float32)
    t = np.arange(T, dtype=np.float32)
    freqs = t[:, None] * inv_freq[None, :]
    cos = np.repeat(np.cos(freqs), 2, axis=-1)
    sin = np.repeat(np.sin(freqs), 2, axis=-1)
    cos_fm = np.concatenate([cos.T, cos.T], axis=0).astype(np.float32)
    sin_fm = np.concatenate([sin.T, sin.T], axis=0).astype(np.float32)
    ident = np.eye(128, dtype=np.float32)
    P = np.zeros((128, 128), dtype=np.float32)
    for i in range(64):
        P[2 * i, 2 * i + 1] = -1.0
        P[2 * i + 1, 2 * i] = 1.0
    permT = P.T.copy()
    masks = np.zeros((128, 4 * 1024), dtype=np.float32)
    ii = np.arange(128)[:, None]
    jj = np.arange(512)[None, :]
    for r in range(4):
        m = (ii + 128 * r <= jj).astype(np.float32)
        masks[:, r * 1024:r * 1024 + 512] = m
        masks[:, r * 1024 + 512:(r + 1) * 1024] = m
    return cos_fm, sin_fm, ident, permT, masks


def make_inputs(full):
    x = np.asarray(full["L0_attn_x"], dtype=np.float32)
    Wqkv = np.asarray(full["L0_attn_Wqkv"], dtype=np.float32)
    Wo = np.asarray(full["L0_attn_Wo"], dtype=np.float32)
    bo = np.asarray(full["L0_attn_bo"], dtype=np.float32)
    rw = np.asarray(full["L0_ffn_router_w"], dtype=np.float32)
    w1 = np.asarray(full["L0_ffn_w1"], dtype=np.float32)
    b1 = np.asarray(full["L0_ffn_b1"], dtype=np.float32)
    w2 = np.asarray(full["L0_ffn_w2"], dtype=np.float32)
    b2 = np.asarray(full["L0_ffn_b2"], dtype=np.float32)
    cos_fm, sin_fm, ident, permT, masks = _host_constants()
    b1r = np.ascontiguousarray(b1.reshape(32, 128).T)
    in_maps = []
    for c in range(8):
        b = c // 2
        hh = c % 2
        cols = np.concatenate([np.arange(512 * hh, 512 * hh + 512),
                               1024 + np.arange(512 * hh, 512 * hh + 512),
                               2048 + np.arange(512 * hh, 512 * hh + 512)])
        in_maps.append({
            "x_in": np.ascontiguousarray(x[b]),
            "x_own": np.ascontiguousarray(x[b, hh * 2048:(hh + 1) * 2048]),
            "wqkv": np.ascontiguousarray(Wqkv[:, cols]),
            "wo_in": Wo,
            "w1_in": w1,
            "w2_in": w2,
            "rw_in": rw,
            "b1r_in": b1r,
            "bo_in": bo.reshape(1, C),
            "b2_in": b2.reshape(1, C),
            "cos_in": cos_fm,
            "sin_in": sin_fm,
            "ident_in": ident,
            "permT_in": permT,
            "masks_in": masks.astype(_bf16),
        })
    return in_maps


_NC_CACHE = {}


def kernel(**inputs):
    in_maps = make_inputs(inputs)
    if "nc" not in _NC_CACHE:
        _NC_CACHE["nc"] = build_program(debug_phase=0)[0]
    nc = _NC_CACHE["nc"]
    res = run_bass_kernel_spmd(nc, in_maps, core_ids=list(range(8)))
    out = np.empty((B, T, C), dtype=np.float32)
    for c in range(8):
        b, hh = c // 2, c % 2
        out[b, hh * 2048:(hh + 1) * 2048, :] = res.results[c]["y_out"]
    return out


# version 19
# speedup vs baseline: 1.1771x; 1.0057x over previous
import contextlib
import numpy as np
import ml_dtypes
_bf16 = ml_dtypes.bfloat16
import concourse.bass as bass
import concourse.mybir as mybir
from concourse.bass_utils import run_bass_kernel_spmd

F32 = mybir.dt.float32
BF16 = mybir.dt.bfloat16
AF = mybir.ActivationFunctionType
OP = mybir.AluOpType
AX = mybir.AxisListType.X

B, T, C, H, D, FF = 4, 4096, 1024, 16, 64, 4096
NG = 8
NB = 32
NCH = 8
SCALE = float(D) ** -0.5

DEBUG = False


def build_program(debug_phase=0):
    nc = bass.Bass("TRN2", target_bir_lowering=False)
    x_in = nc.dram_tensor("x_in", [T, C], F32, kind="ExternalInput").ap()
    x_own = nc.dram_tensor("x_own", [2048, C], F32, kind="ExternalInput").ap()
    wqkv = nc.dram_tensor("wqkv", [C, 1536], F32, kind="ExternalInput").ap()
    wo_in = nc.dram_tensor("wo_in", [C, C], F32, kind="ExternalInput").ap()
    w1_in = nc.dram_tensor("w1_in", [C, FF], F32, kind="ExternalInput").ap()
    w2_in = nc.dram_tensor("w2_in", [FF, C], F32, kind="ExternalInput").ap()
    rw_in = nc.dram_tensor("rw_in", [C, 8], F32, kind="ExternalInput").ap()
    b1r_in = nc.dram_tensor("b1r_in", [128, 32], F32, kind="ExternalInput").ap()
    bo_in = nc.dram_tensor("bo_in", [1, C], F32, kind="ExternalInput").ap()
    b2_in = nc.dram_tensor("b2_in", [1, C], F32, kind="ExternalInput").ap()
    cos_in = nc.dram_tensor("cos_in", [128, T], F32, kind="ExternalInput").ap()
    sin_in = nc.dram_tensor("sin_in", [128, T], F32, kind="ExternalInput").ap()
    ident_in = nc.dram_tensor("ident_in", [128, 128], F32, kind="ExternalInput").ap()
    permT_in = nc.dram_tensor("permT_in", [128, 128], F32, kind="ExternalInput").ap()
    masks_in = nc.dram_tensor("masks_in", [128, 4 * 1024], BF16, kind="ExternalInput").ap()
    y_out = nc.dram_tensor("y_out", [2048, C], F32, kind="ExternalOutput").ap()

    xn_spill = nc.dram_tensor("xn_spill", [1024, 4096], BF16, kind="Internal").ap()
    cc_in = nc.dram_tensor("cc_in", [512, 4096], BF16, kind="Internal")
    cc_out = nc.dram_tensor("cc_out", [1024, 4096], BF16, kind="Internal")
    cc_in_ap = cc_in.ap()
    cc_out_ap = cc_out.ap()

    dbg = {}
    if debug_phase >= 1:
        dbg["k"] = nc.dram_tensor("dbg_k", [128, 4 * T], BF16, kind="ExternalOutput").ap()
        dbg["q"] = nc.dram_tensor("dbg_q", [128, 4 * T], BF16, kind="ExternalOutput").ap()
        dbg["v"] = nc.dram_tensor("dbg_v", [128, 32 * 512], BF16, kind="ExternalOutput").ap()
    if debug_phase >= 2:
        dbg["att"] = nc.dram_tensor("dbg_att", [128, 4 * T], BF16, kind="ExternalOutput").ap()
        dbg["ccout"] = nc.dram_tensor("dbg_ccout", [1024, 4096], BF16, kind="ExternalOutput").ap()

    est = contextlib.ExitStack()
    with est:
        ident = est.enter_context(nc.sbuf_tensor("ident", [128, 128], BF16))
        permT = est.enter_context(nc.sbuf_tensor("permT", [128, 128], BF16))
        ones128 = est.enter_context(nc.sbuf_tensor("ones128", [128, 1], BF16))
        b1r = est.enter_context(nc.sbuf_tensor("b1r", [128, 32], F32))
        rw_fm = est.enter_context(nc.sbuf_tensor("rw_fm", [128, 8 * 8], F32))
        bb = est.enter_context(nc.sbuf_tensor("bb", [128, C], F32))
        g_s = est.enter_context(nc.sbuf_tensor("g_s", [128, 16], F32))
        eps_s = est.enter_context(nc.sbuf_tensor("eps_s", [128, 1], F32))
        rw_bf = est.enter_context(nc.sbuf_tensor("rw_bf", [128, 8 * 8], BF16))

        sems = {}
        def sem(name):
            sems[name] = est.enter_context(nc.semaphore(name))
            return sems[name]
        class DmaRing:
            def __init__(self, name, per=16, ways=2):
                self.ways = ways
                self.s = [est.enter_context(nc.semaphore(name + str(k)))
                          for k in range(ways)]
                self.per = per
            def tag(self, i):
                return self.s[i % self.ways], self.per
            def tag6(self, i):
                return self.s[i % self.ways], self.per
            def wait(self, eng, i):
                eng.wait_ge(self.s[i % self.ways], self.per * (i // self.ways + 1))

        class Chain:
            def __init__(self, name):
                self.h = est.enter_context(nc.semaphore(name))
                self.n = 0
                self.last = None
            def inc(self, instr):
                instr.then_inc(self.h, 1)
                self.n += 1
                self.last = (self.h, self.n)
                return instr
            def note(self, semh, val):
                self.last = (semh, val)
            def wait(self, eng):
                if self.last is not None:
                    eng.wait_ge(self.last[0], self.last[1])

        vch = Chain("vch")
        sch = Chain("sch")
        tch = Chain("tch")
        gch = Chain("gch")
        qx_r = DmaRing("qx", 16, ways=4)
        qat_r = DmaRing("qat_r", 16)
        qwo_r = DmaRing("qwo_r", 16)
        qwf1_r = DmaRing("qwf1_r", 16, ways=4)
        qwf2_r = DmaRing("qwf2_r", 16)
        qxo_r = DmaRing("qxo_r", 16)
        qst_r = DmaRing("qst_r", 16)
        qdn_r = DmaRing("qdn_r", 16)
        qbc_r = DmaRing("qbc_r", 32)
        qmk_r = DmaRing("qmk_r", 16)
        qxs_r = DmaRing("qxs_r", 128)
        qw_r = DmaRing("qw", 16)
        qtab_r = DmaRing("qtab", 32)
        qpc1_r = DmaRing("qpc1", 16)
        qpc2_r = DmaRing("qpc2", 16)
        for n in ["qc",
                  "eS_sq", "eV_cf1", "eS_rstd", "eV_cf2", "eS_ln",
                  "eT_tp", "eV_xg", "eT_mm", "eV_raw", "eT_rot", "eV_e1", "eV_fin",
                  "eT_v", "eV_vv", "eV_wc", "eV_pc", "bar1",
                  "eT_sc", "eS_exp", "eV_msk", "eT_pv", "eV_av", "eV_dn", "eV_rc", "eV_nrm",
                  "qcc", "ecc", "bar2aT", "qxn", "qbb", "eV_bb", "eV_mk",
                  "eS_sq2", "eV_cf12", "eS_rstd2", "eV_cf22", "eS_ln2",
                  "eT_tp2", "eV_xg2", "eV_woc", "eT_y", "eV_ye",
                  "eT_r", "eS_re", "eV_rg",
                  "eT_h", "eS_ge", "eT_f", "eV_w1c", "eV_w2c",
                  "eV_fe", "eT_ft", "eV_ft", "eV_out", "qst",
                  "eS_ac", "eV_va", "eS_rs", "eV_st"]:
            sem(n)

        pav = contextlib.ExitStack()
        k_fm = pav.enter_context(nc.sbuf_tensor("k_fm", [128, 4 * T], BF16))
        q_fm = pav.enter_context(nc.sbuf_tensor("q_fm", [128, 4 * T], BF16))
        v_tm = pav.enter_context(nc.sbuf_tensor("v_tm", [128, 32 * 512], BF16))
        p1s = contextlib.ExitStack()
        p_tp = p1s.enter_context(nc.psum_tensor("p_tp", [128, 1024], BF16))
        p_tp2 = p1s.enter_context(nc.psum_tensor("p_tp2", [128, 1024], BF16))
        p_mm = p1s.enter_context(nc.psum_tensor("p_mm", [128, 2 * 512], F32))
        p_rot = p1s.enter_context(nc.psum_tensor("p_rot", [128, 2 * 512], F32))
        p_v = p1s.enter_context(nc.psum_tensor("p_v", [128, 2 * 512], F32))
        x_blk = p1s.enter_context(nc.sbuf_tensor("x_blk", [128, 4 * C], F32))
        xn_tm = p1s.enter_context(nc.sbuf_tensor("xn_tm", [128, 4 * C], BF16))
        xn_grp = p1s.enter_context(nc.sbuf_tensor("xn_grp", [128, 2 * 8 * 512], BF16))
        wkv_bf = p1s.enter_context(nc.sbuf_tensor("wkv_bf", [128, 8 * 1536], BF16))
        tab_c = p1s.enter_context(nc.sbuf_tensor("tab_c", [128, 2 * 512], F32))
        tab_s = p1s.enter_context(nc.sbuf_tensor("tab_s", [128, 2 * 512], F32))
        kraw = p1s.enter_context(nc.sbuf_tensor("kraw", [128, 2 * 512], BF16))
        tmp1 = p1s.enter_context(nc.sbuf_tensor("tmp1", [128, 2 * 512], BF16))
        tmp2 = p1s.enter_context(nc.sbuf_tensor("tmp2", [128, 2 * 512], BF16))
        sum_s = p1s.enter_context(nc.sbuf_tensor("sum_s", [128, 4], F32))
        sq_s = p1s.enter_context(nc.sbuf_tensor("sq_s", [128, 4], F32))
        m_s = p1s.enter_context(nc.sbuf_tensor("m_s", [128, 4], F32))
        var_s = p1s.enter_context(nc.sbuf_tensor("var_s", [128, 4], F32))
        rstd_s = p1s.enter_context(nc.sbuf_tensor("rstd_s", [128, 4], F32))
        bneg_s = p1s.enter_context(nc.sbuf_tensor("bneg_s", [128, 4], F32))
        scr = p1s.enter_context(nc.sbuf_tensor("scr", [128, C], BF16))
        sq2_s = p1s.enter_context(nc.sbuf_tensor("sq2_s", [128, 4], F32))

        W1U = 16
        W2U = 32


        def p1_sync(eng):
            for Bi in range(NB):
                if Bi >= 4:
                    eng.wait_ge(sems["eS_ln"], Bi - 3)
                s_, p_ = qx_r.tag(Bi)
                eng.dma_start(out=x_blk[:, (Bi % 4) * C:(Bi % 4 + 1) * C],
                              in_=x_in[Bi * 128:(Bi + 1) * 128, :]).then_inc(s_, p_)

        def p1_gpsimd(eng):
            eng.dma_start(out=ident[:], in_=ident_in).then_inc(sems["qc"], 16)
            eng.dma_start(out=permT[:], in_=permT_in).then_inc(sems["qc"], 16)
            eng.dma_start(out=b1r[:], in_=b1r_in).then_inc(sems["qc"], 16)
            for cc in range(NCH):
                eng.dma_start(out=rw_fm[:, cc * 8:(cc + 1) * 8],
                              in_=rw_in[cc * 128:(cc + 1) * 128, :]).then_inc(sems["qc"], 16)
            for cc in range(NCH):
                s_, p_ = qw_r.tag(cc)
                eng.dma_start(out=wkv_bf[:, cc * 1536:(cc + 1) * 1536],
                              in_=wqkv[cc * 128:(cc + 1) * 128, :]).then_inc(s_, p_)
            for g in range(NG):
                if g >= 2:
                    eng.wait_ge(sems["eV_fin"], 8 * (g - 1))
                s_, _ = qtab_r.tag(g)
                eng.dma_start(out=tab_c[:, (g % 2) * 512:(g % 2 + 1) * 512],
                              in_=cos_in[:, g * 512:(g + 1) * 512]).then_inc(s_, 16)
                eng.dma_start(out=tab_s[:, (g % 2) * 512:(g % 2 + 1) * 512],
                              in_=sin_in[:, g * 512:(g + 1) * 512]).then_inc(s_, 16)
                if g >= 1:
                    eng.wait_ge(sems["eV_xg"], 4 * g)
                    sx_, px_ = qxs_r.tag(g - 1)
                    gb = ((g - 1) % 2) * 4096
                    for cc in range(NCH):
                        eng.dma_start(out=xn_spill[cc * 128:(cc + 1) * 128,
                                                   (g - 1) * 512: g * 512],
                                      in_=xn_grp[:, gb + cc * 512: gb + (cc + 1) * 512]).then_inc(sx_, 16)
            eng.wait_ge(sems["eV_xg"], 4 * NG)
            sx_, px_ = qxs_r.tag(NG - 1)
            gb = ((NG - 1) % 2) * 4096
            for cc in range(NCH):
                eng.dma_start(out=xn_spill[cc * 128:(cc + 1) * 128,
                                           (NG - 1) * 512: NG * 512],
                              in_=xn_grp[:, gb + cc * 512: gb + (cc + 1) * 512]).then_inc(sx_, 16)
            qxs_r.wait(eng, NG - 1)
            qxs_r.wait(eng, NG - 2)
        def p1_scalar(eng):
            eng.wait_ge(sems["eV_cf2"], 1)
            for g in range(NG):
                for b in range(4):
                    Bi = 4 * g + b
                    qx_r.wait(eng, Bi)
                    if Bi >= 4:
                        eng.wait_ge(sems["eT_tp"], Bi - 3)
                    eng.activation(scr[:], x_blk[:, b * C:(b + 1) * C], AF.Square,
                                   accum_out=sq_s[:, b:b + 1])
                    i = eng.activation(xn_tm[:, b * C:(b + 1) * C],
                                       x_blk[:, b * C:(b + 1) * C], AF.Copy,
                                       accum_out=sum_s[:, b:b + 1])
                i.then_inc(sems["eS_ac"], 1)
                eng.wait_ge(sems["eV_va"], g + 1)
                sch.inc(eng.activation(var_s[:], var_s[:], AF.Ln, bias=eps_s[:, 0:1]))
                sch.wait(eng)
                eng.activation(rstd_s[:], var_s[:], AF.Exp,
                               scale=-0.5).then_inc(sems["eS_rs"], 1)
                eng.wait_ge(sems["eV_st"], g + 1)
                for b in range(4):
                    eng.activation(xn_tm[:, b * C:(b + 1) * C], x_blk[:, b * C:(b + 1) * C],
                                   AF.Identity, bias=bneg_s[:, b:b + 1],
                                   scale=rstd_s[:, b:b + 1]).then_inc(sems["eS_ln"], 1)

        def p1_vector(eng):
            eng.memset(eps_s[:], 1e-5)
            eng.memset(ones128[:], 1.0).then_inc(sems["eV_cf2"], 1)
            eng.wait_ge(sems["qc"], 176)
            eng.tensor_copy(rw_bf[:], rw_fm[:])

            def _stats(g):
                eng.wait_ge(sems["eS_ac"], g + 1)
                vch.inc(eng.tensor_scalar(m_s[:], sum_s[:], 1.0 / C, None, OP.mult))
                vch.wait(eng)
                vch.inc(eng.tensor_tensor(var_s[:], m_s[:], m_s[:], OP.mult))
                vch.wait(eng)
                vch.inc(eng.tensor_scalar(sq2_s[:], sq_s[:], 1.0 / C, None, OP.mult))
                vch.wait(eng)
                eng.tensor_tensor(var_s[:], sq2_s[:], var_s[:], OP.subtract
                                  ).then_inc(sems["eV_va"], 1)
                vch.note(sems["eV_va"], g + 1)
                eng.wait_ge(sems["eS_rs"], g + 1)
                if g >= 1:
                    eng.wait_ge(sems["eS_ln"], 4 * g)
                vch.inc(eng.tensor_tensor(bneg_s[:], m_s[:], rstd_s[:], OP.mult))
                vch.wait(eng)
                eng.tensor_scalar(bneg_s[:], bneg_s[:], -1.0, None,
                                  OP.mult).then_inc(sems["eV_st"], 1)

            def _evacs(g):
                for b in range(4):
                    Bi = 4 * g + b
                    eng.wait_ge(sems["eT_tp"], Bi + 1)
                    if b == 0 and g >= 2:
                        qxs_r.wait(eng, g - 2)
                    ptp = p_tp if Bi % 2 == 0 else p_tp2
                    srcv = ptp[:].rearrange("p (cc t) -> p cc t", cc=8)
                    dstv = bass.AP(xn_grp[:].tensor, (g % 2) * 4096 + b * 128,
                                   [[8192, 128], [512, 8], [1, 128]])
                    eng.tensor_copy(dstv, srcv).then_inc(sems["eV_xg"], 1)

            def _rope(g):
                for r in range(8):
                    rg = 8 * g + r
                    pr = rg % 2
                    is_k = r < 4
                    jc = r % 4
                    eng.wait_ge(sems["eT_mm"], rg + 1)
                    eng.tensor_copy(kraw[:, pr * 512:(pr + 1) * 512],
                                    p_mm[:, pr * 512:(pr + 1) * 512]).then_inc(sems["eV_raw"], 1)
                    qtab_r.wait(eng, g)
                    eng.tensor_tensor(tmp1[:, pr * 512:(pr + 1) * 512],
                                      p_mm[:, pr * 512:(pr + 1) * 512],
                                      tab_c[:, (g % 2) * 512:(g % 2 + 1) * 512],
                                      OP.mult).then_inc(sems["eV_e1"], 1)
                    vch.note(sems["eV_e1"], rg + 1)
                    eng.wait_ge(sems["eT_rot"], rg + 1)
                    vch.inc(eng.tensor_tensor(tmp2[:, pr * 512:(pr + 1) * 512],
                                              p_rot[:, pr * 512:(pr + 1) * 512],
                                              tab_s[:, (g % 2) * 512:(g % 2 + 1) * 512], OP.mult))
                    vch.wait(eng)
                    dst = (k_fm if is_k else q_fm)[:, jc * T + g * 512: jc * T + (g + 1) * 512]
                    eng.tensor_tensor(dst, tmp1[:, pr * 512:(pr + 1) * 512],
                                      tmp2[:, pr * 512:(pr + 1) * 512], OP.add
                                      ).then_inc(sems["eV_fin"], 1)
                for tb in range(4):
                    vb = 4 * g + tb
                    eng.wait_ge(sems["eT_v"], vb + 1)
                    eng.tensor_copy(v_tm[:, vb * 512:(vb + 1) * 512],
                                    p_v[:, (vb % 2) * 512:(vb % 2 + 1) * 512]
                                    ).then_inc(sems["eV_vv"], 1)

            for g in range(NG):
                _stats(g)
                _evacs(g)
                if g >= 1:
                    _rope(g - 1)
            _rope(NG - 1)

        def p1_tensor(eng):
            eng.wait_ge(sems["qc"], 176)

            def _tp(g):
                for b in range(4):
                    Bi = 4 * g + b
                    pb4 = Bi % 4
                    ptp = p_tp if Bi % 2 == 0 else p_tp2
                    eng.wait_ge(sems["eS_ln"], Bi + 1)
                    if Bi >= 2:
                        eng.wait_ge(sems["eV_xg"], Bi - 1)
                    for j in range(8):
                        i = eng.transpose(ptp[:, j * 128:(j + 1) * 128],
                                          xn_tm[:, pb4 * C + j * 128: pb4 * C + (j + 1) * 128],
                                          ident[:])
                    i.then_inc(sems["eT_tp"], 1)

            def _kvq(g):
                base = (g % 2) * 4096
                for r in range(8):
                    rg = 8 * g + r
                    pr = rg % 2
                    is_k = r < 4
                    jc = r % 4
                    off = (512 + jc * 128) if is_k else (jc * 128)
                    if rg >= 2:
                        eng.wait_ge(sems["eV_e1"], rg - 1)
                    for cc in range(NCH):
                        i = eng.matmul(p_mm[:, pr * 512:(pr + 1) * 512],
                                       wkv_bf[:, cc * 1536 + off: cc * 1536 + off + 128],
                                       xn_grp[:, base + cc * 512: base + (cc + 1) * 512],
                                       start=(cc == 0), stop=(cc == NCH - 1))
                    i.then_inc(sems["eT_mm"], 1)
                    eng.wait_ge(sems["eV_raw"], rg + 1)
                    if rg >= 2:
                        eng.wait_ge(sems["eV_fin"], rg - 1)
                    eng.matmul(p_rot[:, pr * 512:(pr + 1) * 512], permT[:],
                               kraw[:, pr * 512:(pr + 1) * 512],
                               start=True, stop=True).then_inc(sems["eT_rot"], 1)
                for tb in range(4):
                    vb = 4 * g + tb
                    pv_ = vb % 2
                    if vb >= 2:
                        eng.wait_ge(sems["eV_vv"], vb - 1)
                    for cc in range(NCH):
                        i = eng.matmul(p_v[:, pv_ * 512:(pv_ + 1) * 512],
                                       xn_grp[:, base + cc * 512 + tb * 128: base + cc * 512 + (tb + 1) * 128],
                                       wkv_bf[:, cc * 1536 + 1024: cc * 1536 + 1536],
                                       start=(cc == 0), stop=(cc == NCH - 1))
                    i.then_inc(sems["eT_v"], 1)

            for g in range(NG):
                _tp(g)
                if g == 0:
                    qw_r.wait(eng, NCH - 1)
                    qw_r.wait(eng, NCH - 2)
                if g >= 1:
                    eng.wait_ge(sems["eV_xg"], 4 * g)
                    _kvq(g - 1)
            eng.wait_ge(sems["eV_xg"], 4 * NG)
            _kvq(NG - 1)

        p1s.close()
        p2s = contextlib.ExitStack()
        psc = p2s.enter_context(nc.psum_tensor("psc", [128, 3072], F32))
        ppv = p2s.enter_context(nc.psum_tensor("ppv", [128, 512], F32))
        pdn = p2s.enter_context(nc.psum_tensor("pdn", [128, 512], F32))
        pt_buf = p2s.enter_context(nc.sbuf_tensor("pt_buf", [128, 6 * 1024], BF16))
        attn_all = p2s.enter_context(nc.sbuf_tensor("attn_all", [128, 4 * T], BF16))
        dnsp = p2s.enter_context(nc.sbuf_tensor("dnsp", [128, 1024], F32))
        dn_buf = p2s.enter_context(nc.sbuf_tensor("dn_buf", [2, 1024], F32))
        dn_rec = p2s.enter_context(nc.sbuf_tensor("dn_rec", [2, 1024], F32))
        masks4 = p2s.enter_context(nc.sbuf_tensor("masks4", [128, 4 * 1024], BF16))
        bcast_t = p2s.enter_context(nc.sbuf_tensor("bcast_t", [128, 2 * 512], F32))

        STEPS = []
        for qt_ in range(8):
            for pr_ in range(4):
                for kb_ in range(4 * (qt_ + 1)):
                    STEPS.append((qt_, pr_, kb_))
        NSTEP = len(STEPS)
        GEND = {}
        DCTR = {}
        _d = 0
        for si_, st_ in enumerate(STEPS):
            qt_, pr_, kb_ = st_
            GEND[(qt_, pr_)] = si_
            if kb_ >= 4 * (qt_ + 1) - 4:
                _d += 1
                DCTR[si_] = _d

        def _sp_ap(handle, row_off, col_off, rowlen):
            return bass.AP(handle[:].tensor, row_off * rowlen + col_off,
                           [[64 * rowlen, 2], [1, 512]])

        def _pvdn(eng, si):
            qt, pr, kb = STEPS[si]
            nch = 4 * (qt + 1)
            idx2 = 4 * qt + pr
            eng.wait_ge(sems["eS_exp"], si + 1)
            if si in DCTR:
                eng.wait_ge(sems["eV_msk"], DCTR[si])
            if kb == 0 and idx2 >= 1:
                eng.wait_ge(sems["eV_av"], idx2)
                eng.wait_ge(sems["eV_dn"], idx2)
            sp4 = si % 6
            pts = pt_buf[:, sp4 * 1024:(sp4 + 1) * 1024]
            st_, sp_f = (kb == 0), (kb == nch - 1)
            h0 = 2 * pr
            eng.matmul(ppv[0:64, :],
                       v_tm[:, kb * 512 + h0 * 64: kb * 512 + (h0 + 1) * 64],
                       pts[:, 0:512], start=st_, stop=sp_f,
                       tile_position=(0, 0), skip_group_check=True)
            eng.matmul(ppv[64:128, :],
                       v_tm[:, kb * 512 + (h0 + 1) * 64: kb * 512 + (h0 + 2) * 64],
                       pts[:, 512:1024], start=st_, stop=sp_f,
                       tile_position=(0, 64), skip_group_check=True)
            eng.matmul(pdn[0:1, :], ones128[:],
                       pts[:, 0:512], start=st_, stop=sp_f,
                       tile_position=(0, 0), skip_group_check=True)
            eng.matmul(pdn[64:65, :], ones128[:],
                       pts[:, 512:1024], start=st_, stop=sp_f,
                       tile_position=(0, 64), skip_group_check=True
                       ).then_inc(sems["eT_pv"], 1)

        def p2a_tensor(eng):
            eng.wait_ge(sems["bar1"], 5)
            for si in range(NSTEP):
                qt, pr, kb = STEPS[si]
                jc = pr
                sp = si % 3
                if si >= 3:
                    eng.wait_ge(sems["eS_exp"], si - 2)
                for rt in range(2):
                    i = eng.matmul(
                        psc[:, sp * 1024 + rt * 512: sp * 1024 + (rt + 1) * 512],
                        k_fm[rt * 64:(rt + 1) * 64, jc * T + kb * 128: jc * T + (kb + 1) * 128],
                        q_fm[rt * 64:(rt + 1) * 64, jc * T + qt * 512: jc * T + (qt + 1) * 512],
                        start=True, stop=True, tile_position=(rt * 64, 0),
                        skip_group_check=True)
                i.then_inc(sems["eT_sc"], 1)
                if si >= 2:
                    _pvdn(eng, si - 2)
            _pvdn(eng, NSTEP - 2)
            _pvdn(eng, NSTEP - 1)
            eng.nop().then_inc(sems["bar2aT"], 1)

        def p2a_scalar(eng):
            eng.wait_ge(sems["bar1"], 5)
            for si in range(NSTEP):
                eng.wait_ge(sems["eT_sc"], si + 1)
                if si >= 6 and si % 2 == 0:
                    eng.wait_ge(sems["eT_pv"], si - 4)
                eng.activation(pt_buf[:, (si % 6) * 1024:(si % 6 + 1) * 1024],
                               psc[:, (si % 3) * 1024:(si % 3 + 1) * 1024],
                               AF.Exp, scale=SCALE).then_inc(sems["eS_exp"], 1)

        def p2a_vector(eng):
            eng.wait_ge(sems["bar1"], 5)
            qmk_r.wait(eng, 0)

            def _recip(j):
                if j < 0 or j > 31:
                    return
                qdn_r.wait(eng, j)
                eng.reciprocal(dn_rec[:, (j % 2) * 512:(j % 2 + 1) * 512],
                               dn_buf[:, (j % 2) * 512:(j % 2 + 1) * 512]
                               ).then_inc(sems["eV_rc"], 1)

            def _norm(j):
                if j < 0 or j > 31:
                    return
                qt2, pr2 = j // 4, j % 4
                qbc_r.wait(eng, j)
                eng.tensor_tensor(attn_all[:, pr2 * T + qt2 * 512: pr2 * T + (qt2 + 1) * 512],
                                  attn_all[:, pr2 * T + qt2 * 512: pr2 * T + (qt2 + 1) * 512],
                                  bcast_t[:, (j % 2) * 512:(j % 2 + 1) * 512],
                                  OP.mult).then_inc(sems["eV_nrm"], 1)

            for qt in range(8):
                for pr in range(4):
                    idx2 = 4 * qt + pr
                    nch = 4 * (qt + 1)
                    for si in range(GEND[(qt, pr)] - 3, GEND[(qt, pr)] + 1):
                        _q, _p, kb = STEPS[si]
                        r = kb - (nch - 4)
                        sp = si % 6
                        eng.wait_ge(sems["eS_exp"], si + 1)
                        eng.tensor_tensor(pt_buf[:, sp * 1024:(sp + 1) * 1024],
                                          pt_buf[:, sp * 1024:(sp + 1) * 1024],
                                          masks4[:, r * 1024:(r + 1) * 1024],
                                          OP.mult).then_inc(sems["eV_msk"], 1)
                    eng.wait_ge(sems["eT_pv"], GEND[(qt, pr)] + 1)
                    dc = (idx2 % 2) * 512
                    if idx2 >= 2:
                        qdn_r.wait(eng, idx2 - 2)
                    eng.tensor_copy(dnsp[0:1, dc:dc + 512], pdn[0:1, :])
                    eng.tensor_copy(dnsp[64:65, dc:dc + 512], pdn[64:65, :]
                                    ).then_inc(sems["eV_dn"], 1)
                    eng.tensor_copy(attn_all[:, pr * T + qt * 512: pr * T + (qt + 1) * 512],
                                    ppv[:]).then_inc(sems["eV_av"], 1)
                    vch.note(sems["eV_av"], idx2 + 1)
                    _recip(idx2 - 1)
                    _norm(idx2 - 2)
            _recip(31)
            _norm(30)
            _norm(31)

        def p2a_gpsimd(eng):
            eng.wait_ge(sems["bar1"], 5)
            eng.dma_start(out=masks4[:], in_=masks_in).then_inc(qmk_r.s[0], 16)

            def _bcast(j):
                eng.wait_ge(sems["eV_rc"], j + 1)
                s_, p_ = qbc_r.tag(j)
                for half in range(2):
                    srcap = bass.AP(dn_rec[:].tensor, half * 1024 + (j % 2) * 512,
                                    [[1024, 1], [0, 64], [1, 512]])
                    eng.dma_start(out=bcast_t[half * 64:(half + 1) * 64,
                                              (j % 2) * 512:(j % 2 + 1) * 512],
                                  in_=srcap).then_inc(s_, 16)

            for idx2 in range(32):
                eng.wait_ge(sems["eV_dn"], idx2 + 1)
                if idx2 >= 2:
                    eng.wait_ge(sems["eV_rc"], idx2 - 1)
                s_, p_ = qdn_r.tag(idx2)
                eng.dma_start(out=dn_buf[0:2, (idx2 % 2) * 512:(idx2 % 2 + 1) * 512],
                              in_=_sp_ap(dnsp, 0, (idx2 % 2) * 512, 1024)).then_inc(s_, 16)
                if idx2 >= 1:
                    _bcast(idx2 - 1)
            _bcast(31)

        p2s.close()
        pav.close()
        p3s = contextlib.ExitStack()
        p_tp3 = p3s.enter_context(nc.psum_tensor("p_tp3", [128, 1024], BF16))
        p_tp4 = p3s.enter_context(nc.psum_tensor("p_tp4", [128, 1024], BF16))
        p_big = p3s.enter_context(nc.psum_tensor("p_big", [128, 3072], F32))
        xn_own = p3s.enter_context(nc.sbuf_tensor("xn_own", [128, 8 * 2048], BF16))
        w1sl = p3s.enter_context(nc.sbuf_tensor("w1sl", [128, 4 * 1024], BF16))
        y_att = p3s.enter_context(nc.sbuf_tensor("y_att", [128, 16 * 1024], BF16))
        x_tb = p3s.enter_context(nc.sbuf_tensor("x_tb", [128, 1024], F32))
        out_tb = p3s.enter_context(nc.sbuf_tensor("out_tb", [128, 1024], F32))
        e_s = p3s.enter_context(nc.sbuf_tensor("e_s", [128, 2 * 8], F32))
        r_sml = p3s.enter_context(nc.sbuf_tensor("r_sml", [128, 16], F32))
        y_ffn_tm = p3s.enter_context(nc.sbuf_tensor("y_ffn_tm", [128, 8 * 1024], BF16))
        attn_tb = p3s.enter_context(nc.sbuf_tensor("attn_tb", [128, 2 * 1024], BF16))
        wo_bf = p3s.enter_context(nc.sbuf_tensor("wo_bf", [128, 8 * 1024], BF16))
        p3b = contextlib.ExitStack()
        h_half = p3b.enter_context(nc.sbuf_tensor("h_half", [128, 32 * 1024], BF16))
        w2sl = p3b.enter_context(nc.sbuf_tensor("w2sl", [128, 2 * 4096], BF16))
        y_fm_s = p3b.enter_context(nc.sbuf_tensor("y_fm_s", [128, 2 * 1024], BF16))
        p3b.close()

        def p2b_sync(eng):
            eng.wait_ge(sems["bar2aT"], 1)
            pid = eng.partition_id()
            with eng.If(pid % 2 == 0):
                for cc in range(8):
                    eng.dma_start(out=xn_own[:, cc * 2048:(cc + 1) * 2048],
                                  in_=xn_spill[cc * 128:(cc + 1) * 128, 0:2048]
                                  ).then_inc(sems["qxn"], 16)
            with eng.Else():
                for cc in range(8):
                    eng.dma_start(out=xn_own[:, cc * 2048:(cc + 1) * 2048],
                                  in_=xn_spill[cc * 128:(cc + 1) * 128, 2048:4096]
                                  ).then_inc(sems["qxn"], 16)
            eng.wait_ge(sems["ecc"], 1)
            with eng.If(pid % 2 == 0):
                for tb in range(16):
                    if tb >= 2:
                        eng.wait_ge(sems["eT_y"], tb - 1)
                    s_, p_ = qat_r.tag(tb)
                    srcap = bass.AP(cc_out_ap.tensor, tb * 128,
                                    [[4096, 128], [128 * 4096, 8], [1, 128]])
                    eng.dma_start(out=attn_tb[:, (tb % 2) * 1024:(tb % 2 + 1) * 1024],
                                  in_=srcap).then_inc(s_, p_)
            with eng.Else():
                for tb in range(16):
                    if tb >= 2:
                        eng.wait_ge(sems["eT_y"], tb - 1)
                    s_, p_ = qat_r.tag(tb)
                    srcap = bass.AP(cc_out_ap.tensor, 2048 + tb * 128,
                                    [[4096, 128], [128 * 4096, 8], [1, 128]])
                    eng.dma_start(out=attn_tb[:, (tb % 2) * 1024:(tb % 2 + 1) * 1024],
                                  in_=srcap).then_inc(s_, p_)
            for u3 in range(16):
                if u3 >= 1:
                    eng.wait_ge(sems["eV_out"], u3)
                s_, p_ = qxo_r.tag(u3)
                eng.dma_start(out=x_tb[:],
                              in_=x_own[u3 * 128:(u3 + 1) * 128, :]).then_inc(s_, p_)
                if u3 >= 1:
                    s2_, p2_ = qst_r.tag(u3 - 1)
                    eng.dma_start(out=y_out[(u3 - 1) * 128:u3 * 128, :],
                                  in_=out_tb[:]).then_inc(s2_, p2_)
            eng.wait_ge(sems["eV_out"], 16)
            s2_, p2_ = qst_r.tag(15)
            eng.dma_start(out=y_out[15 * 128:16 * 128, :], in_=out_tb[:]).then_inc(s2_, p2_)

        def p2b_gpsimd(eng):
            eng.wait_ge(sems["bar2aT"], 1)
            bo_rep = bass.AP(bo_in.tensor, 0, [[C, 1], [0, 128], [1, C]])
            b2_rep = bass.AP(b2_in.tensor, 0, [[C, 1], [0, 128], [1, C]])
            eng.dma_start(out=bb[:], in_=bo_rep).then_inc(sems["qbb"], 16)
            eng.dma_start(out=out_tb[:], in_=b2_rep).then_inc(sems["qbb"], 16)
            for u1 in range(4):
                s_, p_ = qwf1_r.tag(u1)
                srcap = bass.AP(w1_in.tensor, u1 * 128,
                                [[FF, 128], [128 * FF, 8], [1, 128]])
                eng.dma_start(out=w1sl[:, u1 * 1024:(u1 + 1) * 1024],
                              in_=srcap).then_inc(s_, p_)
            eng.wait_ge(sems["eV_nrm"], 32)
            for jc in range(4):
                eng.dma_start(out=cc_in_ap[jc * 128:(jc + 1) * 128, :],
                              in_=attn_all[:, jc * T:(jc + 1) * T]).then_inc(sems["qcc"], 16)
            eng.wait_ge(sems["qcc"], 64)
            eng.collective_compute(
                "AllGather", OP.bypass,
                replica_groups=[[0, 1], [2, 3], [4, 5], [6, 7]],
                ins=[cc_in_ap], outs=[cc_out_ap],
            ).then_inc(sems["ecc"], 1)
            for cc in range(NCH):
                s_, p_ = qwo_r.tag(cc)
                eng.dma_start(out=wo_bf[:, cc * 1024:(cc + 1) * 1024],
                              in_=wo_in[cc * 128:(cc + 1) * 128, :]).then_inc(s_, p_)
            def _w1load(eng, u1):
                fc = u1 % 32
                eng.wait_ge(sems["eT_h"], u1 - 3)
                s_, p_ = qwf1_r.tag(u1)
                srcap = bass.AP(w1_in.tensor, fc * 128,
                                [[FF, 128], [128 * FF, 8], [1, 128]])
                eng.dma_start(out=w1sl[:, (u1 % 4) * 1024:(u1 % 4 + 1) * 1024],
                              in_=srcap).then_inc(s_, p_)

            def _w2load(eng, u2):
                jc = u2 % 8
                if u2 >= 2:
                    eng.wait_ge(sems["eT_f"], u2 - 1)
                s_, p_ = qwf2_r.tag(u2)
                srcap = bass.AP(w2_in.tensor, jc * 128,
                                [[C, 128], [128 * C, 32], [1, 128]])
                eng.dma_start(out=w2sl[:, (u2 % 2) * 4096:(u2 % 2 + 1) * 4096],
                              in_=srcap).then_inc(s_, p_)

            for u1 in range(4, 32):
                _w1load(eng, u1)
            for u2 in range(0, 8):
                _w2load(eng, u2)
            for u1 in range(32, 64):
                _w1load(eng, u1)
            for u2 in range(8, 16):
                _w2load(eng, u2)
            qst_r.wait(eng, 15)
            qst_r.wait(eng, 14)

        def p2b_scalar(eng):
            for tb in range(16):
                eng.wait_ge(sems["eT_r"], tb + 1)
                if tb >= 2:
                    eng.wait_ge(sems["eV_rg"], tb - 1)
                eng.activation(e_s[:, (tb % 2) * 8:(tb % 2 + 1) * 8],
                               p_big[:, 2048 + (tb % 2) * 512: 2048 + (tb % 2) * 512 + 8],
                               AF.Exp).then_inc(sems["eS_re"], 1)
            for ph in range(2):
                for fc in range(32):
                    u1 = ph * 32 + fc
                    if ph == 0 and fc == 0:
                        eng.wait_ge(sems["qcc"], 64)
                    eng.wait_ge(sems["eT_h"], u1 + 1)
                    if ph == 1 and fc == 0:
                        eng.wait_ge(sems["eT_f"], 8)
                    eng.activation(h_half[:, fc * 1024:(fc + 1) * 1024],
                                   p_big[:, (fc % 2) * 1024:(fc % 2 + 1) * 1024],
                                   AF.Gelu_apprx_tanh,
                                   bias=b1r[:, fc:fc + 1]).then_inc(sems["eS_ge"], 1)

        def p2b_vector(eng):
            eng.wait_ge(sems["qbb"], 32)
            vch.inc(eng.tensor_tensor(bb[:], bb[:], out_tb[:], OP.add))
            vch.wait(eng)
            eng.nop().then_inc(sems["eV_bb"], 1)
            for tb in range(16):
                eng.wait_ge(sems["eS_re"], tb + 1)
                ec = e_s[:, (tb % 2) * 8:(tb % 2 + 1) * 8]
                mx1 = r_sml[:, 0:1]; meq = r_sml[:, 2:10]; mx2 = r_sml[:, 1:2]
                s12 = r_sml[:, 10:11]; r12 = r_sml[:, 11:12]
                ga = r_sml[:, 12:13]; gb = r_sml[:, 13:14]
                vch.inc(eng.reduce_max(mx1, ec, AX)); vch.wait(eng)
                vch.inc(eng.tensor_scalar(meq, ec, mx1, None, OP.is_equal)); vch.wait(eng)
                vch.inc(eng.tensor_scalar(meq, meq, -1.0, 1.0, OP.mult, OP.add)); vch.wait(eng)
                vch.inc(eng.tensor_tensor(meq, ec, meq, OP.mult)); vch.wait(eng)
                vch.inc(eng.reduce_max(mx2, meq, AX)); vch.wait(eng)
                vch.inc(eng.tensor_tensor(s12, mx1, mx2, OP.add)); vch.wait(eng)
                vch.inc(eng.reciprocal(r12, s12)); vch.wait(eng)
                vch.inc(eng.tensor_tensor(ga, mx1, r12, OP.mult)); vch.wait(eng)
                vch.inc(eng.tensor_tensor(gb, mx2, r12, OP.mult)); vch.wait(eng)
                vch.inc(eng.tensor_tensor(g_s[:, tb:tb + 1], ga, gb, OP.add))
                vch.wait(eng)
                eng.nop().then_inc(sems["eV_rg"], 1)
            eng.wait_ge(sems["eV_bb"], 1)
            for tb in range(16):
                eng.wait_ge(sems["eT_y"], tb + 1)
                eng.tensor_tensor(y_att[:, tb * 1024:(tb + 1) * 1024],
                                  p_big[:, (tb % 2) * 1024:(tb % 2 + 1) * 1024],
                                  bb[:], OP.add).then_inc(sems["eV_ye"], 1)
            for ph in range(2):

                def _w2evac(eng, jc2):
                    u2c = ph * 8 + jc2
                    eng.wait_ge(sems["eT_f"], u2c + 1)
                    eng.tensor_copy(y_fm_s[:, (jc2 % 2) * 1024:(jc2 % 2 + 1) * 1024],
                                    p_big[:, (jc2 % 2) * 1024:(jc2 % 2 + 1) * 1024]
                                    ).then_inc(sems["eV_fe"], 1)
                    eng.wait_ge(sems["eT_ft"], u2c + 1)
                    if jc2 == 0:
                        eng.wait_ge(sems["eV_rg"], 16)
                    ptp = p_tp3 if jc2 % 2 == 0 else p_tp4
                    for tbi in range(8):
                        i = eng.tensor_scalar_mul(
                            bass.AP(y_ffn_tm[:].tensor, tbi * 1024 + jc2 * 128,
                                    [[8 * 1024, 128], [1, 128]]),
                            ptp[:, tbi * 128:(tbi + 1) * 128],
                            g_s[:, ph * 8 + tbi: ph * 8 + tbi + 1])
                    i.then_inc(sems["eV_ft"], 1)
                    vch.note(sems["eV_ft"], u2c + 1)

                for jc in range(8):
                    _w2evac(eng, jc)
                for tbl in range(8):
                    u3 = ph * 8 + tbl
                    eng.wait_ge(sems["eV_ye"], u3 + 1)
                    qxo_r.wait(eng, u3)
                    if u3 >= 1:
                        qst_r.wait(eng, u3 - 1)
                    vch.wait(eng)
                    vch.inc(eng.tensor_tensor(
                        out_tb[:], y_ffn_tm[:, tbl * 1024:(tbl + 1) * 1024],
                        x_tb[:], OP.add))
                    vch.wait(eng)
                    eng.tensor_tensor(out_tb[:], out_tb[:],
                                      y_att[:, u3 * 1024:(u3 + 1) * 1024],
                                      OP.add).then_inc(sems["eV_out"], 1)

        def p2b_tensor(eng):
            eng.wait_ge(sems["eS_exp"], NSTEP)
            eng.wait_ge(sems["eV_av"], 32)
            eng.wait_ge(sems["eV_dn"], 32)
            eng.wait_ge(sems["qxn"], 128)
            for tb in range(16):
                if tb >= 2:
                    eng.wait_ge(sems["eS_re"], tb - 1)
                for cc in range(NCH):
                    i = eng.matmul(p_big[:, 2048 + (tb % 2) * 512: 2048 + (tb % 2) * 512 + 8],
                                   xn_own[:, cc * 2048 + tb * 128: cc * 2048 + (tb + 1) * 128],
                                   rw_bf[:, cc * 8:(cc + 1) * 8],
                                   start=(cc == 0), stop=(cc == NCH - 1),
                                   skip_group_check=True)
                i.then_inc(sems["eT_r"], 1)
            def _w1(eng, ph):
                for fc in range(32):
                    u1 = ph * 32 + fc
                    qwf1_r.wait(eng, u1)
                    if u1 >= 2:
                        eng.wait_ge(sems["eS_ge"], u1 - 1)
                    for cc in range(NCH):
                        for tt in range(2):
                            i = eng.matmul(
                                p_big[:, (fc % 2) * 1024 + tt * 512:(fc % 2) * 1024 + (tt + 1) * 512],
                                w1sl[:, (u1 % 4) * 1024 + cc * 128:(u1 % 4) * 1024 + (cc + 1) * 128],
                                xn_own[:, cc * 2048 + ph * 1024 + tt * 512: cc * 2048 + ph * 1024 + (tt + 1) * 512],
                                start=(cc == 0), stop=(cc == NCH - 1), skip_group_check=True)
                    i.then_inc(sems["eT_h"], 1)

            _w1(eng, 0)
            qwo_r.wait(eng, NCH - 1)
            qwo_r.wait(eng, NCH - 2)
            eng.wait_ge(sems["eS_ge"], 32)
            for tb in range(16):
                qat_r.wait(eng, tb)
                if tb >= 2:
                    eng.wait_ge(sems["eV_ye"], tb - 1)
                for gc in range(8):
                    for sl in range(2):
                        i = eng.matmul(
                            p_big[:, (tb % 2) * 1024 + sl * 512:(tb % 2) * 1024 + (sl + 1) * 512],
                            attn_tb[:, (tb % 2) * 1024 + gc * 128:(tb % 2) * 1024 + (gc + 1) * 128],
                            wo_bf[:, gc * 1024 + sl * 512: gc * 1024 + (sl + 1) * 512],
                            start=(gc == 0), stop=(gc == NCH - 1), skip_group_check=True)
                i.then_inc(sems["eT_y"], 1)
            eng.wait_ge(sems["eV_ye"], 16)
            for ph in range(2):
                if ph == 1:
                    eng.wait_ge(sems["eV_fe"], 8)
                    _w1(eng, 1)
                eng.wait_ge(sems["eS_ge"], (ph + 1) * 32)
                for jc in range(8):
                    u2 = ph * 8 + jc
                    qwf2_r.wait(eng, u2)
                    if u2 >= 2:
                        eng.wait_ge(sems["eV_fe"], u2 - 1)
                    for fc in range(32):
                        for tt in range(2):
                            i = eng.matmul(
                                p_big[:, (jc % 2) * 1024 + tt * 512:(jc % 2) * 1024 + (tt + 1) * 512],
                                w2sl[:, (u2 % 2) * 4096 + fc * 128:(u2 % 2) * 4096 + (fc + 1) * 128],
                                h_half[:, fc * 1024 + tt * 512: fc * 1024 + (tt + 1) * 512],
                                start=(fc == 0), stop=(fc == 31), skip_group_check=True)
                    i.then_inc(sems["eT_f"], 1)
                    eng.wait_ge(sems["eV_fe"], u2 + 1)
                    if u2 >= 2:
                        eng.wait_ge(sems["eV_ft"], u2 - 1)
                    ptp = p_tp3 if jc % 2 == 0 else p_tp4
                    for tbi in range(8):
                        i = eng.transpose(ptp[:, tbi * 128:(tbi + 1) * 128],
                                          y_fm_s[:, (jc % 2) * 1024 + tbi * 128:(jc % 2) * 1024 + (tbi + 1) * 128],
                                          ident[:])
                    i.then_inc(sems["eT_ft"], 1)


        with nc.Block() as blk:
            full = (debug_phase == 0 or debug_phase >= 3)

            @blk.sync
            def _(eng):
                p1_sync(eng)
                eng.nop().then_inc(sems["bar1"], 1)
                if full:
                    p2b_sync(eng)

            @blk.gpsimd
            def _(eng):
                p1_gpsimd(eng)
                eng.nop().then_inc(sems["bar1"], 1)
                if full:
                    p2a_gpsimd(eng)
                    p2b_gpsimd(eng)
                if debug_phase == 2:
                    p2a_gpsimd(eng)
                    eng.wait_ge(sems["ecc"], 1)
                    eng.dma_start(out=dbg["att"], in_=attn_all[:]).then_inc(sems["qst"], 16)
                    eng.dma_start(out=dbg["ccout"], in_=cc_out_ap).then_inc(sems["qst"], 16)
                    eng.wait_ge(sems["qst"], 32)
                if debug_phase == 1:
                    eng.wait_ge(sems["eV_fin"], 8 * NG)
                    eng.wait_ge(sems["eV_vv"], 4 * NG)
                    for jc in range(4):
                        eng.dma_start(out=dbg["k"][:, jc * T:(jc + 1) * T],
                                      in_=k_fm[:, jc * T:(jc + 1) * T]).then_inc(sems["qst"], 16)
                        eng.dma_start(out=dbg["q"][:, jc * T:(jc + 1) * T],
                                      in_=q_fm[:, jc * T:(jc + 1) * T]).then_inc(sems["qst"], 16)
                    for i8 in range(8):
                        eng.dma_start(out=dbg["v"][:, i8 * 2048:(i8 + 1) * 2048],
                                      in_=v_tm[:, i8 * 2048:(i8 + 1) * 2048]).then_inc(sems["qst"], 16)
                    eng.wait_ge(sems["qst"], 16 * 16)
                    qpc2_r.wait(eng, W1U + W2U - 1)
                    qpc2_r.wait(eng, W1U + W2U - 2)

            @blk.scalar
            def _(eng):
                p1_scalar(eng)
                eng.nop().then_inc(sems["bar1"], 1)
                if debug_phase >= 2 or full:
                    p2a_scalar(eng)
                if full:
                    p2b_scalar(eng)

            @blk.vector
            def _(eng):
                p1_vector(eng)
                eng.nop().then_inc(sems["bar1"], 1)
                if debug_phase >= 2 or full:
                    p2a_vector(eng)
                if full:
                    p2b_vector(eng)

            @blk.tensor
            def _(eng):
                p1_tensor(eng)
                eng.nop().then_inc(sems["bar1"], 1)
                if debug_phase >= 2 or full:
                    p2a_tensor(eng)
                if full:
                    p2b_tensor(eng)

        p3s.close()

    return nc, dbg



def _host_constants():
    half = D // 2
    inv_freq = (1.0 / 10000.0 ** (np.arange(half, dtype=np.float32) / half)).astype(np.float32)
    t = np.arange(T, dtype=np.float32)
    freqs = t[:, None] * inv_freq[None, :]
    cos = np.repeat(np.cos(freqs), 2, axis=-1)
    sin = np.repeat(np.sin(freqs), 2, axis=-1)
    cos_fm = np.concatenate([cos.T, cos.T], axis=0).astype(np.float32)
    sin_fm = np.concatenate([sin.T, sin.T], axis=0).astype(np.float32)
    ident = np.eye(128, dtype=np.float32)
    P = np.zeros((128, 128), dtype=np.float32)
    for i in range(64):
        P[2 * i, 2 * i + 1] = -1.0
        P[2 * i + 1, 2 * i] = 1.0
    permT = P.T.copy()
    masks = np.zeros((128, 4 * 1024), dtype=np.float32)
    ii = np.arange(128)[:, None]
    jj = np.arange(512)[None, :]
    for r in range(4):
        m = (ii + 128 * r <= jj).astype(np.float32)
        masks[:, r * 1024:r * 1024 + 512] = m
        masks[:, r * 1024 + 512:(r + 1) * 1024] = m
    return cos_fm, sin_fm, ident, permT, masks


def make_inputs(full):
    x = np.asarray(full["L0_attn_x"], dtype=np.float32)
    Wqkv = np.asarray(full["L0_attn_Wqkv"], dtype=np.float32)
    Wo = np.asarray(full["L0_attn_Wo"], dtype=np.float32)
    bo = np.asarray(full["L0_attn_bo"], dtype=np.float32)
    rw = np.asarray(full["L0_ffn_router_w"], dtype=np.float32)
    w1 = np.asarray(full["L0_ffn_w1"], dtype=np.float32)
    b1 = np.asarray(full["L0_ffn_b1"], dtype=np.float32)
    w2 = np.asarray(full["L0_ffn_w2"], dtype=np.float32)
    b2 = np.asarray(full["L0_ffn_b2"], dtype=np.float32)
    cos_fm, sin_fm, ident, permT, masks = _host_constants()
    b1r = np.ascontiguousarray(b1.reshape(32, 128).T)
    in_maps = []
    for c in range(8):
        b = c // 2
        hh = c % 2
        cols = np.concatenate([np.arange(512 * hh, 512 * hh + 512),
                               1024 + np.arange(512 * hh, 512 * hh + 512),
                               2048 + np.arange(512 * hh, 512 * hh + 512)])
        in_maps.append({
            "x_in": np.ascontiguousarray(x[b]),
            "x_own": np.ascontiguousarray(x[b, hh * 2048:(hh + 1) * 2048]),
            "wqkv": np.ascontiguousarray(Wqkv[:, cols]),
            "wo_in": Wo,
            "w1_in": w1,
            "w2_in": w2,
            "rw_in": rw,
            "b1r_in": b1r,
            "bo_in": bo.reshape(1, C),
            "b2_in": b2.reshape(1, C),
            "cos_in": cos_fm,
            "sin_in": sin_fm,
            "ident_in": ident,
            "permT_in": permT,
            "masks_in": masks.astype(_bf16),
        })
    return in_maps


_NC_CACHE = {}


def kernel(**inputs):
    in_maps = make_inputs(inputs)
    if "nc" not in _NC_CACHE:
        _NC_CACHE["nc"] = build_program(debug_phase=0)[0]
    nc = _NC_CACHE["nc"]
    res = run_bass_kernel_spmd(nc, in_maps, core_ids=list(range(8)))
    out = np.empty((B, T, C), dtype=np.float32)
    for c in range(8):
        b, hh = c // 2, c % 2
        out[b, hh * 2048:(hh + 1) * 2048, :] = res.results[c]["y_out"]
    return out


# version 20 (final)
# speedup vs baseline: 1.2120x; 1.0297x over previous
import contextlib
import numpy as np
import ml_dtypes
_bf16 = ml_dtypes.bfloat16
import concourse.bass as bass
import concourse.mybir as mybir
from concourse.bass_utils import run_bass_kernel_spmd

F32 = mybir.dt.float32
BF16 = mybir.dt.bfloat16
AF = mybir.ActivationFunctionType
OP = mybir.AluOpType
AX = mybir.AxisListType.X

B, T, C, H, D, FF = 4, 4096, 1024, 16, 64, 4096
NG = 8
NB = 32
NCH = 8
SCALE = float(D) ** -0.5

DEBUG = False


def build_program(debug_phase=0):
    nc = bass.Bass("TRN2", target_bir_lowering=False)
    x_in = nc.dram_tensor("x_in", [T, C], F32, kind="ExternalInput").ap()
    x_own = nc.dram_tensor("x_own", [2048, C], F32, kind="ExternalInput").ap()
    wqkv = nc.dram_tensor("wqkv", [C, 1536], F32, kind="ExternalInput").ap()
    wo_in = nc.dram_tensor("wo_in", [C, C], F32, kind="ExternalInput").ap()
    w1_in = nc.dram_tensor("w1_in", [C, FF], F32, kind="ExternalInput").ap()
    w2_in = nc.dram_tensor("w2_in", [FF, C], F32, kind="ExternalInput").ap()
    rw_in = nc.dram_tensor("rw_in", [C, 8], F32, kind="ExternalInput").ap()
    b1r_in = nc.dram_tensor("b1r_in", [128, 32], F32, kind="ExternalInput").ap()
    bo_in = nc.dram_tensor("bo_in", [1, C], F32, kind="ExternalInput").ap()
    b2_in = nc.dram_tensor("b2_in", [1, C], F32, kind="ExternalInput").ap()
    cos_in = nc.dram_tensor("cos_in", [128, T], F32, kind="ExternalInput").ap()
    sin_in = nc.dram_tensor("sin_in", [128, T], F32, kind="ExternalInput").ap()
    ident_in = nc.dram_tensor("ident_in", [128, 128], F32, kind="ExternalInput").ap()
    permT_in = nc.dram_tensor("permT_in", [128, 128], F32, kind="ExternalInput").ap()
    masks_in = nc.dram_tensor("masks_in", [128, 4 * 1024], BF16, kind="ExternalInput").ap()
    y_out = nc.dram_tensor("y_out", [2048, C], F32, kind="ExternalOutput").ap()

    xn_spill = nc.dram_tensor("xn_spill", [1024, 4096], BF16, kind="Internal").ap()
    cc_in = nc.dram_tensor("cc_in", [512, 4096], BF16, kind="Internal")
    cc_out = nc.dram_tensor("cc_out", [1024, 4096], BF16, kind="Internal")
    cc_in_ap = cc_in.ap()
    cc_out_ap = cc_out.ap()

    dbg = {}
    if debug_phase >= 1:
        dbg["k"] = nc.dram_tensor("dbg_k", [128, 4 * T], BF16, kind="ExternalOutput").ap()
        dbg["q"] = nc.dram_tensor("dbg_q", [128, 4 * T], BF16, kind="ExternalOutput").ap()
        dbg["v"] = nc.dram_tensor("dbg_v", [128, 32 * 512], BF16, kind="ExternalOutput").ap()
    if debug_phase >= 2:
        dbg["att"] = nc.dram_tensor("dbg_att", [128, 4 * T], BF16, kind="ExternalOutput").ap()
        dbg["ccout"] = nc.dram_tensor("dbg_ccout", [1024, 4096], BF16, kind="ExternalOutput").ap()

    est = contextlib.ExitStack()
    with est:
        ident = est.enter_context(nc.sbuf_tensor("ident", [128, 128], BF16))
        permT = est.enter_context(nc.sbuf_tensor("permT", [128, 128], BF16))
        ones128 = est.enter_context(nc.sbuf_tensor("ones128", [128, 1], BF16))
        b1r = est.enter_context(nc.sbuf_tensor("b1r", [128, 32], F32))
        rw_fm = est.enter_context(nc.sbuf_tensor("rw_fm", [128, 8 * 8], F32))
        bb = est.enter_context(nc.sbuf_tensor("bb", [128, C], F32))
        g_s = est.enter_context(nc.sbuf_tensor("g_s", [128, 16], F32))
        eps_s = est.enter_context(nc.sbuf_tensor("eps_s", [128, 1], F32))
        rw_bf = est.enter_context(nc.sbuf_tensor("rw_bf", [128, 8 * 8], BF16))

        sems = {}
        def sem(name):
            sems[name] = est.enter_context(nc.semaphore(name))
            return sems[name]
        class DmaRing:
            def __init__(self, name, per=16, ways=2):
                self.ways = ways
                self.s = [est.enter_context(nc.semaphore(name + str(k)))
                          for k in range(ways)]
                self.per = per
            def tag(self, i):
                return self.s[i % self.ways], self.per
            def tag6(self, i):
                return self.s[i % self.ways], self.per
            def wait(self, eng, i):
                eng.wait_ge(self.s[i % self.ways], self.per * (i // self.ways + 1))

        class Chain:
            def __init__(self, name):
                self.h = est.enter_context(nc.semaphore(name))
                self.n = 0
                self.last = None
            def inc(self, instr):
                instr.then_inc(self.h, 1)
                self.n += 1
                self.last = (self.h, self.n)
                return instr
            def note(self, semh, val):
                self.last = (semh, val)
            def wait(self, eng):
                if self.last is not None:
                    eng.wait_ge(self.last[0], self.last[1])

        vch = Chain("vch")
        sch = Chain("sch")
        tch = Chain("tch")
        gch = Chain("gch")
        qx_r = DmaRing("qx", 16, ways=4)
        qat_r = DmaRing("qat_r", 16)
        qwo_r = DmaRing("qwo_r", 16)
        qwf1_r = DmaRing("qwf1_r", 16, ways=4)
        qwf2_r = DmaRing("qwf2_r", 16)
        qxo_r = DmaRing("qxo_r", 16)
        qst_r = DmaRing("qst_r", 16)
        qdn_r = DmaRing("qdn_r", 16)
        qbc_r = DmaRing("qbc_r", 32)
        qmk_r = DmaRing("qmk_r", 16)
        qxs_r = DmaRing("qxs_r", 128)
        qw_r = DmaRing("qw", 16)
        qtab_r = DmaRing("qtab", 32)
        qpc1_r = DmaRing("qpc1", 16)
        qpc2_r = DmaRing("qpc2", 16)
        for n in ["qc",
                  "eS_sq", "eV_cf1", "eS_rstd", "eV_cf2", "eS_ln",
                  "eT_tp", "eV_xg", "eT_mm", "eV_raw", "eT_rot", "eV_e1", "eV_fin",
                  "eT_v", "eV_vv", "eV_wc", "eV_pc", "bar1",
                  "eT_sc", "eS_exp", "eV_msk", "eT_pv", "eV_av", "eV_dn", "eV_rc", "eV_nrm",
                  "qcc", "ecc", "bar2aT", "qxn", "qbb", "eV_bb", "eV_mk",
                  "eS_sq2", "eV_cf12", "eS_rstd2", "eV_cf22", "eS_ln2",
                  "eT_tp2", "eV_xg2", "eV_woc", "eT_y", "eV_ye",
                  "eT_r", "eS_re", "eV_rg",
                  "eT_h", "eS_ge", "eT_f", "eV_w1c", "eV_w2c",
                  "eV_fe", "eT_ft", "eV_ft", "eV_out", "qst",
                  "eS_ac", "eV_va", "eS_rs", "eV_st"]:
            sem(n)

        pav = contextlib.ExitStack()
        k_fm = pav.enter_context(nc.sbuf_tensor("k_fm", [128, 4 * T], BF16))
        q_fm = pav.enter_context(nc.sbuf_tensor("q_fm", [128, 4 * T], BF16))
        v_tm = pav.enter_context(nc.sbuf_tensor("v_tm", [128, 32 * 512], BF16))
        p1s = contextlib.ExitStack()
        p_tp = p1s.enter_context(nc.psum_tensor("p_tp", [128, 1024], BF16))
        p_tp2 = p1s.enter_context(nc.psum_tensor("p_tp2", [128, 1024], BF16))
        p_mm = p1s.enter_context(nc.psum_tensor("p_mm", [128, 2 * 512], F32))
        p_rot = p1s.enter_context(nc.psum_tensor("p_rot", [128, 2 * 512], F32))
        p_v = p1s.enter_context(nc.psum_tensor("p_v", [128, 2 * 512], F32))
        x_blk = p1s.enter_context(nc.sbuf_tensor("x_blk", [128, 4 * C], F32))
        xn_tm = p1s.enter_context(nc.sbuf_tensor("xn_tm", [128, 4 * C], BF16))
        xn_grp = p1s.enter_context(nc.sbuf_tensor("xn_grp", [128, 2 * 8 * 512], BF16))
        wkv_bf = p1s.enter_context(nc.sbuf_tensor("wkv_bf", [128, 8 * 1536], BF16))
        tab_c = p1s.enter_context(nc.sbuf_tensor("tab_c", [128, 2 * 512], F32))
        tab_s = p1s.enter_context(nc.sbuf_tensor("tab_s", [128, 2 * 512], F32))
        kraw = p1s.enter_context(nc.sbuf_tensor("kraw", [128, 2 * 512], BF16))
        tmp1 = p1s.enter_context(nc.sbuf_tensor("tmp1", [128, 2 * 512], BF16))
        tmp2 = p1s.enter_context(nc.sbuf_tensor("tmp2", [128, 2 * 512], BF16))
        sum_s = p1s.enter_context(nc.sbuf_tensor("sum_s", [128, 4], F32))
        sq_s = p1s.enter_context(nc.sbuf_tensor("sq_s", [128, 4], F32))
        m_s = p1s.enter_context(nc.sbuf_tensor("m_s", [128, 4], F32))
        var_s = p1s.enter_context(nc.sbuf_tensor("var_s", [128, 4], F32))
        rstd_s = p1s.enter_context(nc.sbuf_tensor("rstd_s", [128, 4], F32))
        bneg_s = p1s.enter_context(nc.sbuf_tensor("bneg_s", [128, 4], F32))
        scr = p1s.enter_context(nc.sbuf_tensor("scr", [128, C], BF16))
        sq2_s = p1s.enter_context(nc.sbuf_tensor("sq2_s", [128, 4], F32))

        W1U = 16
        W2U = 32


        def p1_sync(eng):
            for Bi in range(NB):
                if Bi >= 4:
                    eng.wait_ge(sems["eS_ln"], Bi - 3)
                s_, p_ = qx_r.tag(Bi)
                eng.dma_start(out=x_blk[:, (Bi % 4) * C:(Bi % 4 + 1) * C],
                              in_=x_in[Bi * 128:(Bi + 1) * 128, :]).then_inc(s_, p_)

        def p1_gpsimd(eng):
            eng.dma_start(out=ident[:], in_=ident_in).then_inc(sems["qc"], 16)
            eng.dma_start(out=permT[:], in_=permT_in).then_inc(sems["qc"], 16)
            eng.dma_start(out=b1r[:], in_=b1r_in).then_inc(sems["qc"], 16)
            for cc in range(NCH):
                eng.dma_start(out=rw_fm[:, cc * 8:(cc + 1) * 8],
                              in_=rw_in[cc * 128:(cc + 1) * 128, :]).then_inc(sems["qc"], 16)
            for cc in range(NCH):
                s_, p_ = qw_r.tag(cc)
                eng.dma_start(out=wkv_bf[:, cc * 1536:(cc + 1) * 1536],
                              in_=wqkv[cc * 128:(cc + 1) * 128, :]).then_inc(s_, p_)
            for g in range(NG):
                if g >= 2:
                    eng.wait_ge(sems["eV_fin"], 8 * (g - 1))
                s_, _ = qtab_r.tag(g)
                eng.dma_start(out=tab_c[:, (g % 2) * 512:(g % 2 + 1) * 512],
                              in_=cos_in[:, g * 512:(g + 1) * 512]).then_inc(s_, 16)
                eng.dma_start(out=tab_s[:, (g % 2) * 512:(g % 2 + 1) * 512],
                              in_=sin_in[:, g * 512:(g + 1) * 512]).then_inc(s_, 16)
                if g >= 1:
                    eng.wait_ge(sems["eV_xg"], 4 * g)
                    sx_, px_ = qxs_r.tag(g - 1)
                    gb = ((g - 1) % 2) * 4096
                    for cc in range(NCH):
                        eng.dma_start(out=xn_spill[cc * 128:(cc + 1) * 128,
                                                   (g - 1) * 512: g * 512],
                                      in_=xn_grp[:, gb + cc * 512: gb + (cc + 1) * 512]).then_inc(sx_, 16)
            eng.wait_ge(sems["eV_xg"], 4 * NG)
            sx_, px_ = qxs_r.tag(NG - 1)
            gb = ((NG - 1) % 2) * 4096
            for cc in range(NCH):
                eng.dma_start(out=xn_spill[cc * 128:(cc + 1) * 128,
                                           (NG - 1) * 512: NG * 512],
                              in_=xn_grp[:, gb + cc * 512: gb + (cc + 1) * 512]).then_inc(sx_, 16)
            qxs_r.wait(eng, NG - 1)
            qxs_r.wait(eng, NG - 2)
        def p1_scalar(eng):
            eng.wait_ge(sems["eV_cf2"], 1)
            for g in range(NG):
                for b in range(4):
                    Bi = 4 * g + b
                    qx_r.wait(eng, Bi)
                    if Bi >= 4:
                        eng.wait_ge(sems["eT_tp"], Bi - 3)
                    eng.activation(scr[:], x_blk[:, b * C:(b + 1) * C], AF.Square,
                                   accum_out=sq_s[:, b:b + 1])
                    i = eng.activation(xn_tm[:, b * C:(b + 1) * C],
                                       x_blk[:, b * C:(b + 1) * C], AF.Copy,
                                       accum_out=sum_s[:, b:b + 1])
                i.then_inc(sems["eS_ac"], 1)
                eng.wait_ge(sems["eV_va"], g + 1)
                sch.inc(eng.activation(var_s[:], var_s[:], AF.Ln, bias=eps_s[:, 0:1]))
                sch.wait(eng)
                eng.activation(rstd_s[:], var_s[:], AF.Exp,
                               scale=-0.5).then_inc(sems["eS_rs"], 1)
                eng.wait_ge(sems["eV_st"], g + 1)
                for b in range(4):
                    eng.activation(xn_tm[:, b * C:(b + 1) * C], x_blk[:, b * C:(b + 1) * C],
                                   AF.Identity, bias=bneg_s[:, b:b + 1],
                                   scale=rstd_s[:, b:b + 1]).then_inc(sems["eS_ln"], 1)

        def p1_vector(eng):
            eng.memset(eps_s[:], 1e-5)
            eng.memset(ones128[:], 1.0).then_inc(sems["eV_cf2"], 1)
            eng.wait_ge(sems["qc"], 176)
            eng.tensor_copy(rw_bf[:], rw_fm[:])

            def _stats(g):
                eng.wait_ge(sems["eS_ac"], g + 1)
                vch.inc(eng.tensor_scalar(m_s[:], sum_s[:], 1.0 / C, None, OP.mult))
                vch.wait(eng)
                vch.inc(eng.tensor_tensor(var_s[:], m_s[:], m_s[:], OP.mult))
                vch.wait(eng)
                vch.inc(eng.tensor_scalar(sq2_s[:], sq_s[:], 1.0 / C, None, OP.mult))
                vch.wait(eng)
                eng.tensor_tensor(var_s[:], sq2_s[:], var_s[:], OP.subtract
                                  ).then_inc(sems["eV_va"], 1)
                vch.note(sems["eV_va"], g + 1)
                eng.wait_ge(sems["eS_rs"], g + 1)
                if g >= 1:
                    eng.wait_ge(sems["eS_ln"], 4 * g)
                vch.inc(eng.tensor_tensor(bneg_s[:], m_s[:], rstd_s[:], OP.mult))
                vch.wait(eng)
                eng.tensor_scalar(bneg_s[:], bneg_s[:], -1.0, None,
                                  OP.mult).then_inc(sems["eV_st"], 1)

            def _evacs(g):
                for b in range(4):
                    Bi = 4 * g + b
                    eng.wait_ge(sems["eT_tp"], Bi + 1)
                    if b == 0 and g >= 2:
                        qxs_r.wait(eng, g - 2)
                    ptp = p_tp if Bi % 2 == 0 else p_tp2
                    srcv = ptp[:].rearrange("p (cc t) -> p cc t", cc=8)
                    dstv = bass.AP(xn_grp[:].tensor, (g % 2) * 4096 + b * 128,
                                   [[8192, 128], [512, 8], [1, 128]])
                    eng.tensor_copy(dstv, srcv).then_inc(sems["eV_xg"], 1)

            def _rope(g):
                for r in range(8):
                    rg = 8 * g + r
                    pr = rg % 2
                    is_k = r < 4
                    jc = r % 4
                    eng.wait_ge(sems["eT_mm"], rg + 1)
                    eng.tensor_copy(kraw[:, pr * 512:(pr + 1) * 512],
                                    p_mm[:, pr * 512:(pr + 1) * 512]).then_inc(sems["eV_raw"], 1)
                    qtab_r.wait(eng, g)
                    eng.tensor_tensor(tmp1[:, pr * 512:(pr + 1) * 512],
                                      p_mm[:, pr * 512:(pr + 1) * 512],
                                      tab_c[:, (g % 2) * 512:(g % 2 + 1) * 512],
                                      OP.mult).then_inc(sems["eV_e1"], 1)
                    vch.note(sems["eV_e1"], rg + 1)
                    eng.wait_ge(sems["eT_rot"], rg + 1)
                    vch.inc(eng.tensor_tensor(tmp2[:, pr * 512:(pr + 1) * 512],
                                              p_rot[:, pr * 512:(pr + 1) * 512],
                                              tab_s[:, (g % 2) * 512:(g % 2 + 1) * 512], OP.mult))
                    vch.wait(eng)
                    dst = (k_fm if is_k else q_fm)[:, jc * T + g * 512: jc * T + (g + 1) * 512]
                    eng.tensor_tensor(dst, tmp1[:, pr * 512:(pr + 1) * 512],
                                      tmp2[:, pr * 512:(pr + 1) * 512], OP.add
                                      ).then_inc(sems["eV_fin"], 1)
                for tb in range(4):
                    vb = 4 * g + tb
                    eng.wait_ge(sems["eT_v"], vb + 1)
                    eng.tensor_copy(v_tm[:, vb * 512:(vb + 1) * 512],
                                    p_v[:, (vb % 2) * 512:(vb % 2 + 1) * 512]
                                    ).then_inc(sems["eV_vv"], 1)

            for g in range(NG):
                _stats(g)
                _evacs(g)
                if g >= 1:
                    _rope(g - 1)
            _rope(NG - 1)

        def p1_tensor(eng):
            eng.wait_ge(sems["qc"], 176)

            def _tp(g):
                for b in range(4):
                    Bi = 4 * g + b
                    pb4 = Bi % 4
                    ptp = p_tp if Bi % 2 == 0 else p_tp2
                    eng.wait_ge(sems["eS_ln"], Bi + 1)
                    if Bi >= 2:
                        eng.wait_ge(sems["eV_xg"], Bi - 1)
                    for j in range(8):
                        i = eng.transpose(ptp[:, j * 128:(j + 1) * 128],
                                          xn_tm[:, pb4 * C + j * 128: pb4 * C + (j + 1) * 128],
                                          ident[:])
                    i.then_inc(sems["eT_tp"], 1)

            def _kvq(g):
                base = (g % 2) * 4096
                for r in range(8):
                    rg = 8 * g + r
                    pr = rg % 2
                    is_k = r < 4
                    jc = r % 4
                    off = (512 + jc * 128) if is_k else (jc * 128)
                    if rg >= 2:
                        eng.wait_ge(sems["eV_e1"], rg - 1)
                    for cc in range(NCH):
                        i = eng.matmul(p_mm[:, pr * 512:(pr + 1) * 512],
                                       wkv_bf[:, cc * 1536 + off: cc * 1536 + off + 128],
                                       xn_grp[:, base + cc * 512: base + (cc + 1) * 512],
                                       start=(cc == 0), stop=(cc == NCH - 1))
                    i.then_inc(sems["eT_mm"], 1)
                    eng.wait_ge(sems["eV_raw"], rg + 1)
                    if rg >= 2:
                        eng.wait_ge(sems["eV_fin"], rg - 1)
                    eng.matmul(p_rot[:, pr * 512:(pr + 1) * 512], permT[:],
                               kraw[:, pr * 512:(pr + 1) * 512],
                               start=True, stop=True).then_inc(sems["eT_rot"], 1)
                for tb in range(4):
                    vb = 4 * g + tb
                    pv_ = vb % 2
                    if vb >= 2:
                        eng.wait_ge(sems["eV_vv"], vb - 1)
                    for cc in range(NCH):
                        i = eng.matmul(p_v[:, pv_ * 512:(pv_ + 1) * 512],
                                       xn_grp[:, base + cc * 512 + tb * 128: base + cc * 512 + (tb + 1) * 128],
                                       wkv_bf[:, cc * 1536 + 1024: cc * 1536 + 1536],
                                       start=(cc == 0), stop=(cc == NCH - 1))
                    i.then_inc(sems["eT_v"], 1)

            for g in range(NG):
                _tp(g)
                if g == 0:
                    qw_r.wait(eng, NCH - 1)
                    qw_r.wait(eng, NCH - 2)
                if g >= 1:
                    eng.wait_ge(sems["eV_xg"], 4 * g)
                    _kvq(g - 1)
            eng.wait_ge(sems["eV_xg"], 4 * NG)
            _kvq(NG - 1)

        p1s.close()
        p2s = contextlib.ExitStack()
        psc = p2s.enter_context(nc.psum_tensor("psc", [128, 3072], F32))
        ppv = p2s.enter_context(nc.psum_tensor("ppv", [128, 512], F32))
        pdn = p2s.enter_context(nc.psum_tensor("pdn", [128, 512], F32))
        pt_buf = p2s.enter_context(nc.sbuf_tensor("pt_buf", [128, 8 * 1024], BF16))
        attn_all = p2s.enter_context(nc.sbuf_tensor("attn_all", [128, 4 * T], BF16))
        dnsp = p2s.enter_context(nc.sbuf_tensor("dnsp", [128, 1024], F32))
        dn_buf = p2s.enter_context(nc.sbuf_tensor("dn_buf", [2, 1024], F32))
        dn_rec = p2s.enter_context(nc.sbuf_tensor("dn_rec", [2, 1024], F32))
        masks4 = p2s.enter_context(nc.sbuf_tensor("masks4", [128, 4 * 1024], BF16))
        bcast_t = p2s.enter_context(nc.sbuf_tensor("bcast_t", [128, 2 * 512], F32))

        STEPS = []
        for qt_ in range(8):
            for pr_ in range(4):
                for kb_ in range(4 * (qt_ + 1)):
                    STEPS.append((qt_, pr_, kb_))
        NSTEP = len(STEPS)
        GEND = {}
        DCTR = {}
        _d = 0
        for si_, st_ in enumerate(STEPS):
            qt_, pr_, kb_ = st_
            GEND[(qt_, pr_)] = si_
            if kb_ >= 4 * (qt_ + 1) - 4:
                _d += 1
                DCTR[si_] = _d

        def _sp_ap(handle, row_off, col_off, rowlen):
            return bass.AP(handle[:].tensor, row_off * rowlen + col_off,
                           [[64 * rowlen, 2], [1, 512]])

        def _pvdn(eng, si):
            qt, pr, kb = STEPS[si]
            nch = 4 * (qt + 1)
            idx2 = 4 * qt + pr
            eng.wait_ge(sems["eS_exp"], si + 1)
            if si in DCTR:
                eng.wait_ge(sems["eV_msk"], DCTR[si])
            if kb == 0 and idx2 >= 1:
                eng.wait_ge(sems["eV_av"], idx2)
                eng.wait_ge(sems["eV_dn"], idx2)
            sp4 = si % 8
            pts = pt_buf[:, sp4 * 1024:(sp4 + 1) * 1024]
            st_, sp_f = (kb == 0), (kb == nch - 1)
            h0 = 2 * pr
            eng.matmul(ppv[0:64, :],
                       v_tm[:, kb * 512 + h0 * 64: kb * 512 + (h0 + 1) * 64],
                       pts[:, 0:512], start=st_, stop=sp_f,
                       tile_position=(0, 0), skip_group_check=True)
            eng.matmul(ppv[64:128, :],
                       v_tm[:, kb * 512 + (h0 + 1) * 64: kb * 512 + (h0 + 2) * 64],
                       pts[:, 512:1024], start=st_, stop=sp_f,
                       tile_position=(0, 64), skip_group_check=True)
            eng.matmul(pdn[0:1, :], ones128[:],
                       pts[:, 0:512], start=st_, stop=sp_f,
                       tile_position=(0, 0), skip_group_check=True)
            eng.matmul(pdn[64:65, :], ones128[:],
                       pts[:, 512:1024], start=st_, stop=sp_f,
                       tile_position=(0, 64), skip_group_check=True
                       ).then_inc(sems["eT_pv"], 1)

        def _score(eng, si):
            qt, pr, kb = STEPS[si]
            jc = pr
            sp = si % 3
            if si >= 3:
                eng.wait_ge(sems["eS_exp"], si - 2)
            for rt in range(2):
                i = eng.matmul(
                    psc[:, sp * 1024 + rt * 512: sp * 1024 + (rt + 1) * 512],
                    k_fm[rt * 64:(rt + 1) * 64, jc * T + kb * 128: jc * T + (kb + 1) * 128],
                    q_fm[rt * 64:(rt + 1) * 64, jc * T + qt * 512: jc * T + (qt + 1) * 512],
                    start=True, stop=True, tile_position=(rt * 64, 0),
                    skip_group_check=True)
            i.then_inc(sems["eT_sc"], 1)

        def p2a_tensor(eng):
            eng.wait_ge(sems["bar1"], 5)
            for si in range(0, NSTEP, 2):
                _score(eng, si)
                _score(eng, si + 1)
                for sj in (si - 4, si - 3):
                    if sj >= 0:
                        _pvdn(eng, sj)
            for sj in range(NSTEP - 4, NSTEP):
                _pvdn(eng, sj)
            eng.nop().then_inc(sems["bar2aT"], 1)

        def p2a_scalar(eng):
            eng.wait_ge(sems["bar1"], 5)
            for si in range(NSTEP):
                eng.wait_ge(sems["eT_sc"], si + 1)
                if si >= 8 and si % 2 == 0:
                    eng.wait_ge(sems["eT_pv"], si - 6)
                eng.activation(pt_buf[:, (si % 8) * 1024:(si % 8 + 1) * 1024],
                               psc[:, (si % 3) * 1024:(si % 3 + 1) * 1024],
                               AF.Exp, scale=SCALE).then_inc(sems["eS_exp"], 1)

        def p2a_vector(eng):
            eng.wait_ge(sems["bar1"], 5)
            qmk_r.wait(eng, 0)

            def _recip(j):
                if j < 0 or j > 31:
                    return
                qdn_r.wait(eng, j)
                eng.reciprocal(dn_rec[:, (j % 2) * 512:(j % 2 + 1) * 512],
                               dn_buf[:, (j % 2) * 512:(j % 2 + 1) * 512]
                               ).then_inc(sems["eV_rc"], 1)

            def _norm(j):
                if j < 0 or j > 31:
                    return
                qt2, pr2 = j // 4, j % 4
                qbc_r.wait(eng, j)
                eng.tensor_tensor(attn_all[:, pr2 * T + qt2 * 512: pr2 * T + (qt2 + 1) * 512],
                                  attn_all[:, pr2 * T + qt2 * 512: pr2 * T + (qt2 + 1) * 512],
                                  bcast_t[:, (j % 2) * 512:(j % 2 + 1) * 512],
                                  OP.mult).then_inc(sems["eV_nrm"], 1)

            for qt in range(8):
                for pr in range(4):
                    idx2 = 4 * qt + pr
                    nch = 4 * (qt + 1)
                    for si in range(GEND[(qt, pr)] - 3, GEND[(qt, pr)] + 1):
                        _q, _p, kb = STEPS[si]
                        r = kb - (nch - 4)
                        sp = si % 8
                        eng.wait_ge(sems["eS_exp"], si + 1)
                        eng.tensor_tensor(pt_buf[:, sp * 1024:(sp + 1) * 1024],
                                          pt_buf[:, sp * 1024:(sp + 1) * 1024],
                                          masks4[:, r * 1024:(r + 1) * 1024],
                                          OP.mult).then_inc(sems["eV_msk"], 1)
                    eng.wait_ge(sems["eT_pv"], GEND[(qt, pr)] + 1)
                    dc = (idx2 % 2) * 512
                    if idx2 >= 2:
                        qdn_r.wait(eng, idx2 - 2)
                    eng.tensor_copy(dnsp[0:1, dc:dc + 512], pdn[0:1, :])
                    eng.tensor_copy(dnsp[64:65, dc:dc + 512], pdn[64:65, :]
                                    ).then_inc(sems["eV_dn"], 1)
                    eng.tensor_copy(attn_all[:, pr * T + qt * 512: pr * T + (qt + 1) * 512],
                                    ppv[:]).then_inc(sems["eV_av"], 1)
                    vch.note(sems["eV_av"], idx2 + 1)
                    _recip(idx2 - 1)
                    _norm(idx2 - 2)
            _recip(31)
            _norm(30)
            _norm(31)

        def p2a_gpsimd(eng):
            eng.wait_ge(sems["bar1"], 5)
            eng.dma_start(out=masks4[:], in_=masks_in).then_inc(qmk_r.s[0], 16)

            def _bcast(j):
                eng.wait_ge(sems["eV_rc"], j + 1)
                s_, p_ = qbc_r.tag(j)
                for half in range(2):
                    srcap = bass.AP(dn_rec[:].tensor, half * 1024 + (j % 2) * 512,
                                    [[1024, 1], [0, 64], [1, 512]])
                    eng.dma_start(out=bcast_t[half * 64:(half + 1) * 64,
                                              (j % 2) * 512:(j % 2 + 1) * 512],
                                  in_=srcap).then_inc(s_, 16)

            for idx2 in range(32):
                eng.wait_ge(sems["eV_dn"], idx2 + 1)
                if idx2 >= 2:
                    eng.wait_ge(sems["eV_rc"], idx2 - 1)
                s_, p_ = qdn_r.tag(idx2)
                eng.dma_start(out=dn_buf[0:2, (idx2 % 2) * 512:(idx2 % 2 + 1) * 512],
                              in_=_sp_ap(dnsp, 0, (idx2 % 2) * 512, 1024)).then_inc(s_, 16)
                if idx2 >= 1:
                    _bcast(idx2 - 1)
            _bcast(31)

        p2s.close()
        pav.close()
        p3s = contextlib.ExitStack()
        p_tp3 = p3s.enter_context(nc.psum_tensor("p_tp3", [128, 1024], BF16))
        p_tp4 = p3s.enter_context(nc.psum_tensor("p_tp4", [128, 1024], BF16))
        p_big = p3s.enter_context(nc.psum_tensor("p_big", [128, 3072], F32))
        xn_own = p3s.enter_context(nc.sbuf_tensor("xn_own", [128, 8 * 2048], BF16))
        w1sl = p3s.enter_context(nc.sbuf_tensor("w1sl", [128, 4 * 1024], BF16))
        y_att = p3s.enter_context(nc.sbuf_tensor("y_att", [128, 16 * 1024], BF16))
        x_tb = p3s.enter_context(nc.sbuf_tensor("x_tb", [128, 1024], F32))
        out_tb = p3s.enter_context(nc.sbuf_tensor("out_tb", [128, 1024], F32))
        e_s = p3s.enter_context(nc.sbuf_tensor("e_s", [128, 2 * 8], F32))
        r_sml = p3s.enter_context(nc.sbuf_tensor("r_sml", [128, 16], F32))
        y_ffn_tm = p3s.enter_context(nc.sbuf_tensor("y_ffn_tm", [128, 8 * 1024], BF16))
        attn_tb = p3s.enter_context(nc.sbuf_tensor("attn_tb", [128, 2 * 1024], BF16))
        wo_bf = p3s.enter_context(nc.sbuf_tensor("wo_bf", [128, 8 * 1024], BF16))
        p3b = contextlib.ExitStack()
        h_half = p3b.enter_context(nc.sbuf_tensor("h_half", [128, 32 * 1024], BF16))
        w2sl = p3b.enter_context(nc.sbuf_tensor("w2sl", [128, 2 * 4096], BF16))
        y_fm_s = p3b.enter_context(nc.sbuf_tensor("y_fm_s", [128, 2 * 1024], BF16))
        p3b.close()

        def p2b_sync(eng):
            eng.wait_ge(sems["bar2aT"], 1)
            pid = eng.partition_id()
            with eng.If(pid % 2 == 0):
                for cc in range(8):
                    eng.dma_start(out=xn_own[:, cc * 2048:(cc + 1) * 2048],
                                  in_=xn_spill[cc * 128:(cc + 1) * 128, 0:2048]
                                  ).then_inc(sems["qxn"], 16)
            with eng.Else():
                for cc in range(8):
                    eng.dma_start(out=xn_own[:, cc * 2048:(cc + 1) * 2048],
                                  in_=xn_spill[cc * 128:(cc + 1) * 128, 2048:4096]
                                  ).then_inc(sems["qxn"], 16)
            eng.wait_ge(sems["ecc"], 1)
            with eng.If(pid % 2 == 0):
                for tb in range(16):
                    if tb >= 2:
                        eng.wait_ge(sems["eT_y"], tb - 1)
                    s_, p_ = qat_r.tag(tb)
                    srcap = bass.AP(cc_out_ap.tensor, tb * 128,
                                    [[4096, 128], [128 * 4096, 8], [1, 128]])
                    eng.dma_start(out=attn_tb[:, (tb % 2) * 1024:(tb % 2 + 1) * 1024],
                                  in_=srcap).then_inc(s_, p_)
            with eng.Else():
                for tb in range(16):
                    if tb >= 2:
                        eng.wait_ge(sems["eT_y"], tb - 1)
                    s_, p_ = qat_r.tag(tb)
                    srcap = bass.AP(cc_out_ap.tensor, 2048 + tb * 128,
                                    [[4096, 128], [128 * 4096, 8], [1, 128]])
                    eng.dma_start(out=attn_tb[:, (tb % 2) * 1024:(tb % 2 + 1) * 1024],
                                  in_=srcap).then_inc(s_, p_)
            for u3 in range(16):
                if u3 >= 1:
                    eng.wait_ge(sems["eV_out"], u3)
                s_, p_ = qxo_r.tag(u3)
                eng.dma_start(out=x_tb[:],
                              in_=x_own[u3 * 128:(u3 + 1) * 128, :]).then_inc(s_, p_)
                if u3 >= 1:
                    s2_, p2_ = qst_r.tag(u3 - 1)
                    eng.dma_start(out=y_out[(u3 - 1) * 128:u3 * 128, :],
                                  in_=out_tb[:]).then_inc(s2_, p2_)
            eng.wait_ge(sems["eV_out"], 16)
            s2_, p2_ = qst_r.tag(15)
            eng.dma_start(out=y_out[15 * 128:16 * 128, :], in_=out_tb[:]).then_inc(s2_, p2_)

        def p2b_gpsimd(eng):
            eng.wait_ge(sems["bar2aT"], 1)
            bo_rep = bass.AP(bo_in.tensor, 0, [[C, 1], [0, 128], [1, C]])
            b2_rep = bass.AP(b2_in.tensor, 0, [[C, 1], [0, 128], [1, C]])
            eng.dma_start(out=bb[:], in_=bo_rep).then_inc(sems["qbb"], 16)
            eng.dma_start(out=out_tb[:], in_=b2_rep).then_inc(sems["qbb"], 16)
            for u1 in range(4):
                s_, p_ = qwf1_r.tag(u1)
                srcap = bass.AP(w1_in.tensor, u1 * 128,
                                [[FF, 128], [128 * FF, 8], [1, 128]])
                eng.dma_start(out=w1sl[:, u1 * 1024:(u1 + 1) * 1024],
                              in_=srcap).then_inc(s_, p_)
            eng.wait_ge(sems["eV_nrm"], 32)
            for jc in range(4):
                eng.dma_start(out=cc_in_ap[jc * 128:(jc + 1) * 128, :],
                              in_=attn_all[:, jc * T:(jc + 1) * T]).then_inc(sems["qcc"], 16)
            eng.wait_ge(sems["qcc"], 64)
            eng.collective_compute(
                "AllGather", OP.bypass,
                replica_groups=[[0, 1], [2, 3], [4, 5], [6, 7]],
                ins=[cc_in_ap], outs=[cc_out_ap],
            ).then_inc(sems["ecc"], 1)
            for cc in range(NCH):
                s_, p_ = qwo_r.tag(cc)
                eng.dma_start(out=wo_bf[:, cc * 1024:(cc + 1) * 1024],
                              in_=wo_in[cc * 128:(cc + 1) * 128, :]).then_inc(s_, p_)
            def _w1load(eng, u1):
                fc = u1 % 32
                eng.wait_ge(sems["eT_h"], u1 - 3)
                s_, p_ = qwf1_r.tag(u1)
                srcap = bass.AP(w1_in.tensor, fc * 128,
                                [[FF, 128], [128 * FF, 8], [1, 128]])
                eng.dma_start(out=w1sl[:, (u1 % 4) * 1024:(u1 % 4 + 1) * 1024],
                              in_=srcap).then_inc(s_, p_)

            def _w2load(eng, u2):
                jc = u2 % 8
                if u2 >= 2:
                    eng.wait_ge(sems["eT_f"], u2 - 1)
                s_, p_ = qwf2_r.tag(u2)
                srcap = bass.AP(w2_in.tensor, jc * 128,
                                [[C, 128], [128 * C, 32], [1, 128]])
                eng.dma_start(out=w2sl[:, (u2 % 2) * 4096:(u2 % 2 + 1) * 4096],
                              in_=srcap).then_inc(s_, p_)

            for u1 in range(4, 32):
                _w1load(eng, u1)
            for u2 in range(0, 8):
                _w2load(eng, u2)
            for u1 in range(32, 64):
                _w1load(eng, u1)
            for u2 in range(8, 16):
                _w2load(eng, u2)
            qst_r.wait(eng, 15)
            qst_r.wait(eng, 14)

        def p2b_scalar(eng):
            for tb in range(16):
                eng.wait_ge(sems["eT_r"], tb + 1)
                if tb >= 2:
                    eng.wait_ge(sems["eV_rg"], tb - 1)
                eng.activation(e_s[:, (tb % 2) * 8:(tb % 2 + 1) * 8],
                               p_big[:, 2048 + (tb % 2) * 512: 2048 + (tb % 2) * 512 + 8],
                               AF.Exp).then_inc(sems["eS_re"], 1)
            for ph in range(2):
                for fc in range(32):
                    u1 = ph * 32 + fc
                    if ph == 0 and fc == 0:
                        eng.wait_ge(sems["qcc"], 64)
                    eng.wait_ge(sems["eT_h"], u1 + 1)
                    if ph == 1 and fc == 0:
                        eng.wait_ge(sems["eT_f"], 8)
                    eng.activation(h_half[:, fc * 1024:(fc + 1) * 1024],
                                   p_big[:, (fc % 2) * 1024:(fc % 2 + 1) * 1024],
                                   AF.Gelu_apprx_tanh,
                                   bias=b1r[:, fc:fc + 1]).then_inc(sems["eS_ge"], 1)

        def p2b_vector(eng):
            eng.wait_ge(sems["qbb"], 32)
            vch.inc(eng.tensor_tensor(bb[:], bb[:], out_tb[:], OP.add))
            vch.wait(eng)
            eng.nop().then_inc(sems["eV_bb"], 1)
            for tb in range(16):
                eng.wait_ge(sems["eS_re"], tb + 1)
                ec = e_s[:, (tb % 2) * 8:(tb % 2 + 1) * 8]
                mx1 = r_sml[:, 0:1]; meq = r_sml[:, 2:10]; mx2 = r_sml[:, 1:2]
                s12 = r_sml[:, 10:11]; r12 = r_sml[:, 11:12]
                ga = r_sml[:, 12:13]; gb = r_sml[:, 13:14]
                vch.inc(eng.reduce_max(mx1, ec, AX)); vch.wait(eng)
                vch.inc(eng.tensor_scalar(meq, ec, mx1, None, OP.is_equal)); vch.wait(eng)
                vch.inc(eng.tensor_scalar(meq, meq, -1.0, 1.0, OP.mult, OP.add)); vch.wait(eng)
                vch.inc(eng.tensor_tensor(meq, ec, meq, OP.mult)); vch.wait(eng)
                vch.inc(eng.reduce_max(mx2, meq, AX)); vch.wait(eng)
                vch.inc(eng.tensor_tensor(s12, mx1, mx2, OP.add)); vch.wait(eng)
                vch.inc(eng.reciprocal(r12, s12)); vch.wait(eng)
                vch.inc(eng.tensor_tensor(ga, mx1, r12, OP.mult)); vch.wait(eng)
                vch.inc(eng.tensor_tensor(gb, mx2, r12, OP.mult)); vch.wait(eng)
                vch.inc(eng.tensor_tensor(g_s[:, tb:tb + 1], ga, gb, OP.add))
                vch.wait(eng)
                eng.nop().then_inc(sems["eV_rg"], 1)
            eng.wait_ge(sems["eV_bb"], 1)
            for tb in range(16):
                eng.wait_ge(sems["eT_y"], tb + 1)
                eng.tensor_tensor(y_att[:, tb * 1024:(tb + 1) * 1024],
                                  p_big[:, (tb % 2) * 1024:(tb % 2 + 1) * 1024],
                                  bb[:], OP.add).then_inc(sems["eV_ye"], 1)
            for ph in range(2):

                def _w2evac(eng, jc2):
                    u2c = ph * 8 + jc2
                    eng.wait_ge(sems["eT_f"], u2c + 1)
                    eng.tensor_copy(y_fm_s[:, (jc2 % 2) * 1024:(jc2 % 2 + 1) * 1024],
                                    p_big[:, (jc2 % 2) * 1024:(jc2 % 2 + 1) * 1024]
                                    ).then_inc(sems["eV_fe"], 1)
                    eng.wait_ge(sems["eT_ft"], u2c + 1)
                    if jc2 == 0:
                        eng.wait_ge(sems["eV_rg"], 16)
                    ptp = p_tp3 if jc2 % 2 == 0 else p_tp4
                    for tbi in range(8):
                        i = eng.tensor_scalar_mul(
                            bass.AP(y_ffn_tm[:].tensor, tbi * 1024 + jc2 * 128,
                                    [[8 * 1024, 128], [1, 128]]),
                            ptp[:, tbi * 128:(tbi + 1) * 128],
                            g_s[:, ph * 8 + tbi: ph * 8 + tbi + 1])
                    i.then_inc(sems["eV_ft"], 1)
                    vch.note(sems["eV_ft"], u2c + 1)

                for jc in range(8):
                    _w2evac(eng, jc)
                for tbl in range(8):
                    u3 = ph * 8 + tbl
                    eng.wait_ge(sems["eV_ye"], u3 + 1)
                    qxo_r.wait(eng, u3)
                    if u3 >= 1:
                        qst_r.wait(eng, u3 - 1)
                    vch.wait(eng)
                    vch.inc(eng.tensor_tensor(
                        out_tb[:], y_ffn_tm[:, tbl * 1024:(tbl + 1) * 1024],
                        x_tb[:], OP.add))
                    vch.wait(eng)
                    eng.tensor_tensor(out_tb[:], out_tb[:],
                                      y_att[:, u3 * 1024:(u3 + 1) * 1024],
                                      OP.add).then_inc(sems["eV_out"], 1)

        def p2b_tensor(eng):
            eng.wait_ge(sems["eS_exp"], NSTEP)
            eng.wait_ge(sems["eV_av"], 32)
            eng.wait_ge(sems["eV_dn"], 32)
            eng.wait_ge(sems["qxn"], 128)
            for tb in range(16):
                if tb >= 2:
                    eng.wait_ge(sems["eS_re"], tb - 1)
                for cc in range(NCH):
                    i = eng.matmul(p_big[:, 2048 + (tb % 2) * 512: 2048 + (tb % 2) * 512 + 8],
                                   xn_own[:, cc * 2048 + tb * 128: cc * 2048 + (tb + 1) * 128],
                                   rw_bf[:, cc * 8:(cc + 1) * 8],
                                   start=(cc == 0), stop=(cc == NCH - 1),
                                   skip_group_check=True)
                i.then_inc(sems["eT_r"], 1)
            def _w1(eng, ph):
                for fc in range(32):
                    u1 = ph * 32 + fc
                    qwf1_r.wait(eng, u1)
                    if u1 >= 2:
                        eng.wait_ge(sems["eS_ge"], u1 - 1)
                    for cc in range(NCH):
                        for tt in range(2):
                            i = eng.matmul(
                                p_big[:, (fc % 2) * 1024 + tt * 512:(fc % 2) * 1024 + (tt + 1) * 512],
                                w1sl[:, (u1 % 4) * 1024 + cc * 128:(u1 % 4) * 1024 + (cc + 1) * 128],
                                xn_own[:, cc * 2048 + ph * 1024 + tt * 512: cc * 2048 + ph * 1024 + (tt + 1) * 512],
                                start=(cc == 0), stop=(cc == NCH - 1), skip_group_check=True)
                    i.then_inc(sems["eT_h"], 1)

            _w1(eng, 0)
            qwo_r.wait(eng, NCH - 1)
            qwo_r.wait(eng, NCH - 2)
            eng.wait_ge(sems["eS_ge"], 32)
            for tb in range(16):
                qat_r.wait(eng, tb)
                if tb >= 2:
                    eng.wait_ge(sems["eV_ye"], tb - 1)
                for gc in range(8):
                    for sl in range(2):
                        i = eng.matmul(
                            p_big[:, (tb % 2) * 1024 + sl * 512:(tb % 2) * 1024 + (sl + 1) * 512],
                            attn_tb[:, (tb % 2) * 1024 + gc * 128:(tb % 2) * 1024 + (gc + 1) * 128],
                            wo_bf[:, gc * 1024 + sl * 512: gc * 1024 + (sl + 1) * 512],
                            start=(gc == 0), stop=(gc == NCH - 1), skip_group_check=True)
                i.then_inc(sems["eT_y"], 1)
            eng.wait_ge(sems["eV_ye"], 16)
            for ph in range(2):
                if ph == 1:
                    eng.wait_ge(sems["eV_fe"], 8)
                    _w1(eng, 1)
                eng.wait_ge(sems["eS_ge"], (ph + 1) * 32)
                for jc in range(8):
                    u2 = ph * 8 + jc
                    qwf2_r.wait(eng, u2)
                    if u2 >= 2:
                        eng.wait_ge(sems["eV_fe"], u2 - 1)
                    for fc in range(32):
                        for tt in range(2):
                            i = eng.matmul(
                                p_big[:, (jc % 2) * 1024 + tt * 512:(jc % 2) * 1024 + (tt + 1) * 512],
                                w2sl[:, (u2 % 2) * 4096 + fc * 128:(u2 % 2) * 4096 + (fc + 1) * 128],
                                h_half[:, fc * 1024 + tt * 512: fc * 1024 + (tt + 1) * 512],
                                start=(fc == 0), stop=(fc == 31), skip_group_check=True)
                    i.then_inc(sems["eT_f"], 1)
                    eng.wait_ge(sems["eV_fe"], u2 + 1)
                    if u2 >= 2:
                        eng.wait_ge(sems["eV_ft"], u2 - 1)
                    ptp = p_tp3 if jc % 2 == 0 else p_tp4
                    for tbi in range(8):
                        i = eng.transpose(ptp[:, tbi * 128:(tbi + 1) * 128],
                                          y_fm_s[:, (jc % 2) * 1024 + tbi * 128:(jc % 2) * 1024 + (tbi + 1) * 128],
                                          ident[:])
                    i.then_inc(sems["eT_ft"], 1)


        with nc.Block() as blk:
            full = (debug_phase == 0 or debug_phase >= 3)

            @blk.sync
            def _(eng):
                p1_sync(eng)
                eng.nop().then_inc(sems["bar1"], 1)
                if full:
                    p2b_sync(eng)

            @blk.gpsimd
            def _(eng):
                p1_gpsimd(eng)
                eng.nop().then_inc(sems["bar1"], 1)
                if full:
                    p2a_gpsimd(eng)
                    p2b_gpsimd(eng)
                if debug_phase == 2:
                    p2a_gpsimd(eng)
                    eng.wait_ge(sems["ecc"], 1)
                    eng.dma_start(out=dbg["att"], in_=attn_all[:]).then_inc(sems["qst"], 16)
                    eng.dma_start(out=dbg["ccout"], in_=cc_out_ap).then_inc(sems["qst"], 16)
                    eng.wait_ge(sems["qst"], 32)
                if debug_phase == 1:
                    eng.wait_ge(sems["eV_fin"], 8 * NG)
                    eng.wait_ge(sems["eV_vv"], 4 * NG)
                    for jc in range(4):
                        eng.dma_start(out=dbg["k"][:, jc * T:(jc + 1) * T],
                                      in_=k_fm[:, jc * T:(jc + 1) * T]).then_inc(sems["qst"], 16)
                        eng.dma_start(out=dbg["q"][:, jc * T:(jc + 1) * T],
                                      in_=q_fm[:, jc * T:(jc + 1) * T]).then_inc(sems["qst"], 16)
                    for i8 in range(8):
                        eng.dma_start(out=dbg["v"][:, i8 * 2048:(i8 + 1) * 2048],
                                      in_=v_tm[:, i8 * 2048:(i8 + 1) * 2048]).then_inc(sems["qst"], 16)
                    eng.wait_ge(sems["qst"], 16 * 16)
                    qpc2_r.wait(eng, W1U + W2U - 1)
                    qpc2_r.wait(eng, W1U + W2U - 2)

            @blk.scalar
            def _(eng):
                p1_scalar(eng)
                eng.nop().then_inc(sems["bar1"], 1)
                if debug_phase >= 2 or full:
                    p2a_scalar(eng)
                if full:
                    p2b_scalar(eng)

            @blk.vector
            def _(eng):
                p1_vector(eng)
                eng.nop().then_inc(sems["bar1"], 1)
                if debug_phase >= 2 or full:
                    p2a_vector(eng)
                if full:
                    p2b_vector(eng)

            @blk.tensor
            def _(eng):
                p1_tensor(eng)
                eng.nop().then_inc(sems["bar1"], 1)
                if debug_phase >= 2 or full:
                    p2a_tensor(eng)
                if full:
                    p2b_tensor(eng)

        p3s.close()

    return nc, dbg



def _host_constants():
    half = D // 2
    inv_freq = (1.0 / 10000.0 ** (np.arange(half, dtype=np.float32) / half)).astype(np.float32)
    t = np.arange(T, dtype=np.float32)
    freqs = t[:, None] * inv_freq[None, :]
    cos = np.repeat(np.cos(freqs), 2, axis=-1)
    sin = np.repeat(np.sin(freqs), 2, axis=-1)
    cos_fm = np.concatenate([cos.T, cos.T], axis=0).astype(np.float32)
    sin_fm = np.concatenate([sin.T, sin.T], axis=0).astype(np.float32)
    ident = np.eye(128, dtype=np.float32)
    P = np.zeros((128, 128), dtype=np.float32)
    for i in range(64):
        P[2 * i, 2 * i + 1] = -1.0
        P[2 * i + 1, 2 * i] = 1.0
    permT = P.T.copy()
    masks = np.zeros((128, 4 * 1024), dtype=np.float32)
    ii = np.arange(128)[:, None]
    jj = np.arange(512)[None, :]
    for r in range(4):
        m = (ii + 128 * r <= jj).astype(np.float32)
        masks[:, r * 1024:r * 1024 + 512] = m
        masks[:, r * 1024 + 512:(r + 1) * 1024] = m
    return cos_fm, sin_fm, ident, permT, masks


def make_inputs(full):
    x = np.asarray(full["L0_attn_x"], dtype=np.float32)
    Wqkv = np.asarray(full["L0_attn_Wqkv"], dtype=np.float32)
    Wo = np.asarray(full["L0_attn_Wo"], dtype=np.float32)
    bo = np.asarray(full["L0_attn_bo"], dtype=np.float32)
    rw = np.asarray(full["L0_ffn_router_w"], dtype=np.float32)
    w1 = np.asarray(full["L0_ffn_w1"], dtype=np.float32)
    b1 = np.asarray(full["L0_ffn_b1"], dtype=np.float32)
    w2 = np.asarray(full["L0_ffn_w2"], dtype=np.float32)
    b2 = np.asarray(full["L0_ffn_b2"], dtype=np.float32)
    cos_fm, sin_fm, ident, permT, masks = _host_constants()
    b1r = np.ascontiguousarray(b1.reshape(32, 128).T)
    in_maps = []
    for c in range(8):
        b = c // 2
        hh = c % 2
        cols = np.concatenate([np.arange(512 * hh, 512 * hh + 512),
                               1024 + np.arange(512 * hh, 512 * hh + 512),
                               2048 + np.arange(512 * hh, 512 * hh + 512)])
        in_maps.append({
            "x_in": np.ascontiguousarray(x[b]),
            "x_own": np.ascontiguousarray(x[b, hh * 2048:(hh + 1) * 2048]),
            "wqkv": np.ascontiguousarray(Wqkv[:, cols]),
            "wo_in": Wo,
            "w1_in": w1,
            "w2_in": w2,
            "rw_in": rw,
            "b1r_in": b1r,
            "bo_in": bo.reshape(1, C),
            "b2_in": b2.reshape(1, C),
            "cos_in": cos_fm,
            "sin_in": sin_fm,
            "ident_in": ident,
            "permT_in": permT,
            "masks_in": masks.astype(_bf16),
        })
    return in_maps


_NC_CACHE = {}


def kernel(**inputs):
    in_maps = make_inputs(inputs)
    if "nc" not in _NC_CACHE:
        _NC_CACHE["nc"] = build_program(debug_phase=0)[0]
    nc = _NC_CACHE["nc"]
    res = run_bass_kernel_spmd(nc, in_maps, core_ids=list(range(8)))
    out = np.empty((B, T, C), dtype=np.float32)
    for c in range(8):
        b, hh = c // 2, c % 2
        out[b, hh * 2048:(hh + 1) * 2048, :] = res.results[c]["y_out"]
    return out
